# version 03
# baseline (speedup 1.0000x reference)
import numpy as np
import ml_dtypes
import concourse.bass as bass
import concourse.mybir as mybir
from concourse.bass_utils import run_bass_kernel_spmd

F32 = mybir.dt.float32
BF16 = mybir.dt.bfloat16
AF = mybir.ActivationFunctionType
AL = mybir.AluOpType
AX = mybir.AxisListType
BF = ml_dtypes.bfloat16

BIGNEG = -8.0e37
DMA_ENGS = ("sync", "gpsimd")

B, T, D, H_ALL, HD = 2, 4096, 2048, 32, 64
H = 8
NT = T // 128
CHUNK = 128
NCH = T // CHUNK
WIN = 4
SCALE = HD ** (-0.5)
E, I_FF, SI = 8, 1408, 2816
TQ = 1024
N_CORES = 8


class Prog:
    ENGS = ("tensor", "vector", "scalar", "gpsimd", "sync")

    def __init__(self, nc, nbar=8):
        self.nc = nc
        self.steps = []
        self.nbar = nbar
        self._junk = None
        self.regs = {}

    def brk(self):
        self.steps.append(("scalar", lambda e: e.memzero(self._junk), "n"))

    def brk2(self):
        self.steps.append(("vector", lambda e: e.memzero(self._junk), "n"))

    def t(self, fn): self.steps.append(("tensor", fn, "n"))
    def v(self, fn): self.steps.append(("vector", fn, "n"))
    def a(self, fn): self.steps.append(("scalar", fn, "n"))
    def g(self, fn): self.steps.append(("gpsimd", fn, "n"))
    def s(self, fn): self.steps.append(("sync", fn, "n"))
    def cc(self, fn): self.steps.append(("gpsimd", fn, "cc"))

    def lower(self, block):
        nc = self.nc
        sems = [nc.alloc_semaphore(f"bar{i}") for i in range(self.nbar)]
        cc_sem = nc.alloc_semaphore("cc_sem")
        runs = []
        for eng, fn, kind in self.steps:
            if kind == "n" and runs and runs[-1][0] == eng and runs[-1][2] == "n":
                runs[-1][1].append(fn)
            else:
                runs.append((eng, [fn], kind))
        nruns = len(runs)
        sem_tot = [0] * self.nbar
        run_wait = [None] * nruns
        cum_after = [0] * nruns
        for i, (eng, fns, kind) in enumerate(runs):
            if i > 0:
                run_wait[i] = cum_after[i - 1]
            if kind == "cc":
                cum_after[i] = ("cc", 1)
            else:
                s = i % self.nbar
                amt = 16 * len(fns) if eng in DMA_ENGS else 1
                sem_tot[s] += amt
                cum_after[i] = (s, sem_tot[s])
        per_eng = {e: [] for e in self.ENGS}
        for i, (eng, fns, kind) in enumerate(runs):
            per_eng[eng].append((i, eng, fns, kind))

        def make_body(items):
            def body(eng):
                for (i, ename, fns, kind) in items:
                    if run_wait[i] is not None:
                        si, val = run_wait[i]
                        eng.wait_ge(cc_sem if si == "cc" else sems[si], val)
                    if kind == "cc":
                        fns[0](eng).then_inc(cc_sem)
                        continue
                    s = i % self.nbar
                    if ename in DMA_ENGS:
                        for fn in fns:
                            fn(eng).then_inc(sems[s], 16)
                    else:
                        last = None
                        for fn in fns:
                            rr = fn(eng)
                            if rr is not None:
                                last = rr
                        last.then_inc(sems[s], 1)
            return body
        for e in self.ENGS:
            if per_eng[e]:
                getattr(block, e)(make_body(per_eng[e]))


def cdiv(a, b):
    return (a + b - 1) // b


def host_consts():
    r = np.arange(128)
    c = {}
    c["c_idb"] = np.eye(128, dtype=np.float32).astype(BF)
    c["c_idf"] = np.eye(128, dtype=np.float32)
    c["c_mask_slneg"] = np.where(r[:, None] > r[None, :], -1.0, 0.0).astype(np.float32)
    c["c_mask_su"] = np.where(r[:, None] < r[None, :], 1.0, 0.0).astype(np.float32)
    c["c_mask_le"] = np.where(r[:, None] <= r[None, :], 1.0, 0.0).astype(np.float32)
    c["c_maskA"] = np.where(r[:, None] > r[None, :], BIGNEG, 0.0).astype(np.float32).astype(BF)
    c["c_maskB"] = np.where(r[:, None] < r[None, :], BIGNEG, 0.0).astype(np.float32).astype(BF)
    c["c_ones3"] = np.ones((3, 128), dtype=np.float32).astype(BF)
    c["c_ones1_64"] = np.ones((1, 64), dtype=np.float32)
    c["c_ones3T"] = np.ones((3, 4096), dtype=np.float32).astype(BF)
    c["c_onesblk"] = np.stack([(r < 64).astype(np.float32),
                               (r >= 64).astype(np.float32)], axis=1).astype(BF)
    c["c_onesblkT"] = np.ascontiguousarray(
        np.stack([(r < 64), (r >= 64)], axis=0).astype(np.float32))
    c["c_trilf"] = np.where(r[:, None] <= r[None, :], 1.0, 0.0).astype(np.float32)
    c["c_onescol"] = np.ones((128, 1), dtype=np.float32)
    c["c_fvals"] = np.tile(np.array([[1e-6, 1.0, 1e-5, 1e-12, 0, 0, 0, 0]],
                                    dtype=np.float32), (128, 1))
    c["c_onesrow1"] = np.ones((1, 128), dtype=np.float32)
    return c


CONST_SPECS = {k: v for k, v in host_consts().items()}


class ACtx:
    def __init__(self, nc, guarded=False):
        self._guards = []
        if guarded:
            def A(name, shape, dt):
                g = nc.sbuf_tensor(name, shape, dt)
                h = g.__enter__()
                self._guards.append(g)
                return h
        else:
            A = nc.alloc_sbuf_tensor
        AP_ = nc.alloc_psum_tensor
        AC = nc.alloc_sbuf_tensor
        self.idb = AC("idb", [128, 128], BF16)
        self.idf = AC("idf", [128, 128], F32)
        self.mask_slneg = AC("mask_slneg", [128, 128], F32)
        self.mask_su = AC("mask_su", [128, 128], F32)
        self.maskA = AC("maskA", [128, 128], BF16)
        self.maskB = AC("maskB", [128, 128], BF16)
        self.ones3 = AC("ones3", [3, 128], BF16)
        self.ones1_64 = AC("ones1_64", [1, 64], F32)
        self.onesblk = AC("onesblk", [128, 2], BF16)
        self.onesblkT = AC("onesblkT", [2, 128], F32)
        self.trilf = AC("trilf", [128, 128], F32)
        self.onescol = AC("onescol", [128, 1], F32)
        self.onesrow1 = AC("onesrow1", [1, 128], F32)
        self.junk = AC("junk", [1, 8], F32)
        self.fvals = AC("fvals", [128, 8], F32)
        nc.const_aps.register(F32, 1e-6, self.fvals.ap()[:, 0:1])
        nc.const_aps.register(F32, 1.0, self.fvals.ap()[:, 1:2])
        nc.const_aps.register(F32, 1e-5, self.fvals.ap()[:, 2:3])
        nc.const_aps.register(F32, 1e-12, self.fvals.ap()[:, 3:4])
        self.q_tok = A("q_tok", [128, NT * 64 * H], BF16)
        self.k_tok = A("k_tok", [128, NT * 64 * H], BF16)
        self.w_tok = A("w_tok", [128, NT * 64 * H], BF16)
        self.v65 = A("v65", [128, NT * 65 * H], BF16)
        self.bcol = A("bcol", [128, NT * H], F32)
        self.negG3_tm = A("negG3_tm", [128, NT * 3 * H], BF16)
        self.xtile = A("xtile", [128, 512], BF16)
        self.wblk = A("wblk", [128, 512], BF16)
        self.w1t = A("w1t", [128, 16 * 32], BF16)
        self.w2t = A("w2t", [32, 512], BF16)
        self.bgTw = A("bgTw", [128, 16 * 64], BF16)
        self.bgb = A("bgb", [64, 1], F32)
        self.qgrow = A("qgrow", [1, 512], F32)
        self.kgrow = A("kgrow", [1, 512], F32)
        self.grep = A("grep", [128, 512], F32)
        self.convs = A("convs", [128, 16], F32)
        self.sqbuf = A("sqbuf", [128, 512], F32)
        self.rstd8 = A("rstd8", [128, 8], F32)
        self.var8 = A("var8", [128, 8], F32)
        self.w0T = A("w0T", [32, T], BF16)
        self.wpre = A("wpre", [128, 2052], BF16)
        self.wconv = A("wconv", [128, 2048], F32)
        self.nrm2 = A("nrm2", [2, 512], F32)
        self.bgB = A("bgB", [40, 512], F32)
        self.bgS = A("bgS", [40, 512], F32)
        self.bgS2 = A("bgS2", [40, 512], F32)
        self.gtok = A("gtok", [128, 32], F32)
        self.Gtok = A("Gtok", [128, 32], F32)
        self.gcarry = A("gcarry", [1, 8], F32)
        self.msplit = A("msplit", [128, 32], F32)
        self.spl = A("spl", [128, 3 * 32], BF16)
        self.splf = A("splf", [128, 32], F32)
        self.stage24 = A("stage24", [24, 128], BF16)
        self.Kd = A("Kd70", [70, T], BF16)
        self.q70 = A("q70", [70, 256], BF16)
        self.S = A("S_sb", [64, 128], BF16)
        self.QbT = A("QbT", [64, 384], BF16)
        self.QbTs = A("QbTs", [64, 128], BF16)
        self.negbWb = A("negbWb", [128, 64], BF16)
        self.Xp = A("Xp", [128, 256], F32)
        self.Mp = A("Mp", [128, 256], F32)
        self.QWt_m = A("QWt_m", [128, 128], F32)
        self.RT = A("RT", [128, 128], BF16)
        self.RbT = A("RbT", [128, 128], BF16)
        self.RbTs = A("RbTs", [128, 128], BF16)
        self.QtT = A("QtT", [64, 128], BF16)
        self.negsWK = A("negsWK", [128, 128], BF16)
        self.bsWK = A("bsWK", [128, 128], BF16)
        self.X_u = A("X_u", [128, 128], F32)
        self.RwT = A("RwT", [128, 128], BF16)
        self.V_sb = A("V_sb", [128, 64], BF16)
        self.McT = A("McT", [64, 128], BF16)
        self.D_sb = A("D_sb", [128, 128], BF16)
        self.negG3r = A("negG3r", [3, 128], BF16)
        self.mpart = A("mpart", [128, 12], F32)
        self.mcol = A("mcol", [128, 1], F32)
        self.negm_sb = A("negm_sb", [128, 1], F32)
        self.negmrow4 = A("negmrow4", [1, 512], F32)
        self.zinv = A("zinv", [1, 128], F32)
        self.zrep = A("zrep", [64, 128], F32)
        self.ot_raw = A("ot_raw", [64, 128], F32)
        self.P_buf = A("P_buf", [128, 1024], BF16)
        self.oT = A("oT", [128, T], BF16)
        self.psA = AP_("psA", [128, 512], F32)
        self.psM = AP_("psM", [128, 512], F32)
        self.psQ = AP_("psQ", [128, 512], F32)
        self.big1 = AP_("big1", [128, 512], F32)
        self.big2 = AP_("big2", [128, 512], F32)
        self.psPV = AP_("psPV", [128, 512], F32)
        self.psW = AP_("psW", [128, 512], F32)
        self.psT = AP_("psT", [128, 1024], BF16)


def release_guards(ctx):
    for g in reversed(ctx._guards):
        g.__exit__(None, None, None)
    ctx._guards = []


def emit_load_consts(P, ctx, dram):
    pairs = [("c_idb", ctx.idb), ("c_idf", ctx.idf), ("c_mask_slneg", ctx.mask_slneg),
             ("c_mask_su", ctx.mask_su),
             ("c_maskA", ctx.maskA), ("c_maskB", ctx.maskB), ("c_ones3", ctx.ones3),
             ("c_ones1_64", ctx.ones1_64), ("c_onesblk", ctx.onesblk),
             ("c_onesblkT", ctx.onesblkT), ("c_trilf", ctx.trilf),
             ("c_onescol", ctx.onescol), ("c_onesrow1", ctx.onesrow1),
             ("c_fvals", ctx.fvals)]
    for name, dst in pairs:
        if name in dram:
            P.s(lambda e, name=name, dst=dst: e.dma_start(out=dst.ap(), in_=dram[name]))


def emit_rms_apply(P, ctx, ps, grep, dst_tok, tt, eps):
    P.a(lambda e, ps=ps: e.activation(ctx.sqbuf.ap(), ps, AF.Square))
    P.v(lambda e: e.tensor_reduce(ctx.var8.ap(),
                                  ctx.sqbuf.ap().rearrange("p (h d) -> p h d", d=64),
                                  AX.X, AL.add))
    P.a(lambda e, eps=eps: e.activation(ctx.rstd8.ap(), ctx.var8.ap(), AF.Ln,
                                        scale=1.0 / 64, bias=eps))
    P.brk2()
    P.a(lambda e: e.activation(ctx.var8.ap(), ctx.rstd8.ap(), AF.Exp, scale=-0.5))
    for seg in range(8):
        P.v(lambda e, ps=ps, seg=seg, tt=tt: e.scalar_tensor_tensor(
            dst_tok.ap()[:, tt * 512 + seg * 64: tt * 512 + (seg + 1) * 64],
            ps[:, seg * 64:(seg + 1) * 64], ctx.var8.ap()[:, seg:seg + 1],
            grep.ap()[:, seg * 64:(seg + 1) * 64], AL.mult, AL.mult))


def emit_prep(P, ctx, dram):
    nc = P.nc
    psA = ctx.psA.ap()
    banks = [ctx.psA.ap(), ctx.psM.ap(), ctx.psQ.ap(), ctx.psW.ap()]
    big1, big2, psPV, psT = ctx.big1.ap(), ctx.big2.ap(), ctx.psPV.ap(), ctx.psT.ap()

    P.g(lambda e: e.dma_start(out=ctx.w1t.ap().rearrange("p (k n) -> p k n", n=32),
                              in_=dram["W1T"].rearrange("(k p) n -> p k n", p=128)))
    P.g(lambda e: e.dma_start(out=ctx.w2t.ap(), in_=dram["W2T"]))
    P.g(lambda e: e.dma_start(out=ctx.bgTw.ap().rearrange("p (k n) -> p k n", n=64),
                              in_=dram["bgT"].rearrange("(k p) n -> p k n", p=128)))
    P.s(lambda e: e.dma_start(out=ctx.bgb.ap(), in_=dram["bgb"]))
    P.s(lambda e: e.dma_start(out=ctx.convs.ap(), in_=dram["convs"]))
    P.s(lambda e: e.dma_start(out=ctx.qgrow.ap(), in_=dram["qgrow"]))
    P.s(lambda e: e.dma_start(out=ctx.kgrow.ap(), in_=dram["kgrow"]))
    P.v(lambda e: e.memset(ctx.v65.ap(), 1.0))
    P.v(lambda e: e.memset(ctx.gcarry.ap(), 0.0))
    P.v(lambda e: e.memset(ctx.wpre.ap()[:, 0:4], 0.0))

    for which, wname, grow, dst in [("q", "WqT", ctx.qgrow, ctx.q_tok),
                                    ("k", "WkT", ctx.kgrow, ctx.k_tok),
                                    ("v", "WvT", None, ctx.v65)]:
        if grow is not None:
            P.t(lambda e, grow=grow: e.matmul(
                psA[:, 0:512], ctx.onesrow1.ap(), grow.ap(),
                start=True, stop=True, skip_group_check=True))
            P.a(lambda e: e.copy(ctx.grep.ap(), psA[:, 0:512]))
        for tg in range(8):
            for kt in range(16):
                P.g(lambda e, tg=tg, kt=kt: e.dma_start(
                    out=ctx.xtile.ap(),
                    in_=dram["xT"][kt * 128:(kt + 1) * 128, tg * 512:(tg + 1) * 512]))
                P.g(lambda e, wname=wname, kt=kt: e.dma_start(
                    out=ctx.wblk.ap(),
                    in_=dram[wname][kt * 128:(kt + 1) * 128, :]))
                for sub in range(4):
                    P.t(lambda e, kt=kt, sub=sub: e.matmul(
                        banks[sub][:, 0:512],
                        ctx.xtile.ap()[:, sub * 128:(sub + 1) * 128],
                        ctx.wblk.ap(), start=(kt == 0), stop=(kt == 15),
                        skip_group_check=True))
                if which == "v":
                    P.t(lambda e, kt=kt: e.matmul(
                        psPV[0:32, 0:512], ctx.w1t.ap()[:, kt * 32:(kt + 1) * 32],
                        ctx.xtile.ap()[:, 0:512],
                        start=(kt == 0), stop=(kt == 15), skip_group_check=True))
                    P.t(lambda e, kt=kt: e.matmul(
                        big1[0:64, 0:512], ctx.bgTw.ap()[:, kt * 64:(kt + 1) * 64],
                        ctx.xtile.ap()[:, 0:512],
                        start=(kt == 0), stop=(kt == 15), skip_group_check=True))
            if which in ("q", "k"):
                for sub in range(4):
                    emit_rms_apply(P, ctx, banks[sub][:, 0:512], ctx.grep, dst,
                                   tg * 4 + sub, 1e-6)
            else:
                for sub in range(4):
                    tt = tg * 4 + sub
                    P.v(lambda e, sub=sub, tt=tt: e.tensor_copy(
                        ctx.v65.ap()[:, tt * 520: tt * 520 + 520]
                        .rearrange("p (h c) -> p h c", c=65)[:, :, 0:64],
                        banks[sub][:, 0:512].rearrange("p (h d) -> p h d", d=64)))
                P.a(lambda e, tg=tg: e.copy(ctx.w0T.ap()[:, tg * 512:(tg + 1) * 512],
                                            psPV[0:32, 0:512]))
                emit_bg(P, ctx, tg)

    emit_wchain(P, ctx)


def emit_bg(P, ctx, tg):
    big1, big2 = ctx.big1.ap(), ctx.big2.ap()
    psT = ctx.psT.ap()
    P.a(lambda e: e.activation(ctx.bgS.ap()[0:8, :], big1[0:8, 0:512], AF.Sigmoid,
                               bias=ctx.bgb.ap()[0:8, :]))
    P.v(lambda e: e.tensor_scalar_mul(ctx.bgB.ap()[0:8, :], ctx.bgS.ap()[0:8, :], 2.0))
    for sub in range(4):
        P.t(lambda e, sub=sub: e.transpose(big2[:, 0:8],
                                           ctx.bgB.ap()[0:8, sub * 128:(sub + 1) * 128],
                                           ctx.idf.ap()[0:8, 0:8]))
        P.a(lambda e, sub=sub, tg=tg: e.copy(
            ctx.bcol.ap()[:, (tg * 4 + sub) * 8:(tg * 4 + sub) * 8 + 8], big2[:, 0:8]))
    P.v(lambda e: e.tensor_scalar_add(ctx.bgB.ap()[32:40, :], big1[32:40, 0:512],
                                      ctx.bgb.ap()[32:40, :]))
    P.a(lambda e: e.activation(ctx.bgS.ap()[32:40, :], ctx.bgB.ap()[32:40, :],
                               AF.Relu, scale=-1.0))
    P.a(lambda e: e.activation(ctx.bgS2.ap()[32:40, :], ctx.bgB.ap()[32:40, :], AF.Abs))
    P.brk2()
    P.a(lambda e: e.activation(ctx.bgB.ap()[32:40, :], ctx.bgS2.ap()[32:40, :],
                               AF.Exp, scale=-1.0))
    P.brk2()
    P.a(lambda e: e.activation(ctx.bgS2.ap()[32:40, :], ctx.bgB.ap()[32:40, :],
                               AF.Ln, bias=1.0))
    P.v(lambda e: e.scalar_tensor_tensor(ctx.bgB.ap()[32:40, :],
                                         ctx.bgS2.ap()[32:40, :], -1.0,
                                         ctx.bgS.ap()[32:40, :], AL.mult, AL.subtract))
    for sub in range(4):
        P.t(lambda e, sub=sub: e.transpose(big2[:, 0:8],
                                           ctx.bgB.ap()[32:40, sub * 128:(sub + 1) * 128],
                                           ctx.idf.ap()[32:40, 32:40]))
        P.a(lambda e, sub=sub: e.copy(ctx.gtok.ap()[:, sub * 8:(sub + 1) * 8],
                                      big2[:, 0:8]))
    for sub in range(4):
        P.t(lambda e, sub=sub: e.matmul(big2[:, 16:24], ctx.trilf.ap(),
                                        ctx.gtok.ap()[:, sub * 8:(sub + 1) * 8],
                                        start=True, stop=False, skip_group_check=True))
        P.t(lambda e: e.matmul(big2[:, 16:24], ctx.onesrow1.ap(), ctx.gcarry.ap(),
                               start=False, stop=True, skip_group_check=True))
        P.a(lambda e, sub=sub: e.copy(ctx.Gtok.ap()[:, sub * 8:(sub + 1) * 8],
                                      big2[:, 16:24]))
        P.t(lambda e, sub=sub: e.matmul(big2[0:1, 24:32], ctx.onescol.ap(),
                                        ctx.gtok.ap()[:, sub * 8:(sub + 1) * 8],
                                        start=True, stop=False, skip_group_check=True))
        P.t(lambda e: e.matmul(big2[0:1, 24:32], ctx.onesrow1.ap()[:, 0:1],
                               ctx.gcarry.ap(), start=False, stop=True,
                               skip_group_check=True))
        P.v(lambda e: e.tensor_copy(ctx.gcarry.ap(), big2[0:1, 24:32]))
        P.brk()
    P.v(lambda e: e.tensor_scalar_mul(ctx.msplit.ap(), ctx.Gtok.ap(), -1.0))
    P.a(lambda e: e.copy(ctx.spl.ap()[:, 0:32], ctx.msplit.ap()))
    P.v(lambda e: e.tensor_copy(ctx.splf.ap(), ctx.spl.ap()[:, 0:32]))
    P.brk()
    P.v(lambda e: e.tensor_tensor(ctx.msplit.ap(), ctx.msplit.ap(), ctx.splf.ap(),
                                  AL.subtract))
    P.a(lambda e: e.copy(ctx.spl.ap()[:, 32:64], ctx.msplit.ap()))
    P.v(lambda e: e.tensor_copy(ctx.splf.ap(), ctx.spl.ap()[:, 32:64]))
    P.brk()
    P.v(lambda e: e.tensor_tensor(ctx.msplit.ap(), ctx.msplit.ap(), ctx.splf.ap(),
                                  AL.subtract))
    P.a(lambda e: e.copy(ctx.spl.ap()[:, 64:96], ctx.msplit.ap()))
    for sub in range(4):
        tt = tg * 4 + sub
        P.v(lambda e, sub=sub, tt=tt: e.tensor_copy(
            ctx.negG3_tm.ap()[:, tt * 24: tt * 24 + 24]
            .rearrange("p (h s) -> p h s", s=3),
            ctx.spl.ap().rearrange("p (s q) -> p s q", q=32)[:, :, sub * 8:(sub + 1) * 8]
            .rearrange("p s h -> p h s")))
    for sub in range(4):
        tt = tg * 4 + sub
        P.t(lambda e, tt=tt: e.transpose(ctx.psT.ap()[0:24, 0:128],
                                         ctx.negG3_tm.ap()[:, tt * 24: tt * 24 + 24],
                                         ctx.idb.ap()))
        P.a(lambda e: e.copy(ctx.stage24.ap(), ctx.psT.ap()[0:24, 0:128]))
        P.s(lambda e, tt=tt: e.dma_start(
            out=P.negG3_hbm.rearrange("h s t -> (h s) t")[:, tt * 128:(tt + 1) * 128],
            in_=ctx.stage24.ap()))


def emit_wchain(P, ctx):
    psA, big1, big2, psT = ctx.psA.ap(), ctx.big1.ap(), ctx.big2.ap(), ctx.psT.ap()
    for dt in range(4):
        P.v(lambda e: e.memset(ctx.wpre.ap()[:, 0:4], 0.0))
        for half in range(2):
            for tgh in range(4):
                P.t(lambda e, dt=dt, half=half, tgh=tgh: e.matmul(
                    psA[:, 0:512], ctx.w2t.ap()[:, dt * 128:(dt + 1) * 128],
                    ctx.w0T.ap()[:, half * 2048 + tgh * 512: half * 2048 + (tgh + 1) * 512],
                    start=True, stop=True, skip_group_check=True))
                P.a(lambda e, tgh=tgh: e.copy(
                    ctx.wpre.ap()[:, 4 + tgh * 512: 4 + (tgh + 1) * 512], psA[:, 0:512]))
            P.v(lambda e, dt=dt: e.tensor_scalar_mul(
                ctx.wconv.ap(), ctx.wpre.ap()[:, 1:2049], ctx.convs.ap()[:, dt * 4: dt * 4 + 1]))
            for k in range(1, 4):
                P.brk()
                P.v(lambda e, dt=dt, k=k: e.scalar_tensor_tensor(
                    ctx.wconv.ap(), ctx.wpre.ap()[:, 1 + k: 2049 + k],
                    ctx.convs.ap()[:, dt * 4 + k: dt * 4 + k + 1], ctx.wconv.ap(),
                    AL.mult, AL.add))
            P.brk()
            P.v(lambda e: e.tensor_copy(ctx.wpre.ap()[:, 0:4], ctx.wpre.ap()[:, 2048:2052]))
            P.a(lambda e: e.activation(ctx.wpre.ap()[:, 4:2052], ctx.wconv.ap(), AF.Square))
            for tgh in range(4):
                P.t(lambda e, tgh=tgh: e.matmul(
                    big1[0:2, 0:512], ctx.onesblk.ap(),
                    ctx.wpre.ap()[:, 4 + tgh * 512: 4 + (tgh + 1) * 512],
                    start=True, stop=True, skip_group_check=True))
                P.v(lambda e: e.tensor_scalar_add(ctx.nrm2.ap(), big1[0:2, 0:512], 1e-12))
                P.a(lambda e: e.activation(ctx.bgS.ap()[0:2, :], ctx.nrm2.ap(), AF.Ln))
                P.brk2()
                P.a(lambda e: e.activation(ctx.nrm2.ap(), ctx.bgS.ap()[0:2, :],
                                           AF.Exp, scale=-0.5))
                P.t(lambda e: e.matmul(big2[:, 0:512], ctx.onesblkT.ap(), ctx.nrm2.ap(),
                                       start=True, stop=True, skip_group_check=True))
                P.v(lambda e, tgh=tgh: e.tensor_tensor(
                    ctx.wpre.ap()[:, 4 + tgh * 512: 4 + (tgh + 1) * 512],
                    ctx.wconv.ap()[:, tgh * 512:(tgh + 1) * 512], big2[:, 0:512], AL.mult))
                for s4 in range(4):
                    tt = half * 16 + tgh * 4 + s4
                    P.t(lambda e, tgh=tgh, s4=s4: e.transpose(
                        psT[:, 0:128],
                        ctx.wpre.ap()[:, 4 + tgh * 512 + s4 * 128: 4 + tgh * 512 + (s4 + 1) * 128],
                        ctx.idb.ap()))
                    P.a(lambda e, tt=tt, dt=dt: e.copy(
                        ctx.w_tok.ap()[:, tt * 512 + dt * 128: tt * 512 + (dt + 1) * 128],
                        psT[:, 0:128]))


def emit_head(P, ctx, h, dram, win=WIN):
    n_chunks = T // CHUNK
    P.s(lambda e: e.dma_start(out=ctx.Kd.ap()[64:67, :],
                              in_=dram["c_ones3T"][:, 0:T]))
    P.s(lambda e, h=h: e.dma_start(out=ctx.Kd.ap()[67:70, :], in_=P.negG3_hbm[h]))
    P.s(lambda e: e.dma_start(out=ctx.q70.ap()[67:70, :],
                              in_=dram["c_ones3T"][:, 0:256]))
    P.v(lambda e: e.memset(ctx.q70.ap()[64:67, :], 0.0))
    for c in range(n_chunks):
        emit_chunk(P, ctx, h, c, win)


def emit_chunk(P, ctx, h, c, win):
    t0 = c * CHUNK
    tt = c
    r = c % win
    wstart = (c // win) * win * CHUNK

    qs = ctx.q_tok.ap()[:, tt * 64 * H + 64 * h: tt * 64 * H + 64 * (h + 1)]
    ks = ctx.k_tok.ap()[:, tt * 64 * H + 64 * h: tt * 64 * H + 64 * (h + 1)]
    ws = ctx.w_tok.ap()[:, tt * 64 * H + 64 * h: tt * 64 * H + 64 * (h + 1)]
    bc = ctx.bcol.ap()[:, tt * H + h: tt * H + h + 1]
    gsl = ctx.negG3_tm.ap()[:, tt * 3 * H + 3 * h: tt * 3 * H + 3 * (h + 1)]

    psA, psM, psQ = ctx.psA.ap(), ctx.psM.ap(), ctx.psQ.ap()
    big = [ctx.big1.ap(), ctx.big2.ap()]
    pv, psW, tp = ctx.psPV.ap(), ctx.psW.ap(), ctx.psT.ap()

    QbT = ctx.QbT.ap()[0:64, 0:128]
    KbT = ctx.QbT.ap()[0:64, 128:256]
    WbT = ctx.QbT.ap()[0:64, 256:384]

    P.t(lambda e: e.transpose(tp[0:64, 0:128], qs, ctx.idb.ap()))
    P.a(lambda e: e.copy(QbT, tp[0:64, 0:128]))
    P.t(lambda e: e.transpose(tp[0:64, 128:256], ks, ctx.idb.ap()))
    P.a(lambda e: e.copy(KbT, tp[0:64, 128:256]))
    P.t(lambda e: e.transpose(tp[0:64, 256:384], ws, ctx.idb.ap()))
    P.a(lambda e: e.copy(WbT, tp[0:64, 256:384]))
    P.t(lambda e: e.transpose(tp[0:3, 384:512], gsl, ctx.idb.ap()))
    P.a(lambda e: e.copy(ctx.negG3r.ap(), tp[0:3, 384:512]))
    P.a(lambda e: e.mul(ctx.QbTs.ap(), QbT, SCALE))
    P.v(lambda e: e.tensor_scalar(ctx.negbWb.ap(), ws, bc, -1.0, AL.mult, AL.mult))

    P.t(lambda e: e.matmul(psA[:, 0:128], WbT, WbT, start=True, stop=True,
                           skip_group_check=True))
    P.t(lambda e: e.matmul(psA[:, 128:256], WbT, KbT, start=True, stop=True,
                           skip_group_check=True))
    P.t(lambda e: e.matmul(psA[:, 256:384], WbT, QbT, start=True, stop=True,
                           skip_group_check=True))
    P.v(lambda e: e.scalar_tensor_tensor(ctx.Xp.ap()[:, 0:128], psA[:, 0:128], bc,
                                         ctx.mask_slneg.ap(), AL.mult, AL.mult))
    P.v(lambda e: e.tensor_tensor(ctx.X_u.ap(), psA[:, 0:128], ctx.mask_su.ap(), AL.mult))
    P.v(lambda e: e.tensor_tensor(ctx.negsWK.ap(), psA[:, 128:256],
                                  ctx.mask_slneg.ap(), AL.mult))
    P.v(lambda e: e.tensor_tensor(ctx.QWt_m.ap(), psA[:, 256:384],
                                  ctx.trilf.ap(), AL.mult))
    P.t(lambda e: e.transpose(psW[:, 0:128], ctx.Xp.ap()[:, 0:128], ctx.idf.ap()))
    P.a(lambda e: e.copy(ctx.Xp.ap()[:, 128:256], psW[:, 0:128]))

    P.v(lambda e: e.tensor_tensor(ctx.Mp.ap()[:, 0:128], ctx.Xp.ap()[:, 0:128],
                                  ctx.idf.ap(), AL.add))
    P.v(lambda e: e.tensor_tensor(ctx.Mp.ap()[:, 128:256], ctx.Xp.ap()[:, 128:256],
                                  ctx.idf.ap(), AL.add))
    for lev in range(6):
        P.t(lambda e: e.matmul(psM[:, 0:128], ctx.Xp.ap()[:, 128:256],
                               ctx.Xp.ap()[:, 0:128], start=True, stop=True,
                               skip_group_check=True))
        P.t(lambda e: e.matmul(psM[:, 128:256], ctx.Xp.ap()[:, 0:128],
                               ctx.Xp.ap()[:, 128:256], start=True, stop=True,
                               skip_group_check=True))
        P.a(lambda e: e.copy(ctx.Xp.ap()[:, 0:256], psM[:, 0:256]))
        P.t(lambda e: e.matmul(psM[:, 256:384], ctx.Mp.ap()[:, 128:256],
                               ctx.Xp.ap()[:, 0:128], start=True, stop=True,
                               skip_group_check=True))
        P.t(lambda e: e.matmul(psM[:, 384:512], ctx.Xp.ap()[:, 0:128],
                               ctx.Mp.ap()[:, 128:256], start=True, stop=True,
                               skip_group_check=True))
        P.v(lambda e: e.tensor_tensor(ctx.Mp.ap()[:, 0:256], ctx.Mp.ap()[:, 0:256],
                                      psM[:, 256:512], AL.add))

    Minv = ctx.Mp.ap()[:, 0:128]
    P.t(lambda e: e.matmul(psQ[:, 0:128], Minv, ctx.QWt_m.ap(), start=True,
                           stop=True, skip_group_check=True))
    P.a(lambda e: e.copy(ctx.RT.ap(), psQ[:, 0:128]))
    P.v(lambda e: e.tensor_scalar(ctx.RbT.ap(), ctx.RT.ap(), bc, None, AL.mult))
    P.a(lambda e: e.mul(ctx.RbTs.ap(), ctx.RbT.ap(), SCALE))
    P.t(lambda e: e.matmul(psQ[0:64, 128:256], ctx.negbWb.ap(), ctx.RT.ap(),
                           start=True, stop=True, skip_group_check=True))
    P.v(lambda e: e.tensor_tensor(ctx.QtT.ap(), psQ[0:64, 128:256], QbT, AL.add))
    P.a(lambda e: e.mul(ctx.q70.ap()[0:64, 128:256], ctx.QtT.ap(), SCALE))
    if r == 0:
        P.a(lambda e: e.mul(ctx.q70.ap()[0:64, 0:128], ctx.QtT.ap(), SCALE))
    else:
        P.t(lambda e: e.matmul(psQ[0:64, 256:384], ctx.S.ap()[:, 0:64], ctx.QtT.ap(),
                               start=True, stop=True, skip_group_check=True))
        P.a(lambda e: e.mul(ctx.q70.ap()[0:64, 0:128], psQ[0:64, 256:384], SCALE))

    groups = []
    for j0 in range(0, wstart, 512):
        groups.append((j0, min(512, wstart - j0), 0))
    if t0 > wstart:
        groups.append((wstart, t0 - wstart, 128))
    nmp = 0
    for gi, (j0, wdt, voff) in enumerate(groups):
        bk = big[gi % 2]
        P.t(lambda e, j0=j0, wdt=wdt, voff=voff, bk=bk:
            e.matmul(bk[:, 0:wdt], ctx.q70.ap()[:, voff:voff + 128],
                     ctx.Kd.ap()[:, j0:j0 + wdt], start=True, stop=True,
                     skip_group_check=True))
        P.v(lambda e, wdt=wdt, bk=bk, nmp=nmp:
            e.tensor_reduce(ctx.mpart.ap()[:, nmp:nmp + 1], bk[:, 0:wdt],
                            AX.X, AL.max))
        nmp += 1
    bk = big[len(groups) % 2]
    P.t(lambda e, bk=bk: e.matmul(bk[:, 0:128], ctx.QbTs.ap(), KbT, start=True,
                                  stop=False, skip_group_check=True))
    P.t(lambda e, bk=bk: e.matmul(bk[:, 0:128], ctx.RbTs.ap(), ctx.negsWK.ap(),
                                  start=False, stop=False, skip_group_check=True))
    P.t(lambda e, bk=bk: e.matmul(bk[:, 0:128], ctx.ones3.ap(), ctx.negG3r.ap(),
                                  start=False, stop=False, skip_group_check=True))
    P.t(lambda e, bk=bk: e.matmul(bk[:, 0:128], ctx.maskA.ap(), ctx.idb.ap(),
                                  start=False, stop=True, skip_group_check=True))
    P.v(lambda e, bk=bk, nmp=nmp: e.tensor_reduce(ctx.mpart.ap()[:, nmp:nmp + 1],
                                                  bk[:, 0:128], AX.X, AL.max))
    nmp += 1
    P.brk()
    P.v(lambda e, nmp=nmp: e.tensor_reduce(ctx.mcol.ap()[:, 0:1],
                                           ctx.mpart.ap()[:, 0:nmp], AX.X, AL.max))
    P.a(lambda e: e.mul(ctx.negm_sb.ap(), ctx.mcol.ap()[:, 0:1], -1.0))
    P.t(lambda e: e.transpose(psW[0:1, 0:128], ctx.negm_sb.ap(), ctx.idf.ap()))
    P.a(lambda e: e.copy(ctx.negmrow4.ap()[:, 0:128], psW[0:1, 0:128]))
    for rr in range(1, 4):
        P.v(lambda e, rr=rr: e.tensor_copy(ctx.negmrow4.ap()[:, rr * 128:(rr + 1) * 128],
                                           ctx.negmrow4.ap()[:, 0:128]))

    n_jt = t0 // 128
    total_jt = n_jt + 1
    pv_count = [0]

    def emit_pv(jt, pslice):
        vsl = ctx.v65.ap()[:, jt * 65 * H + 65 * h: jt * 65 * H + 65 * h + 65]
        first = pv_count[0] == 0
        last = pv_count[0] == total_jt - 1
        P.t(lambda e, vsl=vsl, pslice=pslice, first=first, last=last: e.matmul(
            pv[0:65, 0:128], vsl, pslice, start=first, stop=last,
            skip_group_check=True))
        pv_count[0] += 1

    pos = 0
    bswap = 0
    while pos < n_jt:
        grp = []
        while pos < n_jt and len(grp) < 4:
            grp.append(pos)
            pos += 1
        bk2 = big[bswap % 2]
        bswap += 1
        for gi, jt in enumerate(grp):
            voff = 0 if (jt * 128) < wstart else 128
            P.t(lambda e, jt=jt, gi=gi, voff=voff, bk2=bk2:
                e.matmul(bk2[:, gi * 128:(gi + 1) * 128],
                         ctx.Kd.ap()[:, jt * 128:(jt + 1) * 128],
                         ctx.q70.ap()[:, voff:voff + 128],
                         start=(gi == 0), stop=False, skip_group_check=True))
        nc_ = len(grp) * 128
        P.t(lambda e, nc_=nc_, bk2=bk2: e.matmul(
            bk2[:, 0:nc_], ctx.onesrow1.ap(), ctx.negmrow4.ap()[:, 0:nc_],
            start=False, stop=True, skip_group_check=True))
        pb = ctx.P_buf.ap()[:, (bswap % 2) * 512:(bswap % 2) * 512 + nc_]
        P.a(lambda e, pb=pb, nc_=nc_, bk2=bk2: e.activation(pb, bk2[:, 0:nc_], AF.Exp))
        for gi, jt in enumerate(grp):
            emit_pv(jt, ctx.P_buf.ap()[:, (bswap % 2) * 512 + gi * 128:
                                       (bswap % 2) * 512 + (gi + 1) * 128])
    bk2 = big[bswap % 2]
    P.t(lambda e, bk2=bk2: e.matmul(bk2[:, 0:128], KbT, ctx.QbTs.ap(), start=True,
                                    stop=False, skip_group_check=True))
    P.t(lambda e, bk2=bk2: e.matmul(bk2[:, 0:128], ctx.negsWK.ap(), ctx.RbTs.ap(),
                                    start=False, stop=False, skip_group_check=True))
    P.t(lambda e, bk2=bk2: e.matmul(bk2[:, 0:128], ctx.negG3r.ap(), ctx.ones3.ap(),
                                    start=False, stop=False, skip_group_check=True))
    P.t(lambda e, bk2=bk2: e.matmul(bk2[:, 0:128], ctx.onesrow1.ap(),
                                    ctx.negmrow4.ap()[:, 0:128], start=False, stop=False,
                                    skip_group_check=True))
    P.t(lambda e, bk2=bk2: e.matmul(bk2[:, 0:128], ctx.maskB.ap(), ctx.idb.ap(),
                                    start=False, stop=True, skip_group_check=True))
    P.a(lambda e, bk2=bk2: e.activation(ctx.P_buf.ap()[:, 896:1024],
                                        bk2[:, 0:128], AF.Exp))
    emit_pv(n_jt, ctx.P_buf.ap()[:, 896:1024])

    P.v(lambda e: e.reciprocal(ctx.zinv.ap(), pv[64:65, 0:128]))
    P.v(lambda e: e.tensor_copy(ctx.ot_raw.ap(), pv[0:64, 0:128]))
    P.t(lambda e: e.matmul(pv[0:64, 256:384], ctx.ones1_64.ap(), ctx.zinv.ap(),
                           start=True, stop=True, skip_group_check=True))
    P.a(lambda e: e.copy(ctx.zrep.ap(), pv[0:64, 256:384]))
    ob = ctx.oT.ap()[(h % 2) * 64:(h % 2) * 64 + 64, t0:t0 + CHUNK]
    P.v(lambda e, ob=ob: e.tensor_tensor(ob, ctx.ot_raw.ap(), ctx.zrep.ap(), AL.mult))

    P.t(lambda e: e.matmul(psQ[:, 0:128], Minv, ctx.X_u.ap(), start=True,
                           stop=True, skip_group_check=True))
    P.a(lambda e: e.copy(ctx.RwT.ap(), psQ[:, 0:128]))
    P.t(lambda e: e.matmul(psQ[:, 128:192], ctx.RwT.ap(), ctx.negbWb.ap(),
                           start=True, stop=True, skip_group_check=True))
    P.v(lambda e: e.tensor_tensor(ctx.V_sb.ap(), psQ[:, 128:192], ws, AL.add))
    P.t(lambda e: e.matmul(psQ[0:64, 192:256], ctx.V_sb.ap(), ctx.negbWb.ap(),
                           start=True, stop=True, skip_group_check=True))
    P.v(lambda e: e.tensor_tensor(ctx.McT.ap()[0:64, 0:64], psQ[0:64, 192:256],
                                  ctx.idf.ap()[0:64, 0:64], AL.add))
    P.v(lambda e: e.tensor_scalar(ctx.bsWK.ap(), ctx.negsWK.ap(), bc, -1.0,
                                  AL.mult, AL.mult))
    P.t(lambda e: e.matmul(psQ[:, 256:384], ctx.RwT.ap(), ctx.bsWK.ap(),
                           start=True, stop=True, skip_group_check=True))
    P.v(lambda e: e.scalar_tensor_tensor(ctx.D_sb.ap(), psQ[:, 256:384], -1.0,
                                         ctx.negsWK.ap(), AL.mult, AL.subtract))
    if t0 > wstart:
        wdt = t0 - wstart
        P.t(lambda e, wdt=wdt: e.matmul(psW[0:64, 0:wdt], ctx.McT.ap()[0:64, 0:64],
                                        ctx.Kd.ap()[0:64, wstart:t0],
                                        start=True, stop=True, skip_group_check=True))
        P.a(lambda e, wdt=wdt: e.copy(ctx.Kd.ap()[0:64, wstart:t0], psW[0:64, 0:wdt]))
    P.t(lambda e: e.matmul(psQ[0:64, 384:512], ctx.negbWb.ap(), ctx.D_sb.ap(),
                           start=True, stop=True, skip_group_check=True))
    P.v(lambda e: e.tensor_tensor(ctx.Kd.ap()[0:64, t0:t0 + CHUNK],
                                  psQ[0:64, 384:512], KbT, AL.add))
    if r == 0:
        P.t(lambda e: e.transpose(tp[0:64, 0:64], ctx.McT.ap()[0:64, 0:64],
                                  ctx.idb.ap()[0:64, 0:64]))
        P.a(lambda e: e.copy(ctx.S.ap()[:, 0:64], tp[0:64, 0:64]))
        P.a(lambda e: e.copy(ctx.S.ap()[:, 64:128], ctx.McT.ap()[0:64, 0:64]))
    else:
        P.t(lambda e: e.matmul(psW[0:64, 0:64], ctx.McT.ap()[0:64, 0:64],
                               ctx.S.ap()[:, 0:64], start=True, stop=True,
                               skip_group_check=True))
        P.t(lambda e: e.matmul(psW[0:64, 64:128], ctx.S.ap()[:, 0:64],
                               ctx.McT.ap()[0:64, 0:64], start=True, stop=True,
                               skip_group_check=True))
        P.a(lambda e: e.copy(ctx.S.ap()[:, 0:128], psW[0:64, 0:128]))
    if r == win - 1 and wstart > 0:
        for j0 in range(0, wstart, 512):
            wdt = min(512, wstart - j0)
            P.t(lambda e, j0=j0, wdt=wdt: e.matmul(
                psW[0:64, 0:wdt], ctx.S.ap()[:, 64:128],
                ctx.Kd.ap()[0:64, j0:j0 + wdt], start=True, stop=True,
                skip_group_check=True))
            P.a(lambda e, j0=j0, wdt=wdt: e.copy(ctx.Kd.ap()[0:64, j0:j0 + wdt],
                                                 psW[0:64, 0:wdt]))


def build_launchA():
    nc = bass.Bass("TRN2", target_bir_lowering=False)
    dram = {}
    for name, arr in CONST_SPECS.items():
        dt = BF16 if arr.dtype == BF else F32
        dram[name] = nc.dram_tensor(name, list(arr.shape), dt, kind="ExternalInput").ap()
    ins = [("xT", [D, T], F32), ("WqT", [D, 512], F32), ("WkT", [D, 512], F32),
           ("WvT", [D, 512], F32), ("W1T", [D, 32], F32), ("W2T", [32, 512], F32),
           ("bgT", [D, 64], F32), ("bgb", [64, 1], F32), ("convs", [128, 16], F32),
           ("qgrow", [1, 512], F32), ("kgrow", [1, 512], F32)]
    for name, shape, dt in ins:
        dram[name] = nc.dram_tensor("d_" + name, shape, dt, kind="ExternalInput").ap()
    oT_out = nc.dram_tensor("oT_out", [4, 128, T], BF16, kind="ExternalOutput").ap()
    negG3_hbm = nc.dram_tensor("negG3_scr", [H, 3, T], BF16, kind="Internal").ap()

    ctx = ACtx(nc)
    with nc.Block() as block:
        P = Prog(nc, nbar=8)
        P._junk = ctx.junk.ap()
        P.negG3_hbm = negG3_hbm
        emit_load_consts(P, ctx, dram)
        emit_prep(P, ctx, dram)
        for hh in range(H):
            emit_head(P, ctx, hh, dram)
            if hh % 2 == 1:
                P.s(lambda e, hh=hh: e.dma_start(out=oT_out[hh // 2],
                                                 in_=ctx.oT.ap()))
        P.v(lambda e: e.engine_nop())
        P.lower(block)
    return nc


class BCtx:
    def __init__(self, nc, actx=None):
        self._guards = []
        if actx is not None:
            def A(name, shape, dt):
                g = nc.sbuf_tensor(name, shape, dt)
                h = g.__enter__()
                self._guards.append(g)
                return h
            AP_ = None
        else:
            A = nc.alloc_sbuf_tensor
            AP_ = nc.alloc_psum_tensor
        self._actx = actx
        self.moe = A("moe", [128, 8 * 2048], F32)
        self.x2T = A("x2T", [128, 16 * 1024], BF16)
        self.he = A("he", [128, 11 * 1024], BF16)
        self.gblk = A("gblk", [128, 16 * 512], BF16)
        self.ublk = A("ublk", [128, 16 * 512], BF16)
        self.dblk = A("dblk", [128, 11 * 512], BF16)
        self.wo_blk = A("wo_blk", [128, 512], BF16)
        self.otk = A("otk", [128, 2048], BF16)
        self.xbuf = A("xbuf", [128, 2048], F32)
        self.x2b = A("x2b", [128, 2048], BF16)
        self.sqb = A("sqb", [128, 2048], F32)
        self.crep = A("crep", [128, 1024], BF16)
        self.rwT = A("rwT", [128, 16 * 8], BF16)
        self.comb = A("comb", [128, 64], F32)
        self.combb = A("combb", [128, 64], BF16)
        self.combT = A("combT", [8, 1024], BF16)
        self.sel = A("sel", [8, 1024], BF16)
        self.sgb = A("sgb", [128, 22], F32)
        self.sub = A("sub", [128, 22], F32)
        if actx is not None:
            self.idb, self.idf = actx.idb, actx.idf
            self.onesrow1, self.junk, self.fvals = actx.onesrow1, actx.junk, actx.fvals
        else:
            self.idb = A("idb", [128, 128], BF16)
            self.idf = A("idf", [128, 128], F32)
            self.onesrow1 = A("onesrow1", [1, 128], F32)
            self.junk = A("junk", [1, 8], F32)
            self.fvals = A("fvals", [128, 8], F32)
            nc.const_aps.register(F32, 1e-6, self.fvals.ap()[:, 0:1])
            nc.const_aps.register(F32, 1.0, self.fvals.ap()[:, 1:2])
            nc.const_aps.register(F32, 1e-5, self.fvals.ap()[:, 2:3])
            nc.const_aps.register(F32, 1e-12, self.fvals.ap()[:, 3:4])
        self.r1 = A("r1", [128, 1], F32)
        self.r2 = A("r2", [128, 1], F32)
        self.r3 = A("r3", [128, 1], F32)
        self.r4 = A("r4", [128, 1], F32)
        self.r5 = A("r5", [128, 8], F32)
        self.r6 = A("r6", [128, 8], F32)
        self.sg_s = A("sg_s", [128, 512], BF16)
        self.uc_s = A("uc_s", [128, 512], BF16)
        if actx is not None:
            self.ps = [actx.psA, actx.psM, actx.psQ, actx.big1, actx.big2,
                       actx.psPV, actx.psW]
            self.psT = actx.psT
        else:
            self.ps = [AP_(f"ps{i}", [128, 512], F32) for i in range(7)]
            self.psT = AP_("psTb", [128, 1024], BF16)


def emit_ln(P, ctx, src_ap, tile_cols, out_ap):
    P.v(lambda e: e.tensor_reduce(ctx.r1.ap(), src_ap, AX.X, AL.add))
    P.a(lambda e: e.activation(ctx.sqb.ap(), src_ap, AF.Square))
    P.v(lambda e: e.tensor_reduce(ctx.r2.ap(), ctx.sqb.ap(), AX.X, AL.add))
    P.a(lambda e: e.mul(ctx.r3.ap(), ctx.r1.ap(), 1.0 / 2048))
    P.brk2()
    P.a(lambda e: e.activation(ctx.r1.ap(), ctx.r3.ap(), AF.Square))
    P.v(lambda e: e.scalar_tensor_tensor(ctx.r2.ap(), ctx.r2.ap(), 1.0 / 2048,
                                         ctx.r1.ap(), AL.mult, AL.subtract))
    P.a(lambda e: e.activation(ctx.r4.ap(), ctx.r2.ap(), AF.Ln, bias=1e-5))
    P.brk2()
    P.a(lambda e: e.activation(ctx.r2.ap(), ctx.r4.ap(), AF.Exp, scale=-0.5))
    P.a(lambda e: e.mul(ctx.r4.ap(), ctx.r3.ap(), -1.0))
    P.v(lambda e: e.tensor_scalar(out_ap, src_ap, ctx.r4.ap(), ctx.r2.ap(),
                                  AL.add, AL.mult))


def emit_launchB(P, ctx, dram, fused=False):
    nc = P.nc
    ps = [p.ap() for p in ctx.ps]
    psT = ctx.psT.ap()

    def otk_load(e, tt):
        if fused:
            if "tq" not in P.regs:
                pid = e.partition_id()
                tq = e.scalar_reg_alu(AL.mod, pid, 4)
                P.regs["tq"] = e.scalar_reg_alu(AL.mult, tq, TQ)
            src3 = dram["oTq"].rearrange("(k p) t -> p k t", p=128)
            return e.dma_start(
                out=ctx.otk.ap().rearrange("p (k n) -> p k n", n=128),
                in_=src3[:, :, bass.ds(P.regs["tq"] + tt * 128, 128)])
        return e.dma_start(
            out=ctx.otk.ap().rearrange("p (k n) -> p k n", n=128),
            in_=dram["oTq"][:, tt * 128:(tt + 1) * 128]
            .rearrange("(k p) n -> p k n", p=128))

    P.g(lambda e: e.dma_start(out=ctx.rwT.ap().rearrange("p (k n) -> p k n", n=8),
                              in_=dram["rwT"].rearrange("(k p) n -> p k n", p=128)))
    P.s(lambda e: e.dma_start(out=ctx.sgb.ap(), in_=dram["sgb"]))
    P.s(lambda e: e.dma_start(out=ctx.sub.ap(), in_=dram["sub"]))
    P.s(lambda e: e.dma_start(out=ctx.idb.ap(), in_=dram["c_idb"]))
    P.s(lambda e: e.dma_start(out=ctx.idf.ap(), in_=dram["c_idf"]))
    P.s(lambda e: e.dma_start(out=ctx.onesrow1.ap(), in_=dram["c_onesrow1"]))
    P.s(lambda e: e.dma_start(out=ctx.fvals.ap(), in_=dram["c_fvals"]))
    P.s(lambda e: e.dma_start(out=ctx.sel.ap(), in_=dram["sel"]))

    for og in range(4):
        for tt in range(8):
            bk = ps[tt % 2]
            P.g(lambda e, tt=tt: otk_load(e, tt))
            for kt in range(16):
                P.g(lambda e, og=og, kt=kt: e.dma_start(
                    out=ctx.wo_blk.ap(),
                    in_=dram["WoT"][kt * 128:(kt + 1) * 128, og * 512:(og + 1) * 512]))
                P.t(lambda e, kt=kt, bk=bk: e.matmul(
                    bk[:, 0:512], ctx.otk.ap()[:, kt * 128:(kt + 1) * 128],
                    ctx.wo_blk.ap(), start=(kt == 0), stop=(kt == 15),
                    skip_group_check=True))
            P.a(lambda e, tt=tt, og=og, bk=bk: e.copy(
                ctx.moe.ap()[:, tt * 2048 + og * 512: tt * 2048 + (og + 1) * 512],
                bk[:, 0:512]))

    for tt in range(8):
        mt = ctx.moe.ap()[:, tt * 2048:(tt + 1) * 2048]
        P.s(lambda e, tt=tt: e.dma_start(out=ctx.xbuf.ap(),
                                         in_=dram["x_tok"][tt * 128:(tt + 1) * 128, :]))
        P.v(lambda e, mt=mt: e.tensor_tensor(mt, mt, ctx.xbuf.ap(), AL.add))
        P.brk()
        emit_ln(P, ctx, mt, 2048, ctx.xbuf.ap())
        P.brk()
        P.v(lambda e: e.tensor_copy(ctx.x2b.ap(), ctx.xbuf.ap()))
        P.a(lambda e, mt=mt: e.copy(mt, ctx.xbuf.ap()))
        for kt in range(16):
            P.t(lambda e, kt=kt: e.transpose(psT[:, 0:128],
                                             ctx.x2b.ap()[:, kt * 128:(kt + 1) * 128],
                                             ctx.idb.ap()))
            P.a(lambda e, kt=kt, tt=tt: e.copy(
                ctx.x2T.ap()[:, kt * 1024 + tt * 128: kt * 1024 + (tt + 1) * 128],
                psT[:, 0:128]))

    for tt in range(8):
        bk = ps[2]
        for kt in range(16):
            P.t(lambda e, kt=kt, tt=tt, bk=bk: e.matmul(
                bk[:, 0:8], ctx.x2T.ap()[:, kt * 1024 + tt * 128: kt * 1024 + (tt + 1) * 128],
                ctx.rwT.ap()[:, kt * 8:(kt + 1) * 8],
                start=(kt == 0), stop=(kt == 15), skip_group_check=True))
        P.v(lambda e, bk=bk: e.tensor_reduce(ctx.r1.ap(), bk[:, 0:8], AX.X, AL.max))
        P.a(lambda e: e.mul(ctx.r2.ap(), ctx.r1.ap(), -1.0))
        P.brk2()
        P.a(lambda e, bk=bk: e.activation(ctx.r5.ap(), bk[:, 0:8], AF.Exp,
                                          bias=ctx.r2.ap()))
        P.brk()
        P.v(lambda e: e.tensor_reduce(ctx.r1.ap(), ctx.r5.ap(), AX.X, AL.add))
        P.brk()
        P.v(lambda e: e.reciprocal(ctx.r2.ap(), ctx.r1.ap()))
        P.brk()
        P.v(lambda e: e.tensor_scalar(ctx.r5.ap(), ctx.r5.ap(), ctx.r2.ap(), None,
                                      AL.mult))
        P.brk()
        P.v(lambda e: e.tensor_reduce(ctx.r1.ap(), ctx.r5.ap(), AX.X, AL.max))
        P.brk()
        P.v(lambda e: e.tensor_scalar(ctx.r6.ap(), ctx.r5.ap(), ctx.r1.ap(), None,
                                      AL.is_ge))
        P.brk()
        P.v(lambda e: e.scalar_tensor_tensor(ctx.r6.ap(), ctx.r6.ap(), -2.0,
                                             ctx.r5.ap(), AL.mult, AL.add))
        P.brk()
        P.v(lambda e: e.tensor_reduce(ctx.r2.ap(), ctx.r6.ap(), AX.X, AL.max))
        P.brk()
        P.v(lambda e: e.tensor_tensor(ctx.r3.ap(), ctx.r1.ap(), ctx.r2.ap(), AL.add))
        P.brk()
        P.v(lambda e: e.reciprocal(ctx.r4.ap(), ctx.r3.ap()))
        P.v(lambda e: e.tensor_scalar(ctx.r6.ap(), ctx.r5.ap(), ctx.r2.ap(), None,
                                      AL.is_ge))
        P.brk()
        P.v(lambda e, tt=tt: e.scalar_tensor_tensor(
            ctx.comb.ap()[:, tt * 8:(tt + 1) * 8], ctx.r5.ap(), 1.0, ctx.r6.ap(),
            AL.mult, AL.mult))
        P.brk()
        P.v(lambda e, tt=tt: e.tensor_scalar(ctx.comb.ap()[:, tt * 8:(tt + 1) * 8],
                                             ctx.comb.ap()[:, tt * 8:(tt + 1) * 8],
                                             ctx.r4.ap(), None, AL.mult))
        P.brk()
        P.a(lambda e, tt=tt: e.copy(ctx.combb.ap()[:, tt * 8:(tt + 1) * 8],
                                    ctx.comb.ap()[:, tt * 8:(tt + 1) * 8]))
        P.t(lambda e, tt=tt: e.transpose(psT[0:8, 0:128],
                                         ctx.combb.ap()[:, tt * 8:(tt + 1) * 8],
                                         ctx.idb.ap()))
        P.a(lambda e, tt=tt: e.copy(ctx.combT.ap()[:, tt * 128:(tt + 1) * 128],
                                    psT[0:8, 0:128]))

    nit_list = [(0, 4), (4, 4), (8, 3)]
    for ex in range(E):
        for half in range(2):
            P.t(lambda e, ex=ex, half=half: e.matmul(
                ps[6][:, 0:512], ctx.sel.ap()[:, ex * 128:(ex + 1) * 128],
                ctx.combT.ap()[:, half * 512:(half + 1) * 512],
                start=True, stop=True, skip_group_check=True))
            P.a(lambda e, half=half: e.copy(ctx.crep.ap()[:, half * 512:(half + 1) * 512],
                                            ps[6][:, 0:512]))
        for (it0, itn) in nit_list:
            P.g(lambda e, ex=ex, it0=it0, itn=itn: e.dma_start(
                out=ctx.gblk.ap()[:, 0:16 * itn * 128]
                .rearrange("p (k n) -> p k n", n=itn * 128),
                in_=dram["gate_w"][ex][:, it0 * 128:(it0 + itn) * 128]
                .rearrange("(k p) n -> p k n", p=128)))
            P.g(lambda e, ex=ex, it0=it0, itn=itn: e.dma_start(
                out=ctx.ublk.ap()[:, 0:16 * itn * 128]
                .rearrange("p (k n) -> p k n", n=itn * 128),
                in_=dram["up_w"][ex][:, it0 * 128:(it0 + itn) * 128]
                .rearrange("(k p) n -> p k n", p=128)))
            for it in range(itn):
                for tg2 in range(2):
                    bkg, bku = ps[(it + tg2) % 2], ps[2 + (it + tg2) % 2]
                    for kt in range(16):
                        P.t(lambda e, kt=kt, it=it, itn=itn, tg2=tg2, bkg=bkg: e.matmul(
                            bkg[:, 0:512],
                            ctx.gblk.ap()[:, kt * itn * 128 + it * 128: kt * itn * 128 + (it + 1) * 128],
                            ctx.x2T.ap()[:, kt * 1024 + tg2 * 512: kt * 1024 + (tg2 + 1) * 512],
                            start=(kt == 0), stop=(kt == 15), skip_group_check=True))
                        P.t(lambda e, kt=kt, it=it, itn=itn, tg2=tg2, bku=bku: e.matmul(
                            bku[:, 0:512],
                            ctx.ublk.ap()[:, kt * itn * 128 + it * 128: kt * itn * 128 + (it + 1) * 128],
                            ctx.x2T.ap()[:, kt * 1024 + tg2 * 512: kt * 1024 + (tg2 + 1) * 512],
                            start=(kt == 0), stop=(kt == 15), skip_group_check=True))
                    P.a(lambda e, bkg=bkg: e.activation(ctx.sg_s.ap(), bkg[:, 0:512],
                                                        AF.Silu))
                    P.v(lambda e, bku=bku, tg2=tg2: e.tensor_tensor(
                        ctx.uc_s.ap(), bku[:, 0:512],
                        ctx.crep.ap()[:, tg2 * 512:(tg2 + 1) * 512], AL.mult))
                    P.brk()
                    P.v(lambda e, it0=it0, it=it, tg2=tg2: e.tensor_tensor(
                        ctx.he.ap()[:, (it0 + it) * 1024 + tg2 * 512:
                                    (it0 + it) * 1024 + (tg2 + 1) * 512],
                        ctx.sg_s.ap(), ctx.uc_s.ap(), AL.mult))
        for og in range(4):
            P.g(lambda e, ex=ex, og=og: e.dma_start(
                out=ctx.dblk.ap().rearrange("p (k n) -> p k n", n=512),
                in_=dram["down_w"][ex][:, og * 512:(og + 1) * 512]
                .rearrange("(k p) n -> p k n", p=128)))
            for tt in range(8):
                bk = ps[4 + tt % 2]
                for kt in range(11):
                    P.t(lambda e, kt=kt, tt=tt, bk=bk: e.matmul(
                        bk[:, 0:512],
                        ctx.he.ap()[:, kt * 1024 + tt * 128: kt * 1024 + tt * 128 + 128],
                        ctx.dblk.ap()[:, kt * 512:(kt + 1) * 512],
                        start=(kt == 0), stop=(kt == 10), skip_group_check=True))
                P.v(lambda e, tt=tt, og=og, bk=bk: e.tensor_tensor(
                    ctx.moe.ap()[:, tt * 2048 + og * 512: tt * 2048 + (og + 1) * 512],
                    ctx.moe.ap()[:, tt * 2048 + og * 512: tt * 2048 + (og + 1) * 512],
                    bk[:, 0:512], AL.add))

    sh_blocks = [(0, 4), (4, 4), (8, 3), (11, 4), (15, 4), (19, 3)]
    for (it0, itn) in sh_blocks:
        P.g(lambda e, it0=it0, itn=itn: e.dma_start(
            out=ctx.gblk.ap()[:, 0:16 * itn * 128]
            .rearrange("p (k n) -> p k n", n=itn * 128),
            in_=dram["s_gate_w"][:, it0 * 128:(it0 + itn) * 128]
            .rearrange("(k p) n -> p k n", p=128)))
        P.g(lambda e, it0=it0, itn=itn: e.dma_start(
            out=ctx.ublk.ap()[:, 0:16 * itn * 128]
            .rearrange("p (k n) -> p k n", n=itn * 128),
            in_=dram["s_up_w"][:, it0 * 128:(it0 + itn) * 128]
            .rearrange("(k p) n -> p k n", p=128)))
        for it in range(itn):
            itg = it0 + it
            ith = itg % 11
            for tg2 in range(2):
                bkg, bku = ps[(it + tg2) % 2], ps[2 + (it + tg2) % 2]
                for kt in range(16):
                    P.t(lambda e, kt=kt, it=it, itn=itn, tg2=tg2, bkg=bkg: e.matmul(
                        bkg[:, 0:512],
                        ctx.gblk.ap()[:, kt * itn * 128 + it * 128: kt * itn * 128 + (it + 1) * 128],
                        ctx.x2T.ap()[:, kt * 1024 + tg2 * 512: kt * 1024 + (tg2 + 1) * 512],
                        start=(kt == 0), stop=(kt == 15), skip_group_check=True))
                    P.t(lambda e, kt=kt, it=it, itn=itn, tg2=tg2, bku=bku: e.matmul(
                        bku[:, 0:512],
                        ctx.ublk.ap()[:, kt * itn * 128 + it * 128: kt * itn * 128 + (it + 1) * 128],
                        ctx.x2T.ap()[:, kt * 1024 + tg2 * 512: kt * 1024 + (tg2 + 1) * 512],
                        start=(kt == 0), stop=(kt == 15), skip_group_check=True))
                P.a(lambda e, bkg=bkg, itg=itg: e.activation(
                    ctx.sg_s.ap(), bkg[:, 0:512], AF.Silu,
                    bias=ctx.sgb.ap()[:, itg:itg + 1]))
                P.v(lambda e, bku=bku, itg=itg: e.tensor_scalar_add(
                    ctx.uc_s.ap(), bku[:, 0:512], ctx.sub.ap()[:, itg:itg + 1]))
                P.brk()
                P.v(lambda e, ith=ith, tg2=tg2: e.tensor_tensor(
                    ctx.he.ap()[:, ith * 1024 + tg2 * 512: ith * 1024 + (tg2 + 1) * 512],
                    ctx.sg_s.ap(), ctx.uc_s.ap(), AL.mult))
        if (it0 + itn) in (11, 22):
            halfE = 0 if (it0 + itn) == 11 else 1
            for og in range(4):
                P.g(lambda e, halfE=halfE, og=og: e.dma_start(
                    out=ctx.dblk.ap().rearrange("p (k n) -> p k n", n=512),
                    in_=dram["s_down_w"][halfE * 11 * 128: (halfE * 11 + 11) * 128,
                                         og * 512:(og + 1) * 512]
                    .rearrange("(k p) n -> p k n", p=128)))
                for tt in range(8):
                    bk = ps[4 + tt % 2]
                    for kt in range(11):
                        P.t(lambda e, kt=kt, tt=tt, bk=bk: e.matmul(
                            bk[:, 0:512],
                            ctx.he.ap()[:, kt * 1024 + tt * 128: kt * 1024 + tt * 128 + 128],
                            ctx.dblk.ap()[:, kt * 512:(kt + 1) * 512],
                            start=(kt == 0), stop=(kt == 10), skip_group_check=True))
                    P.v(lambda e, tt=tt, og=og, bk=bk: e.tensor_tensor(
                        ctx.moe.ap()[:, tt * 2048 + og * 512: tt * 2048 + (og + 1) * 512],
                        ctx.moe.ap()[:, tt * 2048 + og * 512: tt * 2048 + (og + 1) * 512],
                        bk[:, 0:512], AL.add))

    P.s(lambda e: e.dma_start(out=ctx.xbuf.ap()[0:1, :], in_=dram["sdb"]))
    for og in range(4):
        P.t(lambda e, og=og: e.matmul(ps[6][:, 0:512], ctx.onesrow1.ap(),
                                      ctx.xbuf.ap()[0:1, og * 512:(og + 1) * 512],
                                      start=True, stop=True, skip_group_check=True))
        for tt in range(8):
            P.v(lambda e, tt=tt, og=og: e.tensor_tensor(
                ctx.moe.ap()[:, tt * 2048 + og * 512: tt * 2048 + (og + 1) * 512],
                ctx.moe.ap()[:, tt * 2048 + og * 512: tt * 2048 + (og + 1) * 512],
                ps[6][:, 0:512], AL.add))
    for tt in range(8):
        mt = ctx.moe.ap()[:, tt * 2048:(tt + 1) * 2048]
        emit_ln(P, ctx, mt, 2048, ctx.xbuf.ap())
        P.s(lambda e, tt=tt: e.dma_start(out=P.out_q[tt * 128:(tt + 1) * 128, :],
                                         in_=ctx.xbuf.ap()))
        P.brk()


def build_launchB():
    nc = bass.Bass("TRN2", target_bir_lowering=False)
    dram = {}
    for name in ["c_idb", "c_idf", "c_onesrow1", "c_fvals"]:
        arr = CONST_SPECS[name]
        dt = BF16 if arr.dtype == BF else F32
        dram[name] = nc.dram_tensor(name, list(arr.shape), dt, kind="ExternalInput").ap()
    spec = [("oTq", [D, TQ], BF16), ("x_tok", [TQ, D], F32), ("WoT", [D, D], F32),
            ("rwT", [D, 8], F32), ("sel", [8, 1024], BF16),
            ("s_gate_w", [D, SI], F32), ("s_up_w", [D, SI], F32),
            ("s_down_w", [SI, D], F32), ("sgb", [128, 22], F32),
            ("sub", [128, 22], F32), ("sdb", [1, 2048], F32)]
    for name, shape, dt in spec:
        dram[name] = nc.dram_tensor("d_" + name, shape, dt, kind="ExternalInput").ap()
    dram["gate_w"] = nc.dram_tensor("d_gate_w", [E, D, I_FF], F32, kind="ExternalInput").ap()
    dram["up_w"] = nc.dram_tensor("d_up_w", [E, D, I_FF], F32, kind="ExternalInput").ap()
    dram["down_w"] = nc.dram_tensor("d_down_w", [E, I_FF, D], F32, kind="ExternalInput").ap()
    out_q = nc.dram_tensor("out_q", [TQ, D], F32, kind="ExternalOutput").ap()

    ctx = BCtx(nc)
    with nc.Block() as block:
        P = Prog(nc, nbar=8)
        P._junk = ctx.junk.ap()
        P.out_q = out_q
        emit_launchB(P, ctx, dram)
        P.v(lambda e: e.engine_nop())
        P.lower(block)
    return nc


def build_fused():
    nc = bass.Bass("TRN2", target_bir_lowering=False)
    dram = {}
    for name, arr in CONST_SPECS.items():
        dt = BF16 if arr.dtype == BF else F32
        dram[name] = nc.dram_tensor(name, list(arr.shape), dt, kind="ExternalInput").ap()
    insA = [("xT", [D, T], F32), ("WqT", [D, 512], F32), ("WkT", [D, 512], F32),
            ("WvT", [D, 512], F32), ("W1T", [D, 32], F32), ("W2T", [32, 512], F32),
            ("bgT", [D, 64], F32), ("bgb", [64, 1], F32), ("convs", [128, 16], F32),
            ("qgrow", [1, 512], F32), ("kgrow", [1, 512], F32)]
    insB = [("x_tok", [TQ, D], F32), ("WoT", [D, D], F32),
            ("rwT", [D, 8], F32), ("sel", [8, 1024], BF16),
            ("s_gate_w", [D, SI], F32), ("s_up_w", [D, SI], F32),
            ("s_down_w", [SI, D], F32), ("sgb", [128, 22], F32),
            ("sub", [128, 22], F32), ("sdb", [1, 2048], F32),
            ("gate_w", [E, D, I_FF], F32), ("up_w", [E, D, I_FF], F32),
            ("down_w", [E, I_FF, D], F32)]
    for name, shape, dt in insA + insB:
        dram[name] = nc.dram_tensor("d_" + name, shape, dt, kind="ExternalInput").ap()
    out_q = nc.dram_tensor("out_q", [TQ, D], F32, kind="ExternalOutput").ap()
    negG3_hbm = nc.dram_tensor("negG3_scr", [H, 3, T], BF16, kind="Internal").ap()
    cc_in = nc.dram_tensor("cc_in", [512, T], BF16, kind="Internal").ap()
    cc_out = nc.dram_tensor("cc_out", [D, T], BF16, kind="Internal").ap()

    ctx = ACtx(nc, guarded=True)
    with nc.Block() as block:
        P = Prog(nc, nbar=8)
        P._junk = ctx.junk.ap()
        P.negG3_hbm = negG3_hbm
        P.out_q = out_q
        emit_load_consts(P, ctx, dram)
        emit_prep(P, ctx, dram)
        for hh in range(H):
            emit_head(P, ctx, hh, dram)
            if hh % 2 == 1:
                P.s(lambda e, hh=hh: e.dma_start(
                    out=cc_in[(hh // 2) * 128:(hh // 2) * 128 + 128, :],
                    in_=ctx.oT.ap()))
        release_guards(ctx)
        ctxB = BCtx(nc, actx=ctx)
        P.cc(lambda e: e.collective_compute(
            "AllGather", AL.bypass, [[0, 1, 2, 3], [4, 5, 6, 7]],
            [cc_in], [cc_out]))
        emit_launchB(P, ctxB, dict(dram, oTq=cc_out), fused=True)
        P.v(lambda e: e.engine_nop())
        P.lower(block)
    return nc


def kernel(**inputs):
    consts = host_consts()
    x = np.asarray(inputs["L0_attn_hidden_states"], dtype=np.float32)

    nc = build_fused()
    selc = np.zeros((8, 1024), np.float32)
    for ex in range(8):
        selc[ex, ex * 128:(ex + 1) * 128] = 1.0
    gate_w = np.ascontiguousarray(np.asarray(inputs["L0_ffn_gate_w"], np.float32))
    up_w = np.ascontiguousarray(np.asarray(inputs["L0_ffn_up_w"], np.float32))
    down_w = np.ascontiguousarray(np.asarray(inputs["L0_ffn_down_w"], np.float32))
    s_gate_w = np.ascontiguousarray(np.asarray(inputs["L0_ffn_s_gate_w"], np.float32))
    s_up_w = np.ascontiguousarray(np.asarray(inputs["L0_ffn_s_up_w"], np.float32))
    s_down_w = np.ascontiguousarray(np.asarray(inputs["L0_ffn_s_down_w"], np.float32))
    WoT = np.ascontiguousarray(np.asarray(inputs["L0_attn_Wo"]).T.astype(np.float32))
    rwT = np.ascontiguousarray(np.asarray(inputs["L0_ffn_router_w"]).T.astype(np.float32))
    W1T = np.ascontiguousarray(np.asarray(inputs["L0_attn_W1"]).T.astype(np.float32))
    sgb_l = np.ascontiguousarray(
        np.asarray(inputs["L0_ffn_s_gate_b"], np.float32).reshape(22, 128).T)
    sub_l = np.ascontiguousarray(
        np.asarray(inputs["L0_ffn_s_up_b"], np.float32).reshape(22, 128).T)
    sdb_l = np.asarray(inputs["L0_ffn_s_down_b"], np.float32)[None, :]
    xT = [np.ascontiguousarray(x[b].T) for b in range(B)]

    in_maps = []
    for c in range(N_CORES):
        b = c // 4
        hg = c % 4
        tq = c % 4
        hs = slice(hg * 8 * HD, (hg + 1) * 8 * HD)
        hh = slice(hg * 8, (hg + 1) * 8)
        tsl = slice(tq * TQ, (tq + 1) * TQ)
        m = dict(consts)
        m["d_xT"] = xT[b]
        m["d_WqT"] = np.ascontiguousarray(np.asarray(inputs["L0_attn_Wq"])[hs].T.astype(np.float32))
        m["d_WkT"] = np.ascontiguousarray(np.asarray(inputs["L0_attn_Wk"])[hs].T.astype(np.float32))
        m["d_WvT"] = np.ascontiguousarray(np.asarray(inputs["L0_attn_Wv"])[hs].T.astype(np.float32))
        m["d_W1T"] = W1T
        m["d_W2T"] = np.ascontiguousarray(np.asarray(inputs["L0_attn_W2"]).T[:, hs].astype(np.float32))
        bg = np.zeros((D, 64), np.float32)
        bg[:, 0:8] = np.asarray(inputs["L0_attn_bt_w"])[hh].T
        bg[:, 32:40] = np.asarray(inputs["L0_attn_g_w"])[hh].T
        m["d_bgT"] = bg
        bgb = np.zeros((64, 1), np.float32)
        bgb[0:8, 0] = np.asarray(inputs["L0_attn_bt_b"])[hh]
        bgb[32:40, 0] = np.asarray(inputs["L0_attn_g_b"])[hh]
        m["d_bgb"] = bgb
        cw = np.asarray(inputs["L0_attn_conv_w"])[hs]
        m["d_convs"] = np.ascontiguousarray(
            cw.reshape(4, 128, 4).transpose(1, 0, 2).reshape(128, 16)).astype(np.float32)
        m["d_qgrow"] = np.tile(np.asarray(inputs["L0_attn_q_norm_w"], np.float32), 8)[None, :]
        m["d_kgrow"] = np.tile(np.asarray(inputs["L0_attn_k_norm_w"], np.float32), 8)[None, :]
        m["d_x_tok"] = np.ascontiguousarray(x[b][tsl])
        m["d_WoT"] = WoT
        m["d_rwT"] = rwT
        m["d_sel"] = selc.astype(BF)
        m["d_gate_w"] = gate_w
        m["d_up_w"] = up_w
        m["d_down_w"] = down_w
        m["d_s_gate_w"] = s_gate_w
        m["d_s_up_w"] = s_up_w
        m["d_s_down_w"] = s_down_w
        m["d_sgb"] = sgb_l
        m["d_sub"] = sub_l
        m["d_sdb"] = sdb_l
        in_maps.append(m)

    res = run_bass_kernel_spmd(nc, in_maps, core_ids=list(range(N_CORES)))
    out = np.zeros((B, T, D), dtype=np.float32)
    for c in range(N_CORES):
        b = c // 4
        tq = c % 4
        out[b, tq * TQ:(tq + 1) * TQ] = res.results[c]["out_q"]
    return out


# version 05
# speedup vs baseline: 1.1163x; 1.1163x over previous
import numpy as np
import ml_dtypes
import concourse.bass as bass
import concourse.mybir as mybir
from concourse.bass_utils import run_bass_kernel_spmd

F32 = mybir.dt.float32
BF16 = mybir.dt.bfloat16
AF = mybir.ActivationFunctionType
AL = mybir.AluOpType
AX = mybir.AxisListType
BF = ml_dtypes.bfloat16

BIGNEG = -8.0e37
DMA_ENGS = ("sync", "gpsimd")

B, T, D, H_ALL, HD = 2, 4096, 2048, 32, 64
H = 8
NT = T // 128
CHUNK = 128
NCH = T // CHUNK
WIN = 4
SCALE = HD ** (-0.5)
E, I_FF, SI = 8, 1408, 2816
TQ = 1024
N_CORES = 8


class Prog:
    ENGS = ("tensor", "vector", "scalar", "gpsimd", "sync")

    def __init__(self, nc, nbar=8):
        self.nc = nc
        self.steps = []
        self.nbar = nbar
        self._junk = None
        self.regs = {}
        self._tok_requests = []
        self._gd_count = 0
        self.gsem = None

    class Tok:
        __slots__ = ("slot", "val")
        def __init__(self):
            self.slot = None
            self.val = None

    def token(self):
        t = Prog.Tok()
        self._tok_requests.append((len(self.steps) - 1, t))
        return t

    def gd(self, fn, wait=None):
        n = self._gd_count
        self._gd_count += 1
        self.steps.append(("gpsimd", fn, ("gd", wait)))
        return n

    def gwait(self, n):
        def w(e):
            e.wait_ge(self.gsem, 16 * (n + 1))
        return w

    def brk(self):
        self.steps.append(("scalar", lambda e: e.memzero(self._junk), "n"))

    def brk2(self):
        self.steps.append(("vector", lambda e: e.memzero(self._junk), "n"))

    def t(self, fn): self.steps.append(("tensor", fn, "n"))
    def v(self, fn): self.steps.append(("vector", fn, "n"))
    def a(self, fn): self.steps.append(("scalar", fn, "n"))
    def g(self, fn): self.steps.append(("gpsimd", fn, "n"))
    def s(self, fn): self.steps.append(("sync", fn, "n"))
    def cc(self, fn): self.steps.append(("gpsimd", fn, "cc"))

    def lower(self, block):
        nc = self.nc
        sems = [nc.alloc_semaphore(f"bar{i}") for i in range(self.nbar)]
        cc_sem = nc.alloc_semaphore("cc_sem")
        self.gsem = nc.alloc_semaphore("gsem")
        runs = []
        step_run = {}
        for si, (eng, fn, kind) in enumerate(self.steps):
            if isinstance(kind, tuple) and kind[0] == "gd":
                runs.append((eng, [fn], "gd", kind[1]))
            elif (kind == "n" and runs and runs[-1][0] == eng
                  and runs[-1][2] == "n"):
                runs[-1][1].append(fn)
            else:
                runs.append((eng, [fn], kind, None))
            step_run[si] = len(runs) - 1
        runs = [r if len(r) == 4 else (r[0], r[1], r[2], None) for r in runs]
        nruns = len(runs)
        sem_tot = [0] * self.nbar
        run_wait = [None] * nruns
        cum_after = [None] * nruns
        prev_chain = None
        chain_idx = 0
        for i, (eng, fns, kind, extra) in enumerate(runs):
            if kind == "gd":
                continue
            if prev_chain is not None:
                run_wait[i] = cum_after[prev_chain]
            if kind == "cc":
                cum_after[i] = ("cc", 1)
            else:
                s = chain_idx % self.nbar
                amt = 16 * len(fns) if eng in DMA_ENGS else 1
                sem_tot[s] += amt
                cum_after[i] = (s, sem_tot[s])
            prev_chain = i
            chain_idx += 1
        for si, tok in self._tok_requests:
            ri = step_run[si]
            assert cum_after[ri] is not None, "token on a non-chained step"
            tok.slot, tok.val = cum_after[ri]
        per_eng = {e: [] for e in self.ENGS}
        for i, (eng, fns, kind, extra) in enumerate(runs):
            per_eng[eng].append((i, eng, fns, kind, extra))

        def make_body(items):
            def body(eng):
                for (i, ename, fns, kind, extra) in items:
                    if kind == "gd":
                        if extra is not None:
                            eng.wait_ge(cc_sem if extra.slot == "cc"
                                        else sems[extra.slot], extra.val)
                        fns[0](eng).then_inc(self.gsem, 16)
                        continue
                    if run_wait[i] is not None:
                        si_, val = run_wait[i]
                        eng.wait_ge(cc_sem if si_ == "cc" else sems[si_], val)
                    if kind == "cc":
                        fns[0](eng).then_inc(cc_sem)
                        continue
                    s = None
                    for j in range(self.nbar):
                        pass
                    s, _v = cum_after[i]
                    if ename in DMA_ENGS:
                        for fn in fns:
                            fn(eng).then_inc(sems[s], 16)
                    else:
                        last = None
                        for fn in fns:
                            rr = fn(eng)
                            if rr is not None:
                                last = rr
                        last.then_inc(sems[s], 1)
            return body
        for e in self.ENGS:
            if per_eng[e]:
                getattr(block, e)(make_body(per_eng[e]))


def cdiv(a, b):
    return (a + b - 1) // b


def host_consts():
    r = np.arange(128)
    c = {}
    c["c_idb"] = np.eye(128, dtype=np.float32).astype(BF)
    c["c_idf"] = np.eye(128, dtype=np.float32)
    c["c_mask_slneg"] = np.where(r[:, None] > r[None, :], -1.0, 0.0).astype(np.float32)
    c["c_mask_su"] = np.where(r[:, None] < r[None, :], 1.0, 0.0).astype(np.float32)
    c["c_mask_le"] = np.where(r[:, None] <= r[None, :], 1.0, 0.0).astype(np.float32)
    c["c_maskA"] = np.where(r[:, None] > r[None, :], BIGNEG, 0.0).astype(np.float32).astype(BF)
    c["c_maskB"] = np.where(r[:, None] < r[None, :], BIGNEG, 0.0).astype(np.float32).astype(BF)
    c["c_ones3"] = np.ones((3, 128), dtype=np.float32).astype(BF)
    c["c_ones1_64"] = np.ones((1, 64), dtype=np.float32)
    c["c_ones3T"] = np.ones((3, 4096), dtype=np.float32).astype(BF)
    c["c_onesblk"] = np.stack([(r < 64).astype(np.float32),
                               (r >= 64).astype(np.float32)], axis=1).astype(BF)
    c["c_onesblkT"] = np.ascontiguousarray(
        np.stack([(r < 64), (r >= 64)], axis=0).astype(np.float32))
    c["c_trilf"] = np.where(r[:, None] <= r[None, :], 1.0, 0.0).astype(np.float32)
    c["c_onescol"] = np.ones((128, 1), dtype=np.float32)
    c["c_fvals"] = np.tile(np.array([[1e-6, 1.0, 1e-5, 1e-12, 0, 0, 0, 0]],
                                    dtype=np.float32), (128, 1))
    c["c_onesrow1"] = np.ones((1, 128), dtype=np.float32)
    return c


CONST_SPECS = {k: v for k, v in host_consts().items()}


class ACtx:
    def __init__(self, nc, guarded=False):
        self._guards = []
        if guarded:
            def A(name, shape, dt):
                g = nc.sbuf_tensor(name, shape, dt)
                h = g.__enter__()
                self._guards.append(g)
                return h
        else:
            A = nc.alloc_sbuf_tensor
        AP_ = nc.alloc_psum_tensor
        AC = nc.alloc_sbuf_tensor
        self.idb = AC("idb", [128, 128], BF16)
        self.idf = AC("idf", [128, 128], F32)
        self.mask_slneg = AC("mask_slneg", [128, 128], F32)
        self.mask_su = AC("mask_su", [128, 128], F32)
        self.maskA = AC("maskA", [128, 128], BF16)
        self.maskB = AC("maskB", [128, 128], BF16)
        self.ones3 = AC("ones3", [3, 128], BF16)
        self.ones1_64 = AC("ones1_64", [1, 64], F32)
        self.onesblk = AC("onesblk", [128, 2], BF16)
        self.onesblkT = AC("onesblkT", [2, 128], F32)
        self.trilf = AC("trilf", [128, 128], F32)
        self.onescol = AC("onescol", [128, 1], F32)
        self.onesrow1 = AC("onesrow1", [1, 128], F32)
        self.junk = AC("junk", [1, 8], F32)
        self.fvals = AC("fvals", [128, 8], F32)
        nc.const_aps.register(F32, 1e-6, self.fvals.ap()[:, 0:1])
        nc.const_aps.register(F32, 1.0, self.fvals.ap()[:, 1:2])
        nc.const_aps.register(F32, 1e-5, self.fvals.ap()[:, 2:3])
        nc.const_aps.register(F32, 1e-12, self.fvals.ap()[:, 3:4])
        self.q_tok = A("q_tok", [128, NT * 64 * H], BF16)
        self.k_tok = A("k_tok", [128, NT * 64 * H], BF16)
        self.w_tok = A("w_tok", [128, NT * 64 * H], BF16)
        self.v65 = A("v65", [128, NT * 65 * H], BF16)
        self.bcol = A("bcol", [128, NT * H], F32)
        self.negG3_tm = A("negG3_tm", [128, NT * 3 * H], BF16)
        self.xtile = A("xtile", [128, 512], BF16)
        self.wblk = A("wblk", [128, 512], BF16)
        self.w1t = A("w1t", [128, 16 * 32], BF16)
        self.w2t = A("w2t", [32, 512], BF16)
        self.bgTw = A("bgTw", [128, 16 * 64], BF16)
        self.bgb = A("bgb", [64, 1], F32)
        self.qgrow = A("qgrow", [1, 512], F32)
        self.kgrow = A("kgrow", [1, 512], F32)
        self.grep = A("grep", [128, 512], F32)
        self.convs = A("convs", [128, 16], F32)
        self.sqbuf = A("sqbuf", [128, 512], F32)
        self.rstd8 = A("rstd8", [128, 8], F32)
        self.var8 = A("var8", [128, 8], F32)
        self.w0T = A("w0T", [32, T], BF16)
        self.wpre = A("wpre", [128, 2052], BF16)
        self.wconv = A("wconv", [128, 2048], F32)
        self.nrm2 = A("nrm2", [2, 512], F32)
        self.bgB = A("bgB", [40, 512], F32)
        self.bgS = A("bgS", [40, 512], F32)
        self.bgS2 = A("bgS2", [40, 512], F32)
        self.gtok = A("gtok", [128, 32], F32)
        self.Gtok = A("Gtok", [128, 32], F32)
        self.gcarry = A("gcarry", [1, 8], F32)
        self.msplit = A("msplit", [128, 32], F32)
        self.spl = A("spl", [128, 3 * 32], BF16)
        self.splf = A("splf", [128, 32], F32)
        self.stage24 = A("stage24", [24, 128], BF16)
        self.Kd = A("Kd70", [70, T], BF16)
        self.q70 = A("q70", [70, 256], BF16)
        self.S = A("S_sb", [64, 128], BF16)
        self.QbT = A("QbT", [64, 384], BF16)
        self.QbTs = A("QbTs", [64, 128], BF16)
        self.negbWb = A("negbWb", [128, 64], BF16)
        self.Xp = A("Xp", [128, 256], F32)
        self.Mp = A("Mp", [128, 256], F32)
        self.QWt_m = A("QWt_m", [128, 128], F32)
        self.RT = A("RT", [128, 128], BF16)
        self.RbT = A("RbT", [128, 128], BF16)
        self.RbTs = A("RbTs", [128, 128], BF16)
        self.QtT = A("QtT", [64, 128], BF16)
        self.negsWK = A("negsWK", [128, 128], BF16)
        self.bsWK = A("bsWK", [128, 128], BF16)
        self.X_u = A("X_u", [128, 128], F32)
        self.RwT = A("RwT", [128, 128], BF16)
        self.V_sb = A("V_sb", [128, 64], BF16)
        self.McT = A("McT", [64, 128], BF16)
        self.D_sb = A("D_sb", [128, 128], BF16)
        self.negG3r = A("negG3r", [3, 128], BF16)
        self.mpart = A("mpart", [128, 12], F32)
        self.mcol = A("mcol", [128, 1], F32)
        self.negm_sb = A("negm_sb", [128, 1], F32)
        self.negmrow4 = A("negmrow4", [1, 512], F32)
        self.zinv = A("zinv", [1, 128], F32)
        self.zrep = A("zrep", [64, 128], F32)
        self.ot_raw = A("ot_raw", [64, 128], F32)
        self.P_buf = A("P_buf", [128, 1024], BF16)
        self.oT = A("oT", [128, T], BF16)
        self.psA = AP_("psA", [128, 512], F32)
        self.psM = AP_("psM", [128, 512], F32)
        self.psQ = AP_("psQ", [128, 512], F32)
        self.big1 = AP_("big1", [128, 512], F32)
        self.big2 = AP_("big2", [128, 512], F32)
        self.psPV = AP_("psPV", [128, 512], F32)
        self.psW = AP_("psW", [128, 512], F32)
        self.psT = AP_("psT", [128, 1024], BF16)


def release_guards(ctx):
    for g in reversed(ctx._guards):
        g.__exit__(None, None, None)
    ctx._guards = []


def emit_load_consts(P, ctx, dram):
    pairs = [("c_idb", ctx.idb), ("c_idf", ctx.idf), ("c_mask_slneg", ctx.mask_slneg),
             ("c_mask_su", ctx.mask_su),
             ("c_maskA", ctx.maskA), ("c_maskB", ctx.maskB), ("c_ones3", ctx.ones3),
             ("c_ones1_64", ctx.ones1_64), ("c_onesblk", ctx.onesblk),
             ("c_onesblkT", ctx.onesblkT), ("c_trilf", ctx.trilf),
             ("c_onescol", ctx.onescol), ("c_onesrow1", ctx.onesrow1),
             ("c_fvals", ctx.fvals)]
    for name, dst in pairs:
        if name in dram:
            P.s(lambda e, name=name, dst=dst: e.dma_start(out=dst.ap(), in_=dram[name]))


def emit_rms_apply(P, ctx, ps, grep, dst_tok, tt, eps):
    P.a(lambda e, ps=ps: e.activation(ctx.sqbuf.ap(), ps, AF.Square))
    P.v(lambda e: e.tensor_reduce(ctx.var8.ap(),
                                  ctx.sqbuf.ap().rearrange("p (h d) -> p h d", d=64),
                                  AX.X, AL.add))
    P.a(lambda e, eps=eps: e.activation(ctx.rstd8.ap(), ctx.var8.ap(), AF.Ln,
                                        scale=1.0 / 64, bias=eps))
    P.brk2()
    P.a(lambda e: e.activation(ctx.var8.ap(), ctx.rstd8.ap(), AF.Exp, scale=-0.5))
    for seg in range(8):
        P.v(lambda e, ps=ps, seg=seg, tt=tt: e.scalar_tensor_tensor(
            dst_tok.ap()[:, tt * 512 + seg * 64: tt * 512 + (seg + 1) * 64],
            ps[:, seg * 64:(seg + 1) * 64], ctx.var8.ap()[:, seg:seg + 1],
            grep.ap()[:, seg * 64:(seg + 1) * 64], AL.mult, AL.mult))


def emit_prep(P, ctx, dram):
    nc = P.nc
    psA = ctx.psA.ap()
    banks = [ctx.psA.ap(), ctx.psM.ap(), ctx.psQ.ap(), ctx.psW.ap()]
    big1, big2, psPV, psT = ctx.big1.ap(), ctx.big2.ap(), ctx.psPV.ap(), ctx.psT.ap()

    P.g(lambda e: e.dma_start(out=ctx.w1t.ap().rearrange("p (k n) -> p k n", n=32),
                              in_=dram["W1T"].rearrange("(k p) n -> p k n", p=128)))
    P.g(lambda e: e.dma_start(out=ctx.w2t.ap(), in_=dram["W2T"]))
    P.g(lambda e: e.dma_start(out=ctx.bgTw.ap().rearrange("p (k n) -> p k n", n=64),
                              in_=dram["bgT"].rearrange("(k p) n -> p k n", p=128)))
    P.s(lambda e: e.dma_start(out=ctx.bgb.ap(), in_=dram["bgb"]))
    P.s(lambda e: e.dma_start(out=ctx.convs.ap(), in_=dram["convs"]))
    P.s(lambda e: e.dma_start(out=ctx.qgrow.ap(), in_=dram["qgrow"]))
    P.s(lambda e: e.dma_start(out=ctx.kgrow.ap(), in_=dram["kgrow"]))
    P.v(lambda e: e.memset(ctx.v65.ap(), 1.0))
    P.v(lambda e: e.memset(ctx.gcarry.ap(), 0.0))
    P.v(lambda e: e.memset(ctx.wpre.ap()[:, 0:4], 0.0))

    for which, wname, grow, dst in [("q", "WqT", ctx.qgrow, ctx.q_tok),
                                    ("k", "WkT", ctx.kgrow, ctx.k_tok),
                                    ("v", "WvT", None, ctx.v65)]:
        if grow is not None:
            P.t(lambda e, grow=grow: e.matmul(
                psA[:, 0:512], ctx.onesrow1.ap(), grow.ap(),
                start=True, stop=True, skip_group_check=True))
            P.a(lambda e: e.copy(ctx.grep.ap(), psA[:, 0:512]))
        for tg in range(8):
            for kt in range(16):
                P.g(lambda e, tg=tg, kt=kt: e.dma_start(
                    out=ctx.xtile.ap(),
                    in_=dram["xT"][kt * 128:(kt + 1) * 128, tg * 512:(tg + 1) * 512]))
                P.g(lambda e, wname=wname, kt=kt: e.dma_start(
                    out=ctx.wblk.ap(),
                    in_=dram[wname][kt * 128:(kt + 1) * 128, :]))
                for sub in range(4):
                    P.t(lambda e, kt=kt, sub=sub: e.matmul(
                        banks[sub][:, 0:512],
                        ctx.xtile.ap()[:, sub * 128:(sub + 1) * 128],
                        ctx.wblk.ap(), start=(kt == 0), stop=(kt == 15),
                        skip_group_check=True))
                if which == "v":
                    P.t(lambda e, kt=kt: e.matmul(
                        psPV[0:32, 0:512], ctx.w1t.ap()[:, kt * 32:(kt + 1) * 32],
                        ctx.xtile.ap()[:, 0:512],
                        start=(kt == 0), stop=(kt == 15), skip_group_check=True))
                    P.t(lambda e, kt=kt: e.matmul(
                        big1[0:64, 0:512], ctx.bgTw.ap()[:, kt * 64:(kt + 1) * 64],
                        ctx.xtile.ap()[:, 0:512],
                        start=(kt == 0), stop=(kt == 15), skip_group_check=True))
            if which in ("q", "k"):
                for sub in range(4):
                    emit_rms_apply(P, ctx, banks[sub][:, 0:512], ctx.grep, dst,
                                   tg * 4 + sub, 1e-6)
            else:
                for sub in range(4):
                    tt = tg * 4 + sub
                    P.v(lambda e, sub=sub, tt=tt: e.tensor_copy(
                        ctx.v65.ap()[:, tt * 520: tt * 520 + 520]
                        .rearrange("p (h c) -> p h c", c=65)[:, :, 0:64],
                        banks[sub][:, 0:512].rearrange("p (h d) -> p h d", d=64)))
                P.a(lambda e, tg=tg: e.copy(ctx.w0T.ap()[:, tg * 512:(tg + 1) * 512],
                                            psPV[0:32, 0:512]))
                emit_bg(P, ctx, tg)

    emit_wchain(P, ctx)


def emit_bg(P, ctx, tg):
    big1, big2 = ctx.big1.ap(), ctx.big2.ap()
    psT = ctx.psT.ap()
    P.a(lambda e: e.activation(ctx.bgS.ap()[0:8, :], big1[0:8, 0:512], AF.Sigmoid,
                               bias=ctx.bgb.ap()[0:8, :]))
    P.v(lambda e: e.tensor_scalar_mul(ctx.bgB.ap()[0:8, :], ctx.bgS.ap()[0:8, :], 2.0))
    for sub in range(4):
        P.t(lambda e, sub=sub: e.transpose(big2[:, 0:8],
                                           ctx.bgB.ap()[0:8, sub * 128:(sub + 1) * 128],
                                           ctx.idf.ap()[0:8, 0:8]))
        P.a(lambda e, sub=sub, tg=tg: e.copy(
            ctx.bcol.ap()[:, (tg * 4 + sub) * 8:(tg * 4 + sub) * 8 + 8], big2[:, 0:8]))
    P.v(lambda e: e.tensor_scalar_add(ctx.bgB.ap()[32:40, :], big1[32:40, 0:512],
                                      ctx.bgb.ap()[32:40, :]))
    P.a(lambda e: e.activation(ctx.bgS.ap()[32:40, :], ctx.bgB.ap()[32:40, :],
                               AF.Relu, scale=-1.0))
    P.a(lambda e: e.activation(ctx.bgS2.ap()[32:40, :], ctx.bgB.ap()[32:40, :], AF.Abs))
    P.brk2()
    P.a(lambda e: e.activation(ctx.bgB.ap()[32:40, :], ctx.bgS2.ap()[32:40, :],
                               AF.Exp, scale=-1.0))
    P.brk2()
    P.a(lambda e: e.activation(ctx.bgS2.ap()[32:40, :], ctx.bgB.ap()[32:40, :],
                               AF.Ln, bias=1.0))
    P.v(lambda e: e.scalar_tensor_tensor(ctx.bgB.ap()[32:40, :],
                                         ctx.bgS2.ap()[32:40, :], -1.0,
                                         ctx.bgS.ap()[32:40, :], AL.mult, AL.subtract))
    for sub in range(4):
        P.t(lambda e, sub=sub: e.transpose(big2[:, 0:8],
                                           ctx.bgB.ap()[32:40, sub * 128:(sub + 1) * 128],
                                           ctx.idf.ap()[32:40, 32:40]))
        P.a(lambda e, sub=sub: e.copy(ctx.gtok.ap()[:, sub * 8:(sub + 1) * 8],
                                      big2[:, 0:8]))
    for sub in range(4):
        P.t(lambda e, sub=sub: e.matmul(big2[:, 16:24], ctx.trilf.ap(),
                                        ctx.gtok.ap()[:, sub * 8:(sub + 1) * 8],
                                        start=True, stop=False, skip_group_check=True))
        P.t(lambda e: e.matmul(big2[:, 16:24], ctx.onesrow1.ap(), ctx.gcarry.ap(),
                               start=False, stop=True, skip_group_check=True))
        P.a(lambda e, sub=sub: e.copy(ctx.Gtok.ap()[:, sub * 8:(sub + 1) * 8],
                                      big2[:, 16:24]))
        P.t(lambda e, sub=sub: e.matmul(big2[0:1, 24:32], ctx.onescol.ap(),
                                        ctx.gtok.ap()[:, sub * 8:(sub + 1) * 8],
                                        start=True, stop=False, skip_group_check=True))
        P.t(lambda e: e.matmul(big2[0:1, 24:32], ctx.onesrow1.ap()[:, 0:1],
                               ctx.gcarry.ap(), start=False, stop=True,
                               skip_group_check=True))
        P.v(lambda e: e.tensor_copy(ctx.gcarry.ap(), big2[0:1, 24:32]))
        P.brk()
    P.v(lambda e: e.tensor_scalar_mul(ctx.msplit.ap(), ctx.Gtok.ap(), -1.0))
    P.a(lambda e: e.copy(ctx.spl.ap()[:, 0:32], ctx.msplit.ap()))
    P.v(lambda e: e.tensor_copy(ctx.splf.ap(), ctx.spl.ap()[:, 0:32]))
    P.brk()
    P.v(lambda e: e.tensor_tensor(ctx.msplit.ap(), ctx.msplit.ap(), ctx.splf.ap(),
                                  AL.subtract))
    P.a(lambda e: e.copy(ctx.spl.ap()[:, 32:64], ctx.msplit.ap()))
    P.v(lambda e: e.tensor_copy(ctx.splf.ap(), ctx.spl.ap()[:, 32:64]))
    P.brk()
    P.v(lambda e: e.tensor_tensor(ctx.msplit.ap(), ctx.msplit.ap(), ctx.splf.ap(),
                                  AL.subtract))
    P.a(lambda e: e.copy(ctx.spl.ap()[:, 64:96], ctx.msplit.ap()))
    for sub in range(4):
        tt = tg * 4 + sub
        P.v(lambda e, sub=sub, tt=tt: e.tensor_copy(
            ctx.negG3_tm.ap()[:, tt * 24: tt * 24 + 24]
            .rearrange("p (h s) -> p h s", s=3),
            ctx.spl.ap().rearrange("p (s q) -> p s q", q=32)[:, :, sub * 8:(sub + 1) * 8]
            .rearrange("p s h -> p h s")))
    for sub in range(4):
        tt = tg * 4 + sub
        P.t(lambda e, tt=tt: e.transpose(ctx.psT.ap()[0:24, 0:128],
                                         ctx.negG3_tm.ap()[:, tt * 24: tt * 24 + 24],
                                         ctx.idb.ap()))
        P.a(lambda e: e.copy(ctx.stage24.ap(), ctx.psT.ap()[0:24, 0:128]))
        P.s(lambda e, tt=tt: e.dma_start(
            out=P.negG3_hbm.rearrange("h s t -> (h s) t")[:, tt * 128:(tt + 1) * 128],
            in_=ctx.stage24.ap()))


def emit_wchain(P, ctx):
    psA, big1, big2, psT = ctx.psA.ap(), ctx.big1.ap(), ctx.big2.ap(), ctx.psT.ap()
    for dt in range(4):
        P.v(lambda e: e.memset(ctx.wpre.ap()[:, 0:4], 0.0))
        for half in range(2):
            for tgh in range(4):
                P.t(lambda e, dt=dt, half=half, tgh=tgh: e.matmul(
                    psA[:, 0:512], ctx.w2t.ap()[:, dt * 128:(dt + 1) * 128],
                    ctx.w0T.ap()[:, half * 2048 + tgh * 512: half * 2048 + (tgh + 1) * 512],
                    start=True, stop=True, skip_group_check=True))
                P.a(lambda e, tgh=tgh: e.copy(
                    ctx.wpre.ap()[:, 4 + tgh * 512: 4 + (tgh + 1) * 512], psA[:, 0:512]))
            P.v(lambda e, dt=dt: e.tensor_scalar_mul(
                ctx.wconv.ap(), ctx.wpre.ap()[:, 1:2049], ctx.convs.ap()[:, dt * 4: dt * 4 + 1]))
            for k in range(1, 4):
                P.brk()
                P.v(lambda e, dt=dt, k=k: e.scalar_tensor_tensor(
                    ctx.wconv.ap(), ctx.wpre.ap()[:, 1 + k: 2049 + k],
                    ctx.convs.ap()[:, dt * 4 + k: dt * 4 + k + 1], ctx.wconv.ap(),
                    AL.mult, AL.add))
            P.brk()
            P.v(lambda e: e.tensor_copy(ctx.wpre.ap()[:, 0:4], ctx.wpre.ap()[:, 2048:2052]))
            P.a(lambda e: e.activation(ctx.wpre.ap()[:, 4:2052], ctx.wconv.ap(), AF.Square))
            for tgh in range(4):
                P.t(lambda e, tgh=tgh: e.matmul(
                    big1[0:2, 0:512], ctx.onesblk.ap(),
                    ctx.wpre.ap()[:, 4 + tgh * 512: 4 + (tgh + 1) * 512],
                    start=True, stop=True, skip_group_check=True))
                P.v(lambda e: e.tensor_scalar_add(ctx.nrm2.ap(), big1[0:2, 0:512], 1e-12))
                P.a(lambda e: e.activation(ctx.bgS.ap()[0:2, :], ctx.nrm2.ap(), AF.Ln))
                P.brk2()
                P.a(lambda e: e.activation(ctx.nrm2.ap(), ctx.bgS.ap()[0:2, :],
                                           AF.Exp, scale=-0.5))
                P.t(lambda e: e.matmul(big2[:, 0:512], ctx.onesblkT.ap(), ctx.nrm2.ap(),
                                       start=True, stop=True, skip_group_check=True))
                P.v(lambda e, tgh=tgh: e.tensor_tensor(
                    ctx.wpre.ap()[:, 4 + tgh * 512: 4 + (tgh + 1) * 512],
                    ctx.wconv.ap()[:, tgh * 512:(tgh + 1) * 512], big2[:, 0:512], AL.mult))
                for s4 in range(4):
                    tt = half * 16 + tgh * 4 + s4
                    P.t(lambda e, tgh=tgh, s4=s4: e.transpose(
                        psT[:, 0:128],
                        ctx.wpre.ap()[:, 4 + tgh * 512 + s4 * 128: 4 + tgh * 512 + (s4 + 1) * 128],
                        ctx.idb.ap()))
                    P.a(lambda e, tt=tt, dt=dt: e.copy(
                        ctx.w_tok.ap()[:, tt * 512 + dt * 128: tt * 512 + (dt + 1) * 128],
                        psT[:, 0:128]))


def emit_head(P, ctx, h, dram, win=WIN):
    n_chunks = T // CHUNK
    P.s(lambda e: e.dma_start(out=ctx.Kd.ap()[64:67, :],
                              in_=dram["c_ones3T"][:, 0:T]))
    P.s(lambda e, h=h: e.dma_start(out=ctx.Kd.ap()[67:70, :], in_=P.negG3_hbm[h]))
    P.s(lambda e: e.dma_start(out=ctx.q70.ap()[67:70, :],
                              in_=dram["c_ones3T"][:, 0:256]))
    P.v(lambda e: e.memset(ctx.q70.ap()[64:67, :], 0.0))
    for c in range(n_chunks):
        emit_chunk(P, ctx, h, c, win)


def emit_chunk(P, ctx, h, c, win):
    t0 = c * CHUNK
    tt = c
    r = c % win
    wstart = (c // win) * win * CHUNK

    qs = ctx.q_tok.ap()[:, tt * 64 * H + 64 * h: tt * 64 * H + 64 * (h + 1)]
    ks = ctx.k_tok.ap()[:, tt * 64 * H + 64 * h: tt * 64 * H + 64 * (h + 1)]
    ws = ctx.w_tok.ap()[:, tt * 64 * H + 64 * h: tt * 64 * H + 64 * (h + 1)]
    bc = ctx.bcol.ap()[:, tt * H + h: tt * H + h + 1]
    gsl = ctx.negG3_tm.ap()[:, tt * 3 * H + 3 * h: tt * 3 * H + 3 * (h + 1)]

    psA, psM, psQ = ctx.psA.ap(), ctx.psM.ap(), ctx.psQ.ap()
    big = [ctx.big1.ap(), ctx.big2.ap()]
    pv, psW, tp = ctx.psPV.ap(), ctx.psW.ap(), ctx.psT.ap()

    QbT = ctx.QbT.ap()[0:64, 0:128]
    KbT = ctx.QbT.ap()[0:64, 128:256]
    WbT = ctx.QbT.ap()[0:64, 256:384]

    P.t(lambda e: e.transpose(tp[0:64, 0:128], qs, ctx.idb.ap()))
    P.a(lambda e: e.copy(QbT, tp[0:64, 0:128]))
    P.t(lambda e: e.transpose(tp[0:64, 128:256], ks, ctx.idb.ap()))
    P.a(lambda e: e.copy(KbT, tp[0:64, 128:256]))
    P.t(lambda e: e.transpose(tp[0:64, 256:384], ws, ctx.idb.ap()))
    P.a(lambda e: e.copy(WbT, tp[0:64, 256:384]))
    P.t(lambda e: e.transpose(tp[0:3, 384:512], gsl, ctx.idb.ap()))
    P.a(lambda e: e.copy(ctx.negG3r.ap(), tp[0:3, 384:512]))
    P.a(lambda e: e.mul(ctx.QbTs.ap(), QbT, SCALE))
    P.v(lambda e: e.tensor_scalar(ctx.negbWb.ap(), ws, bc, -1.0, AL.mult, AL.mult))

    P.t(lambda e: e.matmul(psA[:, 0:128], WbT, WbT, start=True, stop=True,
                           skip_group_check=True))
    P.t(lambda e: e.matmul(psA[:, 128:256], WbT, KbT, start=True, stop=True,
                           skip_group_check=True))
    P.t(lambda e: e.matmul(psA[:, 256:384], WbT, QbT, start=True, stop=True,
                           skip_group_check=True))
    P.v(lambda e: e.scalar_tensor_tensor(ctx.Xp.ap()[:, 0:128], psA[:, 0:128], bc,
                                         ctx.mask_slneg.ap(), AL.mult, AL.mult))
    P.v(lambda e: e.tensor_tensor(ctx.X_u.ap(), psA[:, 0:128], ctx.mask_su.ap(), AL.mult))
    P.v(lambda e: e.tensor_tensor(ctx.negsWK.ap(), psA[:, 128:256],
                                  ctx.mask_slneg.ap(), AL.mult))
    P.v(lambda e: e.tensor_tensor(ctx.QWt_m.ap(), psA[:, 256:384],
                                  ctx.trilf.ap(), AL.mult))
    P.t(lambda e: e.transpose(psW[:, 0:128], ctx.Xp.ap()[:, 0:128], ctx.idf.ap()))
    P.a(lambda e: e.copy(ctx.Xp.ap()[:, 128:256], psW[:, 0:128]))

    P.v(lambda e: e.tensor_tensor(ctx.Mp.ap()[:, 0:128], ctx.Xp.ap()[:, 0:128],
                                  ctx.idf.ap(), AL.add))
    P.v(lambda e: e.tensor_tensor(ctx.Mp.ap()[:, 128:256], ctx.Xp.ap()[:, 128:256],
                                  ctx.idf.ap(), AL.add))
    for lev in range(6):
        P.t(lambda e: e.matmul(psM[:, 0:128], ctx.Xp.ap()[:, 128:256],
                               ctx.Xp.ap()[:, 0:128], start=True, stop=True,
                               skip_group_check=True))
        P.t(lambda e: e.matmul(psM[:, 128:256], ctx.Xp.ap()[:, 0:128],
                               ctx.Xp.ap()[:, 128:256], start=True, stop=True,
                               skip_group_check=True))
        P.a(lambda e: e.copy(ctx.Xp.ap()[:, 0:256], psM[:, 0:256]))
        P.t(lambda e: e.matmul(psM[:, 256:384], ctx.Mp.ap()[:, 128:256],
                               ctx.Xp.ap()[:, 0:128], start=True, stop=True,
                               skip_group_check=True))
        P.t(lambda e: e.matmul(psM[:, 384:512], ctx.Xp.ap()[:, 0:128],
                               ctx.Mp.ap()[:, 128:256], start=True, stop=True,
                               skip_group_check=True))
        P.v(lambda e: e.tensor_tensor(ctx.Mp.ap()[:, 0:256], ctx.Mp.ap()[:, 0:256],
                                      psM[:, 256:512], AL.add))

    Minv = ctx.Mp.ap()[:, 0:128]
    P.t(lambda e: e.matmul(psQ[:, 0:128], Minv, ctx.QWt_m.ap(), start=True,
                           stop=True, skip_group_check=True))
    P.a(lambda e: e.copy(ctx.RT.ap(), psQ[:, 0:128]))
    P.v(lambda e: e.tensor_scalar(ctx.RbT.ap(), ctx.RT.ap(), bc, None, AL.mult))
    P.a(lambda e: e.mul(ctx.RbTs.ap(), ctx.RbT.ap(), SCALE))
    P.t(lambda e: e.matmul(psQ[0:64, 128:256], ctx.negbWb.ap(), ctx.RT.ap(),
                           start=True, stop=True, skip_group_check=True))
    P.v(lambda e: e.tensor_tensor(ctx.QtT.ap(), psQ[0:64, 128:256], QbT, AL.add))
    P.a(lambda e: e.mul(ctx.q70.ap()[0:64, 128:256], ctx.QtT.ap(), SCALE))
    if r == 0:
        P.a(lambda e: e.mul(ctx.q70.ap()[0:64, 0:128], ctx.QtT.ap(), SCALE))
    else:
        P.t(lambda e: e.matmul(psQ[0:64, 256:384], ctx.S.ap()[:, 0:64], ctx.QtT.ap(),
                               start=True, stop=True, skip_group_check=True))
        P.a(lambda e: e.mul(ctx.q70.ap()[0:64, 0:128], psQ[0:64, 256:384], SCALE))

    groups = []
    for j0 in range(0, wstart, 512):
        groups.append((j0, min(512, wstart - j0), 0))
    if t0 > wstart:
        groups.append((wstart, t0 - wstart, 128))
    nmp = 0
    for gi, (j0, wdt, voff) in enumerate(groups):
        bk = big[gi % 2]
        P.t(lambda e, j0=j0, wdt=wdt, voff=voff, bk=bk:
            e.matmul(bk[:, 0:wdt], ctx.q70.ap()[:, voff:voff + 128],
                     ctx.Kd.ap()[:, j0:j0 + wdt], start=True, stop=True,
                     skip_group_check=True))
        P.v(lambda e, wdt=wdt, bk=bk, nmp=nmp:
            e.tensor_reduce(ctx.mpart.ap()[:, nmp:nmp + 1], bk[:, 0:wdt],
                            AX.X, AL.max))
        nmp += 1
    bk = big[len(groups) % 2]
    P.t(lambda e, bk=bk: e.matmul(bk[:, 0:128], ctx.QbTs.ap(), KbT, start=True,
                                  stop=False, skip_group_check=True))
    P.t(lambda e, bk=bk: e.matmul(bk[:, 0:128], ctx.RbTs.ap(), ctx.negsWK.ap(),
                                  start=False, stop=False, skip_group_check=True))
    P.t(lambda e, bk=bk: e.matmul(bk[:, 0:128], ctx.ones3.ap(), ctx.negG3r.ap(),
                                  start=False, stop=False, skip_group_check=True))
    P.t(lambda e, bk=bk: e.matmul(bk[:, 0:128], ctx.maskA.ap(), ctx.idb.ap(),
                                  start=False, stop=True, skip_group_check=True))
    P.v(lambda e, bk=bk, nmp=nmp: e.tensor_reduce(ctx.mpart.ap()[:, nmp:nmp + 1],
                                                  bk[:, 0:128], AX.X, AL.max))
    nmp += 1
    P.brk()
    P.v(lambda e, nmp=nmp: e.tensor_reduce(ctx.mcol.ap()[:, 0:1],
                                           ctx.mpart.ap()[:, 0:nmp], AX.X, AL.max))
    P.a(lambda e: e.mul(ctx.negm_sb.ap(), ctx.mcol.ap()[:, 0:1], -1.0))
    P.t(lambda e: e.transpose(psW[0:1, 0:128], ctx.negm_sb.ap(), ctx.idf.ap()))
    P.a(lambda e: e.copy(ctx.negmrow4.ap()[:, 0:128], psW[0:1, 0:128]))
    for rr in range(1, 4):
        P.v(lambda e, rr=rr: e.tensor_copy(ctx.negmrow4.ap()[:, rr * 128:(rr + 1) * 128],
                                           ctx.negmrow4.ap()[:, 0:128]))

    n_jt = t0 // 128
    total_jt = n_jt + 1
    pv_count = [0]

    def emit_pv(jt, pslice):
        vsl = ctx.v65.ap()[:, jt * 65 * H + 65 * h: jt * 65 * H + 65 * h + 65]
        first = pv_count[0] == 0
        last = pv_count[0] == total_jt - 1
        P.t(lambda e, vsl=vsl, pslice=pslice, first=first, last=last: e.matmul(
            pv[0:65, 0:128], vsl, pslice, start=first, stop=last,
            skip_group_check=True))
        pv_count[0] += 1

    pos = 0
    bswap = 0
    while pos < n_jt:
        grp = []
        while pos < n_jt and len(grp) < 4:
            grp.append(pos)
            pos += 1
        bk2 = big[bswap % 2]
        bswap += 1
        for gi, jt in enumerate(grp):
            voff = 0 if (jt * 128) < wstart else 128
            P.t(lambda e, jt=jt, gi=gi, voff=voff, bk2=bk2:
                e.matmul(bk2[:, gi * 128:(gi + 1) * 128],
                         ctx.Kd.ap()[:, jt * 128:(jt + 1) * 128],
                         ctx.q70.ap()[:, voff:voff + 128],
                         start=(gi == 0), stop=False, skip_group_check=True))
        nc_ = len(grp) * 128
        P.t(lambda e, nc_=nc_, bk2=bk2: e.matmul(
            bk2[:, 0:nc_], ctx.onesrow1.ap(), ctx.negmrow4.ap()[:, 0:nc_],
            start=False, stop=True, skip_group_check=True))
        pb = ctx.P_buf.ap()[:, (bswap % 2) * 512:(bswap % 2) * 512 + nc_]
        P.a(lambda e, pb=pb, nc_=nc_, bk2=bk2: e.activation(pb, bk2[:, 0:nc_], AF.Exp))
        for gi, jt in enumerate(grp):
            emit_pv(jt, ctx.P_buf.ap()[:, (bswap % 2) * 512 + gi * 128:
                                       (bswap % 2) * 512 + (gi + 1) * 128])
    bk2 = big[bswap % 2]
    P.t(lambda e, bk2=bk2: e.matmul(bk2[:, 0:128], KbT, ctx.QbTs.ap(), start=True,
                                    stop=False, skip_group_check=True))
    P.t(lambda e, bk2=bk2: e.matmul(bk2[:, 0:128], ctx.negsWK.ap(), ctx.RbTs.ap(),
                                    start=False, stop=False, skip_group_check=True))
    P.t(lambda e, bk2=bk2: e.matmul(bk2[:, 0:128], ctx.negG3r.ap(), ctx.ones3.ap(),
                                    start=False, stop=False, skip_group_check=True))
    P.t(lambda e, bk2=bk2: e.matmul(bk2[:, 0:128], ctx.onesrow1.ap(),
                                    ctx.negmrow4.ap()[:, 0:128], start=False, stop=False,
                                    skip_group_check=True))
    P.t(lambda e, bk2=bk2: e.matmul(bk2[:, 0:128], ctx.maskB.ap(), ctx.idb.ap(),
                                    start=False, stop=True, skip_group_check=True))
    P.a(lambda e, bk2=bk2: e.activation(ctx.P_buf.ap()[:, 896:1024],
                                        bk2[:, 0:128], AF.Exp))
    emit_pv(n_jt, ctx.P_buf.ap()[:, 896:1024])

    P.v(lambda e: e.reciprocal(ctx.zinv.ap(), pv[64:65, 0:128]))
    P.v(lambda e: e.tensor_copy(ctx.ot_raw.ap(), pv[0:64, 0:128]))
    P.t(lambda e: e.matmul(pv[0:64, 256:384], ctx.ones1_64.ap(), ctx.zinv.ap(),
                           start=True, stop=True, skip_group_check=True))
    P.a(lambda e: e.copy(ctx.zrep.ap(), pv[0:64, 256:384]))
    ob = ctx.oT.ap()[(h % 2) * 64:(h % 2) * 64 + 64, t0:t0 + CHUNK]
    P.v(lambda e, ob=ob: e.tensor_tensor(ob, ctx.ot_raw.ap(), ctx.zrep.ap(), AL.mult))

    P.t(lambda e: e.matmul(psQ[:, 0:128], Minv, ctx.X_u.ap(), start=True,
                           stop=True, skip_group_check=True))
    P.a(lambda e: e.copy(ctx.RwT.ap(), psQ[:, 0:128]))
    P.t(lambda e: e.matmul(psQ[:, 128:192], ctx.RwT.ap(), ctx.negbWb.ap(),
                           start=True, stop=True, skip_group_check=True))
    P.v(lambda e: e.tensor_tensor(ctx.V_sb.ap(), psQ[:, 128:192], ws, AL.add))
    P.t(lambda e: e.matmul(psQ[0:64, 192:256], ctx.V_sb.ap(), ctx.negbWb.ap(),
                           start=True, stop=True, skip_group_check=True))
    P.v(lambda e: e.tensor_tensor(ctx.McT.ap()[0:64, 0:64], psQ[0:64, 192:256],
                                  ctx.idf.ap()[0:64, 0:64], AL.add))
    P.v(lambda e: e.tensor_scalar(ctx.bsWK.ap(), ctx.negsWK.ap(), bc, -1.0,
                                  AL.mult, AL.mult))
    P.t(lambda e: e.matmul(psQ[:, 256:384], ctx.RwT.ap(), ctx.bsWK.ap(),
                           start=True, stop=True, skip_group_check=True))
    P.v(lambda e: e.scalar_tensor_tensor(ctx.D_sb.ap(), psQ[:, 256:384], -1.0,
                                         ctx.negsWK.ap(), AL.mult, AL.subtract))
    if t0 > wstart:
        wdt = t0 - wstart
        P.t(lambda e, wdt=wdt: e.matmul(psW[0:64, 0:wdt], ctx.McT.ap()[0:64, 0:64],
                                        ctx.Kd.ap()[0:64, wstart:t0],
                                        start=True, stop=True, skip_group_check=True))
        P.a(lambda e, wdt=wdt: e.copy(ctx.Kd.ap()[0:64, wstart:t0], psW[0:64, 0:wdt]))
    P.t(lambda e: e.matmul(psQ[0:64, 384:512], ctx.negbWb.ap(), ctx.D_sb.ap(),
                           start=True, stop=True, skip_group_check=True))
    P.v(lambda e: e.tensor_tensor(ctx.Kd.ap()[0:64, t0:t0 + CHUNK],
                                  psQ[0:64, 384:512], KbT, AL.add))
    if r == 0:
        P.t(lambda e: e.transpose(tp[0:64, 0:64], ctx.McT.ap()[0:64, 0:64],
                                  ctx.idb.ap()[0:64, 0:64]))
        P.a(lambda e: e.copy(ctx.S.ap()[:, 0:64], tp[0:64, 0:64]))
        P.a(lambda e: e.copy(ctx.S.ap()[:, 64:128], ctx.McT.ap()[0:64, 0:64]))
    else:
        P.t(lambda e: e.matmul(psW[0:64, 0:64], ctx.McT.ap()[0:64, 0:64],
                               ctx.S.ap()[:, 0:64], start=True, stop=True,
                               skip_group_check=True))
        P.t(lambda e: e.matmul(psW[0:64, 64:128], ctx.S.ap()[:, 0:64],
                               ctx.McT.ap()[0:64, 0:64], start=True, stop=True,
                               skip_group_check=True))
        P.a(lambda e: e.copy(ctx.S.ap()[:, 0:128], psW[0:64, 0:128]))
    if r == win - 1 and wstart > 0:
        for j0 in range(0, wstart, 512):
            wdt = min(512, wstart - j0)
            P.t(lambda e, j0=j0, wdt=wdt: e.matmul(
                psW[0:64, 0:wdt], ctx.S.ap()[:, 64:128],
                ctx.Kd.ap()[0:64, j0:j0 + wdt], start=True, stop=True,
                skip_group_check=True))
            P.a(lambda e, j0=j0, wdt=wdt: e.copy(ctx.Kd.ap()[0:64, j0:j0 + wdt],
                                                 psW[0:64, 0:wdt]))


def build_launchA():
    nc = bass.Bass("TRN2", target_bir_lowering=False)
    dram = {}
    for name, arr in CONST_SPECS.items():
        dt = BF16 if arr.dtype == BF else F32
        dram[name] = nc.dram_tensor(name, list(arr.shape), dt, kind="ExternalInput").ap()
    ins = [("xT", [D, T], F32), ("WqT", [D, 512], F32), ("WkT", [D, 512], F32),
           ("WvT", [D, 512], F32), ("W1T", [D, 32], F32), ("W2T", [32, 512], F32),
           ("bgT", [D, 64], F32), ("bgb", [64, 1], F32), ("convs", [128, 16], F32),
           ("qgrow", [1, 512], F32), ("kgrow", [1, 512], F32)]
    for name, shape, dt in ins:
        dram[name] = nc.dram_tensor("d_" + name, shape, dt, kind="ExternalInput").ap()
    oT_out = nc.dram_tensor("oT_out", [4, 128, T], BF16, kind="ExternalOutput").ap()
    negG3_hbm = nc.dram_tensor("negG3_scr", [H, 3, T], BF16, kind="Internal").ap()

    ctx = ACtx(nc)
    with nc.Block() as block:
        P = Prog(nc, nbar=8)
        P._junk = ctx.junk.ap()
        P.negG3_hbm = negG3_hbm
        emit_load_consts(P, ctx, dram)
        emit_prep(P, ctx, dram)
        for hh in range(H):
            emit_head(P, ctx, hh, dram)
            if hh % 2 == 1:
                P.s(lambda e, hh=hh: e.dma_start(out=oT_out[hh // 2],
                                                 in_=ctx.oT.ap()))
        P.v(lambda e: e.engine_nop())
        P.lower(block)
    return nc


class BCtx:
    def __init__(self, nc, actx=None):
        self._guards = []
        if actx is not None:
            def A(name, shape, dt):
                g = nc.sbuf_tensor(name, shape, dt)
                h = g.__enter__()
                self._guards.append(g)
                return h
            AP_ = None
        else:
            A = nc.alloc_sbuf_tensor
            AP_ = nc.alloc_psum_tensor
        self._actx = actx
        self.moe = A("moe", [128, 8 * 2048], F32)
        self.x2T = A("x2T", [128, 16 * 1024], BF16)
        self.he = A("he", [128, 11 * 1024], BF16)
        self.gblk = A("gblk", [128, 2 * 16 * 256], BF16)
        self.ublk = A("ublk", [128, 2 * 16 * 256], BF16)
        self.dblk = A("dblk", [128, 2 * 11 * 512], BF16)
        self.otk = A("otk", [128, 2 * 2048], BF16)
        self.xbuf = A("xbuf", [128, 2048], F32)
        self.x2b = A("x2b", [128, 2048], BF16)
        self.crep = A("crep", [128, 1024], BF16)
        self.rwT = A("rwT", [128, 16 * 8], BF16)
        self.comb = A("comb", [128, 64], F32)
        self.combb = A("combb", [128, 64], BF16)
        self.combT = A("combT", [8, 1024], BF16)
        self.sel = A("sel", [8, 1024], BF16)
        self.sgb = A("sgb", [128, 22], F32)
        self.sub = A("sub", [128, 22], F32)
        if actx is not None:
            self.idb, self.idf = actx.idb, actx.idf
            self.onesrow1, self.junk, self.fvals = actx.onesrow1, actx.junk, actx.fvals
        else:
            self.idb = A("idb", [128, 128], BF16)
            self.idf = A("idf", [128, 128], F32)
            self.onesrow1 = A("onesrow1", [1, 128], F32)
            self.junk = A("junk", [1, 8], F32)
            self.fvals = A("fvals", [128, 8], F32)
            nc.const_aps.register(F32, 1e-6, self.fvals.ap()[:, 0:1])
            nc.const_aps.register(F32, 1.0, self.fvals.ap()[:, 1:2])
            nc.const_aps.register(F32, 1e-5, self.fvals.ap()[:, 2:3])
            nc.const_aps.register(F32, 1e-12, self.fvals.ap()[:, 3:4])
        self.r1 = A("r1", [128, 1], F32)
        self.r2 = A("r2", [128, 1], F32)
        self.r3 = A("r3", [128, 1], F32)
        self.r4 = A("r4", [128, 1], F32)
        self.r5 = A("r5", [128, 8], F32)
        self.r6 = A("r6", [128, 8], F32)
        self.sg_s = A("sg_s", [128, 512], BF16)
        self.uc_s = A("uc_s", [128, 512], BF16)
        if actx is not None:
            self.ps = [actx.psA, actx.psM, actx.psQ, actx.big1, actx.big2,
                       actx.psPV, actx.psW]
            self.psT = actx.psT
        else:
            self.ps = [AP_(f"ps{i}", [128, 512], F32) for i in range(7)]
            self.psT = AP_("psTb", [128, 1024], BF16)


def emit_ln(P, ctx, src_ap, tile_cols, out_ap):
    P.v(lambda e: e.tensor_reduce(ctx.r1.ap(), src_ap, AX.X, AL.add))
    P.a(lambda e: e.activation(ctx.xbuf.ap(), src_ap, AF.Square))
    P.v(lambda e: e.tensor_reduce(ctx.r2.ap(), ctx.xbuf.ap(), AX.X, AL.add))
    P.a(lambda e: e.mul(ctx.r3.ap(), ctx.r1.ap(), 1.0 / 2048))
    P.brk2()
    P.a(lambda e: e.activation(ctx.r1.ap(), ctx.r3.ap(), AF.Square))
    P.v(lambda e: e.scalar_tensor_tensor(ctx.r2.ap(), ctx.r2.ap(), 1.0 / 2048,
                                         ctx.r1.ap(), AL.mult, AL.subtract))
    P.a(lambda e: e.activation(ctx.r4.ap(), ctx.r2.ap(), AF.Ln, bias=1e-5))
    P.brk2()
    P.a(lambda e: e.activation(ctx.r2.ap(), ctx.r4.ap(), AF.Exp, scale=-0.5))
    P.a(lambda e: e.mul(ctx.r4.ap(), ctx.r3.ap(), -1.0))
    P.v(lambda e: e.tensor_scalar(out_ap, src_ap, ctx.r4.ap(), ctx.r2.ap(),
                                  AL.add, AL.mult))


def emit_launchB(P, ctx, dram, fused=False):
    nc = P.nc
    ps = [p.ap() for p in ctx.ps]
    psT = ctx.psT.ap()

    def otk_load(e, tt, par):
        dst = ctx.otk.ap()[:, par * 2048:(par + 1) * 2048] \
            .rearrange("p (k n) -> p k n", n=128)
        if fused:
            if "tq" not in P.regs:
                pid = e.partition_id()
                tq = e.scalar_reg_alu(AL.mod, pid, 4)
                P.regs["tq"] = e.scalar_reg_alu(AL.mult, tq, TQ)
            src3 = dram["oTq"].rearrange("(k p) t -> p k t", p=128)
            return e.dma_start(out=dst,
                               in_=src3[:, :, bass.ds(P.regs["tq"] + tt * 128, 128)])
        return e.dma_start(out=dst, in_=dram["oTq"][:, tt * 128:(tt + 1) * 128]
                           .rearrange("(k p) n -> p k n", p=128))

    n_rwT = P.gd(lambda e: e.dma_start(
        out=ctx.rwT.ap().rearrange("p (k n) -> p k n", n=8),
        in_=dram["rwT"].rearrange("(k p) n -> p k n", p=128)))
    P.s(lambda e: e.dma_start(out=ctx.sgb.ap(), in_=dram["sgb"]))
    P.s(lambda e: e.dma_start(out=ctx.sub.ap(), in_=dram["sub"]))
    P.s(lambda e: e.dma_start(out=ctx.idb.ap(), in_=dram["c_idb"]))
    P.s(lambda e: e.dma_start(out=ctx.idf.ap(), in_=dram["c_idf"]))
    P.s(lambda e: e.dma_start(out=ctx.onesrow1.ap(), in_=dram["c_onesrow1"]))
    P.s(lambda e: e.dma_start(out=ctx.fvals.ap(), in_=dram["c_fvals"]))
    P.s(lambda e: e.dma_start(out=ctx.sel.ap(), in_=dram["sel"]))

    wo_slab = [ctx.gblk.ap()[:, 0:8192], ctx.ublk.ap()[:, 0:8192]]
    wo_ord = {}
    for og in range(2):
        wo_ord[og] = P.gd(lambda e, og=og: e.dma_start(
            out=wo_slab[og % 2].rearrange("p (k n) -> p k n", n=512),
            in_=dram["WoT"][:, og * 512:(og + 1) * 512]
            .rearrange("(k p) n -> p k n", p=128)))
    cc_tok = getattr(P, "cc_tok", None)
    otk_ord = {}
    otk_ord[0] = P.gd(lambda e: otk_load(e, 0, 0), wait=cc_tok)
    wo_tok = {}
    otk_tok = {}
    for og in range(4):
        for tt in range(8):
            bk = ps[tt % 2]
            idx = og * 8 + tt
            for kt in range(16):
                def mmfn(e, kt=kt, bk=bk, og=og, idx=idx):
                    if kt == 0:
                        P.gwait(wo_ord[og])(e)
                        P.gwait(otk_ord[idx])(e)
                    return e.matmul(
                        bk[:, 0:512],
                        ctx.otk.ap()[:, (idx % 2) * 2048 + kt * 128:
                                     (idx % 2) * 2048 + (kt + 1) * 128],
                        wo_slab[og % 2][:, kt * 512:(kt + 1) * 512],
                        start=(kt == 0), stop=(kt == 15), skip_group_check=True)
                P.t(mmfn)
            P.a(lambda e, tt=tt, og=og, bk=bk: e.copy(
                ctx.moe.ap()[:, tt * 2048 + og * 512: tt * 2048 + (og + 1) * 512],
                bk[:, 0:512]))
            otk_tok[idx] = P.token()
            nxt = idx + 1
            if nxt < 32:
                wt = otk_tok.get(nxt - 2) or cc_tok
                otk_ord[nxt] = P.gd(
                    lambda e, nxt=nxt: otk_load(e, nxt % 8, nxt % 2), wait=wt)
        wo_tok[og] = otk_tok[og * 8 + 7]
        if og + 2 < 4:
            wo_ord[og + 2] = P.gd(lambda e, og=og: e.dma_start(
                out=wo_slab[og % 2].rearrange("p (k n) -> p k n", n=512),
                in_=dram["WoT"][:, (og + 2) * 512:(og + 3) * 512]
                .rearrange("(k p) n -> p k n", p=128)), wait=wo_tok[og])

    for tt in range(8):
        mt = ctx.moe.ap()[:, tt * 2048:(tt + 1) * 2048]
        P.s(lambda e, tt=tt: e.dma_start(out=ctx.xbuf.ap(),
                                         in_=dram["x_tok"][tt * 128:(tt + 1) * 128, :]))
        P.v(lambda e, mt=mt: e.tensor_tensor(mt, mt, ctx.xbuf.ap(), AL.add))
        P.brk()
        emit_ln(P, ctx, mt, 2048, ctx.xbuf.ap())
        P.brk()
        P.v(lambda e: e.tensor_copy(ctx.x2b.ap(), ctx.xbuf.ap()))
        P.a(lambda e, mt=mt: e.copy(mt, ctx.xbuf.ap()))
        for kt in range(16):
            P.t(lambda e, kt=kt: e.transpose(psT[:, 0:128],
                                             ctx.x2b.ap()[:, kt * 128:(kt + 1) * 128],
                                             ctx.idb.ap()))
            P.a(lambda e, kt=kt, tt=tt: e.copy(
                ctx.x2T.ap()[:, kt * 1024 + tt * 128: kt * 1024 + (tt + 1) * 128],
                psT[:, 0:128]))

    for tt in range(8):
        bk = ps[2]
        for kt in range(16):
            def rfn(e, kt=kt, tt=tt, bk=bk):
                if kt == 0 and tt == 0:
                    P.gwait(n_rwT)(e)
                return e.matmul(
                    bk[:, 0:8],
                    ctx.x2T.ap()[:, kt * 1024 + tt * 128: kt * 1024 + (tt + 1) * 128],
                    ctx.rwT.ap()[:, kt * 8:(kt + 1) * 8],
                    start=(kt == 0), stop=(kt == 15), skip_group_check=True)
            P.t(rfn)
        P.v(lambda e, bk=bk: e.tensor_reduce(ctx.r1.ap(), bk[:, 0:8], AX.X, AL.max))
        P.a(lambda e: e.mul(ctx.r2.ap(), ctx.r1.ap(), -1.0))
        P.brk2()
        P.a(lambda e, bk=bk: e.activation(ctx.r5.ap(), bk[:, 0:8], AF.Exp,
                                          bias=ctx.r2.ap()))
        P.brk()
        P.v(lambda e: e.tensor_reduce(ctx.r1.ap(), ctx.r5.ap(), AX.X, AL.add))
        P.brk()
        P.v(lambda e: e.reciprocal(ctx.r2.ap(), ctx.r1.ap()))
        P.brk()
        P.v(lambda e: e.tensor_scalar(ctx.r5.ap(), ctx.r5.ap(), ctx.r2.ap(), None,
                                      AL.mult))
        P.brk()
        P.v(lambda e: e.tensor_reduce(ctx.r1.ap(), ctx.r5.ap(), AX.X, AL.max))
        P.brk()
        P.v(lambda e: e.tensor_scalar(ctx.r6.ap(), ctx.r5.ap(), ctx.r1.ap(), None,
                                      AL.is_ge))
        P.brk()
        P.v(lambda e: e.scalar_tensor_tensor(ctx.r6.ap(), ctx.r6.ap(), -2.0,
                                             ctx.r5.ap(), AL.mult, AL.add))
        P.brk()
        P.v(lambda e: e.tensor_reduce(ctx.r2.ap(), ctx.r6.ap(), AX.X, AL.max))
        P.brk()
        P.v(lambda e: e.tensor_tensor(ctx.r3.ap(), ctx.r1.ap(), ctx.r2.ap(), AL.add))
        P.brk()
        P.v(lambda e: e.reciprocal(ctx.r4.ap(), ctx.r3.ap()))
        P.v(lambda e: e.tensor_scalar(ctx.r6.ap(), ctx.r5.ap(), ctx.r2.ap(), None,
                                      AL.is_ge))
        P.brk()
        P.v(lambda e, tt=tt: e.scalar_tensor_tensor(
            ctx.comb.ap()[:, tt * 8:(tt + 1) * 8], ctx.r5.ap(), 1.0, ctx.r6.ap(),
            AL.mult, AL.mult))
        P.brk()
        P.v(lambda e, tt=tt: e.tensor_scalar(ctx.comb.ap()[:, tt * 8:(tt + 1) * 8],
                                             ctx.comb.ap()[:, tt * 8:(tt + 1) * 8],
                                             ctx.r4.ap(), None, AL.mult))
        P.brk()
        P.a(lambda e, tt=tt: e.copy(ctx.combb.ap()[:, tt * 8:(tt + 1) * 8],
                                    ctx.comb.ap()[:, tt * 8:(tt + 1) * 8]))
        P.t(lambda e, tt=tt: e.transpose(psT[0:8, 0:128],
                                         ctx.combb.ap()[:, tt * 8:(tt + 1) * 8],
                                         ctx.idb.ap()))
        P.a(lambda e, tt=tt: e.copy(ctx.combT.ap()[:, tt * 128:(tt + 1) * 128],
                                    psT[0:8, 0:128]))

    ex_blocks = [(0, 2), (2, 2), (4, 2), (6, 2), (8, 2), (10, 1)]
    sh_blocks = [(0, 2), (2, 2), (4, 2), (6, 2), (8, 2), (10, 1),
                 (11, 2), (13, 2), (15, 2), (17, 2), (19, 2), (21, 1)]
    items = []
    for ex in range(E):
        for (it0, itn) in ex_blocks:
            items.append(("gu", ("gate_w", ex), ("up_w", ex), it0, itn))
        for og in range(4):
            items.append(("d", ("down_w", ex), 0, og))
    for bi, (it0, itn) in enumerate(sh_blocks):
        items.append(("gu", ("s_gate_w", None), ("s_up_w", None), it0, itn))
        if (it0 + itn) in (11, 22):
            halfE = 0 if (it0 + itn) == 11 else 1
            for og in range(4):
                items.append(("d", ("s_down_w", None), halfE * 11 * 128, og))

    def wsrc(spec):
        name, ex = spec
        return dram[name][ex] if ex is not None else dram[name]

    def load_item(e, it):
        kind = it[0]
        if kind == "gu2":
            pass
        if kind == "gu":
            _, gsrc, usrc, it0, itn = it
            return None
        return None

    ords = {}
    toks = {}

    def emit_load(i, wait):
        it = items[i]
        if it[0] == "gu":
            _, gsrc, usrc, it0, itn = it
            par = i % 2
            o1 = P.gd(lambda e, gsrc=gsrc, it0=it0, itn=itn, par=par: e.dma_start(
                out=ctx.gblk.ap()[:, par * 4096: par * 4096 + 16 * itn * 128]
                .rearrange("p (k n) -> p k n", n=itn * 128),
                in_=wsrc(gsrc)[:, it0 * 128:(it0 + itn) * 128]
                .rearrange("(k p) n -> p k n", p=128)), wait=wait)
            o2 = P.gd(lambda e, usrc=usrc, it0=it0, itn=itn, par=par: e.dma_start(
                out=ctx.ublk.ap()[:, par * 4096: par * 4096 + 16 * itn * 128]
                .rearrange("p (k n) -> p k n", n=itn * 128),
                in_=wsrc(usrc)[:, it0 * 128:(it0 + itn) * 128]
                .rearrange("(k p) n -> p k n", p=128)), wait=None)
            ords[i] = o2
        else:
            _, dsrc, rows0, og = it
            par = i % 2
            ords[i] = P.gd(lambda e, dsrc=dsrc, rows0=rows0, og=og, par=par:
                           e.dma_start(
                out=ctx.dblk.ap()[:, par * 5632: par * 5632 + 11 * 512]
                .rearrange("p (k n) -> p k n", n=512),
                in_=wsrc(dsrc)[rows0: rows0 + 11 * 128, og * 512:(og + 1) * 512]
                .rearrange("(k p) n -> p k n", p=128)), wait=wait)

    emit_load(0, wo_tok[2])
    emit_load(1, wo_tok[3])
    crep_done = -1
    for i, it in enumerate(items):
        par = i % 2
        if it[0] == "gu":
            _, gsrc, usrc, it0, itn = it
            shared = gsrc[1] is None
            ex = gsrc[1]
            if not shared and ex != crep_done and it0 == 0:
                for half in range(2):
                    P.t(lambda e, ex=ex, half=half: e.matmul(
                        ps[6][:, 0:512], ctx.sel.ap()[:, ex * 128:(ex + 1) * 128],
                        ctx.combT.ap()[:, half * 512:(half + 1) * 512],
                        start=True, stop=True, skip_group_check=True))
                    P.a(lambda e, half=half: e.copy(
                        ctx.crep.ap()[:, half * 512:(half + 1) * 512],
                        ps[6][:, 0:512]))
                crep_done = ex
            for itl in range(itn):
                itg = it0 + itl
                ith = itg % 11
                for tg2 in range(2):
                    bkg, bku = ps[(itl + tg2) % 2], ps[2 + (itl + tg2) % 2]
                    for kt in range(16):
                        def gfn(e, kt=kt, itl=itl, itn=itn, tg2=tg2, bkg=bkg,
                                par=par, i=i):
                            if kt == 0:
                                P.gwait(ords[i])(e)
                            return e.matmul(
                                bkg[:, 0:512],
                                ctx.gblk.ap()[:, par * 4096 + kt * itn * 128 + itl * 128:
                                              par * 4096 + kt * itn * 128 + (itl + 1) * 128],
                                ctx.x2T.ap()[:, kt * 1024 + tg2 * 512:
                                             kt * 1024 + (tg2 + 1) * 512],
                                start=(kt == 0), stop=(kt == 15),
                                skip_group_check=True)
                        P.t(gfn)
                        P.t(lambda e, kt=kt, itl=itl, itn=itn, tg2=tg2, bku=bku,
                            par=par: e.matmul(
                            bku[:, 0:512],
                            ctx.ublk.ap()[:, par * 4096 + kt * itn * 128 + itl * 128:
                                          par * 4096 + kt * itn * 128 + (itl + 1) * 128],
                            ctx.x2T.ap()[:, kt * 1024 + tg2 * 512:
                                         kt * 1024 + (tg2 + 1) * 512],
                            start=(kt == 0), stop=(kt == 15), skip_group_check=True))
                    if shared:
                        P.a(lambda e, bkg=bkg, itg=itg: e.activation(
                            ctx.sg_s.ap(), bkg[:, 0:512], AF.Silu,
                            bias=ctx.sgb.ap()[:, itg:itg + 1]))
                        P.v(lambda e, bku=bku, itg=itg: e.tensor_scalar_add(
                            ctx.uc_s.ap(), bku[:, 0:512], ctx.sub.ap()[:, itg:itg + 1]))
                    else:
                        P.a(lambda e, bkg=bkg: e.activation(ctx.sg_s.ap(),
                                                            bkg[:, 0:512], AF.Silu))
                        P.v(lambda e, bku=bku, tg2=tg2: e.tensor_tensor(
                            ctx.uc_s.ap(), bku[:, 0:512],
                            ctx.crep.ap()[:, tg2 * 512:(tg2 + 1) * 512], AL.mult))
                    P.brk()
                    P.v(lambda e, ith=ith, tg2=tg2: e.tensor_tensor(
                        ctx.he.ap()[:, ith * 1024 + tg2 * 512:
                                    ith * 1024 + (tg2 + 1) * 512],
                        ctx.sg_s.ap(), ctx.uc_s.ap(), AL.mult))
            toks[i] = P.token()
        else:
            _, dsrc, rows0, og = it
            for tt in range(8):
                bk = ps[4 + tt % 2]
                for kt in range(11):
                    def dfn(e, kt=kt, tt=tt, bk=bk, par=par, i=i):
                        if kt == 0 and tt == 0:
                            P.gwait(ords[i])(e)
                        return e.matmul(
                            bk[:, 0:512],
                            ctx.he.ap()[:, kt * 1024 + tt * 128:
                                        kt * 1024 + tt * 128 + 128],
                            ctx.dblk.ap()[:, par * 5632 + kt * 512:
                                          par * 5632 + (kt + 1) * 512],
                            start=(kt == 0), stop=(kt == 10),
                            skip_group_check=True)
                    P.t(dfn)
                P.v(lambda e, tt=tt, og=og, bk=bk: e.tensor_tensor(
                    ctx.moe.ap()[:, tt * 2048 + og * 512: tt * 2048 + (og + 1) * 512],
                    ctx.moe.ap()[:, tt * 2048 + og * 512: tt * 2048 + (og + 1) * 512],
                    bk[:, 0:512], AL.add))
            toks[i] = P.token()
        if i + 2 < len(items):
            emit_load(i + 2, toks.get(i))

    P.s(lambda e: e.dma_start(out=ctx.xbuf.ap()[0:1, :], in_=dram["sdb"]))
    for og in range(4):
        P.t(lambda e, og=og: e.matmul(ps[6][:, 0:512], ctx.onesrow1.ap(),
                                      ctx.xbuf.ap()[0:1, og * 512:(og + 1) * 512],
                                      start=True, stop=True, skip_group_check=True))
        for tt in range(8):
            P.v(lambda e, tt=tt, og=og: e.tensor_tensor(
                ctx.moe.ap()[:, tt * 2048 + og * 512: tt * 2048 + (og + 1) * 512],
                ctx.moe.ap()[:, tt * 2048 + og * 512: tt * 2048 + (og + 1) * 512],
                ps[6][:, 0:512], AL.add))
    for tt in range(8):
        mt = ctx.moe.ap()[:, tt * 2048:(tt + 1) * 2048]
        emit_ln(P, ctx, mt, 2048, ctx.xbuf.ap())
        P.s(lambda e, tt=tt: e.dma_start(out=P.out_q[tt * 128:(tt + 1) * 128, :],
                                         in_=ctx.xbuf.ap()))
        P.brk()


def build_launchB():
    nc = bass.Bass("TRN2", target_bir_lowering=False)
    dram = {}
    for name in ["c_idb", "c_idf", "c_onesrow1", "c_fvals"]:
        arr = CONST_SPECS[name]
        dt = BF16 if arr.dtype == BF else F32
        dram[name] = nc.dram_tensor(name, list(arr.shape), dt, kind="ExternalInput").ap()
    spec = [("oTq", [D, TQ], BF16), ("x_tok", [TQ, D], F32), ("WoT", [D, D], F32),
            ("rwT", [D, 8], F32), ("sel", [8, 1024], BF16),
            ("s_gate_w", [D, SI], F32), ("s_up_w", [D, SI], F32),
            ("s_down_w", [SI, D], F32), ("sgb", [128, 22], F32),
            ("sub", [128, 22], F32), ("sdb", [1, 2048], F32)]
    for name, shape, dt in spec:
        dram[name] = nc.dram_tensor("d_" + name, shape, dt, kind="ExternalInput").ap()
    dram["gate_w"] = nc.dram_tensor("d_gate_w", [E, D, I_FF], F32, kind="ExternalInput").ap()
    dram["up_w"] = nc.dram_tensor("d_up_w", [E, D, I_FF], F32, kind="ExternalInput").ap()
    dram["down_w"] = nc.dram_tensor("d_down_w", [E, I_FF, D], F32, kind="ExternalInput").ap()
    out_q = nc.dram_tensor("out_q", [TQ, D], F32, kind="ExternalOutput").ap()

    ctx = BCtx(nc)
    with nc.Block() as block:
        P = Prog(nc, nbar=8)
        P._junk = ctx.junk.ap()
        P.out_q = out_q
        emit_launchB(P, ctx, dram)
        P.v(lambda e: e.engine_nop())
        P.lower(block)
    return nc


def build_fused():
    nc = bass.Bass("TRN2", target_bir_lowering=False)
    dram = {}
    for name, arr in CONST_SPECS.items():
        dt = BF16 if arr.dtype == BF else F32
        dram[name] = nc.dram_tensor(name, list(arr.shape), dt, kind="ExternalInput").ap()
    insA = [("xT", [D, T], F32), ("WqT", [D, 512], F32), ("WkT", [D, 512], F32),
            ("WvT", [D, 512], F32), ("W1T", [D, 32], F32), ("W2T", [32, 512], F32),
            ("bgT", [D, 64], F32), ("bgb", [64, 1], F32), ("convs", [128, 16], F32),
            ("qgrow", [1, 512], F32), ("kgrow", [1, 512], F32)]
    insB = [("x_tok", [TQ, D], F32), ("WoT", [D, D], F32),
            ("rwT", [D, 8], F32), ("sel", [8, 1024], BF16),
            ("s_gate_w", [D, SI], F32), ("s_up_w", [D, SI], F32),
            ("s_down_w", [SI, D], F32), ("sgb", [128, 22], F32),
            ("sub", [128, 22], F32), ("sdb", [1, 2048], F32),
            ("gate_w", [E, D, I_FF], F32), ("up_w", [E, D, I_FF], F32),
            ("down_w", [E, I_FF, D], F32)]
    for name, shape, dt in insA + insB:
        dram[name] = nc.dram_tensor("d_" + name, shape, dt, kind="ExternalInput").ap()
    out_q = nc.dram_tensor("out_q", [TQ, D], F32, kind="ExternalOutput").ap()
    negG3_hbm = nc.dram_tensor("negG3_scr", [H, 3, T], BF16, kind="Internal").ap()
    cc_in = nc.dram_tensor("cc_in", [512, T], BF16, kind="Internal").ap()
    cc_out = nc.dram_tensor("cc_out", [D, T], BF16, kind="Internal").ap()

    ctx = ACtx(nc, guarded=True)
    with nc.Block() as block:
        P = Prog(nc, nbar=8)
        P._junk = ctx.junk.ap()
        P.negG3_hbm = negG3_hbm
        P.out_q = out_q
        emit_load_consts(P, ctx, dram)
        emit_prep(P, ctx, dram)
        for hh in range(H):
            emit_head(P, ctx, hh, dram)
            if hh % 2 == 1:
                P.s(lambda e, hh=hh: e.dma_start(
                    out=cc_in[(hh // 2) * 128:(hh // 2) * 128 + 128, :],
                    in_=ctx.oT.ap()))
        release_guards(ctx)
        ctxB = BCtx(nc, actx=ctx)
        P.cc(lambda e: e.collective_compute(
            "AllGather", AL.bypass, [[0, 1, 2, 3], [4, 5, 6, 7]],
            [cc_in], [cc_out]))
        P.cc_tok = P.token()
        emit_launchB(P, ctxB, dict(dram, oTq=cc_out), fused=True)
        P.v(lambda e: e.engine_nop())
        P.lower(block)
    return nc


def kernel(**inputs):
    consts = host_consts()
    x = np.asarray(inputs["L0_attn_hidden_states"], dtype=np.float32)

    nc = build_fused()
    selc = np.zeros((8, 1024), np.float32)
    for ex in range(8):
        selc[ex, ex * 128:(ex + 1) * 128] = 1.0
    gate_w = np.ascontiguousarray(np.asarray(inputs["L0_ffn_gate_w"], np.float32))
    up_w = np.ascontiguousarray(np.asarray(inputs["L0_ffn_up_w"], np.float32))
    down_w = np.ascontiguousarray(np.asarray(inputs["L0_ffn_down_w"], np.float32))
    s_gate_w = np.ascontiguousarray(np.asarray(inputs["L0_ffn_s_gate_w"], np.float32))
    s_up_w = np.ascontiguousarray(np.asarray(inputs["L0_ffn_s_up_w"], np.float32))
    s_down_w = np.ascontiguousarray(np.asarray(inputs["L0_ffn_s_down_w"], np.float32))
    WoT = np.ascontiguousarray(np.asarray(inputs["L0_attn_Wo"]).T.astype(np.float32))
    rwT = np.ascontiguousarray(np.asarray(inputs["L0_ffn_router_w"]).T.astype(np.float32))
    W1T = np.ascontiguousarray(np.asarray(inputs["L0_attn_W1"]).T.astype(np.float32))
    sgb_l = np.ascontiguousarray(
        np.asarray(inputs["L0_ffn_s_gate_b"], np.float32).reshape(22, 128).T)
    sub_l = np.ascontiguousarray(
        np.asarray(inputs["L0_ffn_s_up_b"], np.float32).reshape(22, 128).T)
    sdb_l = np.asarray(inputs["L0_ffn_s_down_b"], np.float32)[None, :]
    xT = [np.ascontiguousarray(x[b].T) for b in range(B)]

    in_maps = []
    for c in range(N_CORES):
        b = c // 4
        hg = c % 4
        tq = c % 4
        hs = slice(hg * 8 * HD, (hg + 1) * 8 * HD)
        hh = slice(hg * 8, (hg + 1) * 8)
        tsl = slice(tq * TQ, (tq + 1) * TQ)
        m = dict(consts)
        m["d_xT"] = xT[b]
        m["d_WqT"] = np.ascontiguousarray(np.asarray(inputs["L0_attn_Wq"])[hs].T.astype(np.float32))
        m["d_WkT"] = np.ascontiguousarray(np.asarray(inputs["L0_attn_Wk"])[hs].T.astype(np.float32))
        m["d_WvT"] = np.ascontiguousarray(np.asarray(inputs["L0_attn_Wv"])[hs].T.astype(np.float32))
        m["d_W1T"] = W1T
        m["d_W2T"] = np.ascontiguousarray(np.asarray(inputs["L0_attn_W2"]).T[:, hs].astype(np.float32))
        bg = np.zeros((D, 64), np.float32)
        bg[:, 0:8] = np.asarray(inputs["L0_attn_bt_w"])[hh].T
        bg[:, 32:40] = np.asarray(inputs["L0_attn_g_w"])[hh].T
        m["d_bgT"] = bg
        bgb = np.zeros((64, 1), np.float32)
        bgb[0:8, 0] = np.asarray(inputs["L0_attn_bt_b"])[hh]
        bgb[32:40, 0] = np.asarray(inputs["L0_attn_g_b"])[hh]
        m["d_bgb"] = bgb
        cw = np.asarray(inputs["L0_attn_conv_w"])[hs]
        m["d_convs"] = np.ascontiguousarray(
            cw.reshape(4, 128, 4).transpose(1, 0, 2).reshape(128, 16)).astype(np.float32)
        m["d_qgrow"] = np.tile(np.asarray(inputs["L0_attn_q_norm_w"], np.float32), 8)[None, :]
        m["d_kgrow"] = np.tile(np.asarray(inputs["L0_attn_k_norm_w"], np.float32), 8)[None, :]
        m["d_x_tok"] = np.ascontiguousarray(x[b][tsl])
        m["d_WoT"] = WoT
        m["d_rwT"] = rwT
        m["d_sel"] = selc.astype(BF)
        m["d_gate_w"] = gate_w
        m["d_up_w"] = up_w
        m["d_down_w"] = down_w
        m["d_s_gate_w"] = s_gate_w
        m["d_s_up_w"] = s_up_w
        m["d_s_down_w"] = s_down_w
        m["d_sgb"] = sgb_l
        m["d_sub"] = sub_l
        m["d_sdb"] = sdb_l
        in_maps.append(m)

    res = run_bass_kernel_spmd(nc, in_maps, core_ids=list(range(N_CORES)))
    out = np.zeros((B, T, D), dtype=np.float32)
    for c in range(N_CORES):
        b = c // 4
        tq = c % 4
        out[b, tq * TQ:(tq + 1) * TQ] = res.results[c]["out_q"]
    return out


# version 07
# speedup vs baseline: 1.1368x; 1.0184x over previous
import numpy as np
import ml_dtypes
import concourse.bass as bass
import concourse.mybir as mybir
from concourse.bass_utils import run_bass_kernel_spmd

F32 = mybir.dt.float32
BF16 = mybir.dt.bfloat16
AF = mybir.ActivationFunctionType
AL = mybir.AluOpType
AX = mybir.AxisListType
BF = ml_dtypes.bfloat16

BIGNEG = -8.0e37
DMA_ENGS = ("sync", "gpsimd")

B, T, D, H_ALL, HD = 2, 4096, 2048, 32, 64
H = 8
NT = T // 128
CHUNK = 128
NCH = T // CHUNK
WIN = 4
SCALE = HD ** (-0.5)
E, I_FF, SI = 8, 1408, 2816
TQ = 1024
N_CORES = 8


class Prog:
    ENGS = ("tensor", "vector", "scalar", "gpsimd", "sync")

    def __init__(self, nc, nbar=8):
        self.nc = nc
        self.steps = []
        self.nbar = nbar
        self._junk = None
        self.regs = {}
        self._tok_requests = []
        self._gd_count = 0
        self.gsem = None
        self.chain = 0
        self.nchains = 2
        self._sems = None
        self._ccsem = None

    class Tok:
        __slots__ = ("chain", "slot", "val")
        def __init__(self):
            self.chain = None
            self.slot = None
            self.val = None

    def token(self):
        t = Prog.Tok()
        self._tok_requests.append((len(self.steps) - 1, t))
        return t

    def join(self, chain, tok):
        def fn(e, tok=tok):
            e.wait_ge(self._semof(tok), tok.val)
            return e.memzero(self._junk)
        self.steps.append(("scalar", fn, "n", chain))

    def _semof(self, tok):
        if tok.chain == "cc":
            return self._ccsem
        return self._sems[tok.chain][tok.slot]

    def gd(self, fn, wait=None):
        n = self._gd_count
        self._gd_count += 1
        self.steps.append(("gpsimd", fn, ("gd", wait), 0))
        return n

    def gwait(self, n):
        def w(e):
            e.wait_ge(self.gsem, 16 * (n + 1))
        return w

    def brk(self):
        self.steps.append(("scalar", lambda e: e.memzero(self._junk), "n", self.chain))

    def brk2(self):
        self.steps.append(("vector", lambda e: e.memzero(self._junk), "n", self.chain))

    def t(self, fn): self.steps.append(("tensor", fn, "n", self.chain))
    def v(self, fn): self.steps.append(("vector", fn, "n", self.chain))
    def a(self, fn): self.steps.append(("scalar", fn, "n", self.chain))
    def g(self, fn): self.steps.append(("gpsimd", fn, "n", self.chain))
    def s(self, fn): self.steps.append(("sync", fn, "n", self.chain))
    def cc(self, fn): self.steps.append(("gpsimd", fn, "cc", 0))

    def lower(self, block):
        nc = self.nc
        NB = self.nbar
        sems = [[nc.alloc_semaphore(f"bar{c}_{i}") for i in range(NB)]
                for c in range(self.nchains)]
        cc_sem = nc.alloc_semaphore("cc_sem")
        self.gsem = nc.alloc_semaphore("gsem")
        self._sems = sems
        self._ccsem = cc_sem
        runs = []
        step_run = {}
        for si, (eng, fn, kind, ch) in enumerate(self.steps):
            if isinstance(kind, tuple) and kind[0] == "gd":
                runs.append([eng, [fn], "gd", kind[1], 0])
            elif (kind == "n" and runs and runs[-1][0] == eng
                  and runs[-1][2] == "n" and runs[-1][4] == ch):
                runs[-1][1].append(fn)
            else:
                runs.append([eng, [fn], kind, None, ch])
            step_run[si] = len(runs) - 1
        nruns = len(runs)
        sem_tot = [[0] * NB for _ in range(self.nchains)]
        run_wait = [None] * nruns
        cum_after = [None] * nruns
        prev_chain_run = [None] * self.nchains
        chain_idx = [0] * self.nchains
        for i, (eng, fns, kind, extra, ch) in enumerate(runs):
            if kind == "gd":
                continue
            if prev_chain_run[ch] is not None:
                run_wait[i] = cum_after[prev_chain_run[ch]]
            if kind == "cc":
                cum_after[i] = ("cc", 0, 1)
            else:
                s = chain_idx[ch] % NB
                amt = 16 * len(fns) if eng in DMA_ENGS else 1
                sem_tot[ch][s] += amt
                cum_after[i] = (ch, s, sem_tot[ch][s])
            prev_chain_run[ch] = i
            chain_idx[ch] += 1
        for si, tok in self._tok_requests:
            ri = step_run[si]
            assert cum_after[ri] is not None, "token on a non-chained step"
            tok.chain, tok.slot, tok.val = cum_after[ri]
        per_eng = {e: [] for e in self.ENGS}
        for i, r in enumerate(runs):
            per_eng[r[0]].append((i, r[0], r[1], r[2], r[3], r[4]))

        def semof(marker):
            ch, s, v = marker
            return (cc_sem if ch == "cc" else sems[ch][s]), v

        def make_body(items):
            def body(eng):
                for (i, ename, fns, kind, extra, ch) in items:
                    if kind == "gd":
                        if extra is not None:
                            sm, v = semof((extra.chain, extra.slot, extra.val))
                            eng.wait_ge(sm, v)
                        fns[0](eng).then_inc(self.gsem, 16)
                        continue
                    if run_wait[i] is not None:
                        sm, v = semof(run_wait[i])
                        eng.wait_ge(sm, v)
                    if kind == "cc":
                        fns[0](eng).then_inc(cc_sem)
                        continue
                    _c, s, _v = cum_after[i]
                    if ename in DMA_ENGS:
                        for fn in fns:
                            fn(eng).then_inc(sems[ch][s], 16)
                    else:
                        last = None
                        for fn in fns:
                            rr = fn(eng)
                            if rr is not None:
                                last = rr
                        last.then_inc(sems[ch][s], 1)
            return body
        for e in self.ENGS:
            if per_eng[e]:
                getattr(block, e)(make_body(per_eng[e]))


def cdiv(a, b):
    return (a + b - 1) // b


def host_consts():
    r = np.arange(128)
    c = {}
    c["c_idb"] = np.eye(128, dtype=np.float32).astype(BF)
    c["c_idf"] = np.eye(128, dtype=np.float32)
    c["c_mask_slneg"] = np.where(r[:, None] > r[None, :], -1.0, 0.0).astype(np.float32)
    c["c_mask_su"] = np.where(r[:, None] < r[None, :], 1.0, 0.0).astype(np.float32)
    c["c_mask_le"] = np.where(r[:, None] <= r[None, :], 1.0, 0.0).astype(np.float32)
    c["c_maskA"] = np.where(r[:, None] > r[None, :], BIGNEG, 0.0).astype(np.float32).astype(BF)
    c["c_maskB"] = np.where(r[:, None] < r[None, :], BIGNEG, 0.0).astype(np.float32).astype(BF)
    c["c_ones3"] = np.ones((3, 128), dtype=np.float32).astype(BF)
    c["c_ones1_64"] = np.ones((1, 64), dtype=np.float32)
    c["c_ones3T"] = np.ones((3, 4096), dtype=np.float32).astype(BF)
    c["c_onesblk"] = np.stack([(r < 64).astype(np.float32),
                               (r >= 64).astype(np.float32)], axis=1).astype(BF)
    c["c_onesblkT"] = np.ascontiguousarray(
        np.stack([(r < 64), (r >= 64)], axis=0).astype(np.float32))
    c["c_trilf"] = np.where(r[:, None] <= r[None, :], 1.0, 0.0).astype(np.float32)
    c["c_onescol"] = np.ones((128, 1), dtype=np.float32)
    c["c_fvals"] = np.tile(np.array([[1e-6, 1.0, 1e-5, 1e-12, 0, 0, 0, 0]],
                                    dtype=np.float32), (128, 1))
    c["c_onesrow1"] = np.ones((1, 128), dtype=np.float32)
    return c


CONST_SPECS = {k: v for k, v in host_consts().items()}


class ChainCtx:
    def __init__(self, A, nc, cid, psT):
        self.cid = cid
        sfx = f"_c{cid}"
        self.Kd = A("Kd70" + sfx, [70, T], BF16)
        self.q70 = A("q70" + sfx, [70, 256], BF16)
        self.S = A("S_sb" + sfx, [64, 128], BF16)
        self.QbT = A("QbT" + sfx, [64, 384], BF16)
        self.QbTs = A("QbTs" + sfx, [64, 128], BF16)
        self.negbWb = A("negbWb" + sfx, [128, 64], BF16)
        self.Xp = A("Xp" + sfx, [128, 256], F32)
        self.Mp = A("Mp" + sfx, [128, 256], F32)
        self.QWt_m = A("QWt_m" + sfx, [128, 128], F32)
        self.RT = A("RT" + sfx, [128, 128], BF16)
        self.RbT = A("RbT" + sfx, [128, 128], BF16)
        self.RbTs = A("RbTs" + sfx, [128, 128], BF16)
        self.QtT = A("QtT" + sfx, [64, 128], BF16)
        self.negsWK = A("negsWK" + sfx, [128, 128], BF16)
        self.bsWK = A("bsWK" + sfx, [128, 128], BF16)
        self.X_u = A("X_u" + sfx, [128, 128], F32)
        self.RwT = A("RwT" + sfx, [128, 128], BF16)
        self.V_sb = A("V_sb" + sfx, [128, 64], BF16)
        self.McT = A("McT" + sfx, [64, 128], BF16)
        self.D_sb = A("D_sb" + sfx, [128, 128], BF16)
        self.negG3r = A("negG3r" + sfx, [3, 128], BF16)
        self.mpart = A("mpart" + sfx, [128, 12], F32)
        self.mcol = A("mcol" + sfx, [128, 1], F32)
        self.negm_sb = A("negm_sb" + sfx, [128, 1], F32)
        self.negmrow4 = A("negmrow4" + sfx, [1, 512], F32)
        self.zinv = A("zinv" + sfx, [1, 128], F32)
        self.zrep = A("zrep" + sfx, [64, 128], F32)
        self.ot_raw = A("ot_raw" + sfx, [64, 128], F32)
        self.P_buf = A("P_buf" + sfx, [128, 1024], BF16)
        self.seq = None
        self.big = None
        self.pvw = None
        self.tp = psT


class ACtx:
    def __init__(self, nc, guarded=False):
        self._guards = []
        self._guards_attn = []
        self.nc = nc
        if guarded:
            def A(name, shape, dt):
                g = nc.sbuf_tensor(name, shape, dt)
                h = g.__enter__()
                self._guards.append(g)
                return h
        else:
            A = nc.alloc_sbuf_tensor
        self._A_guarded = guarded
        AP_ = nc.alloc_psum_tensor
        AC = nc.alloc_sbuf_tensor
        self.idb = AC("idb", [128, 128], BF16)
        self.idf = AC("idf", [128, 128], F32)
        self.mask_slneg = AC("mask_slneg", [128, 128], F32)
        self.mask_su = AC("mask_su", [128, 128], F32)
        self.maskA = AC("maskA", [128, 128], BF16)
        self.maskB = AC("maskB", [128, 128], BF16)
        self.ones3 = AC("ones3", [3, 128], BF16)
        self.ones1_64 = AC("ones1_64", [1, 64], F32)
        self.onesblk = AC("onesblk", [128, 2], BF16)
        self.onesblkT = AC("onesblkT", [2, 128], F32)
        self.trilf = AC("trilf", [128, 128], F32)
        self.onescol = AC("onescol", [128, 1], F32)
        self.onesrow1 = AC("onesrow1", [1, 128], F32)
        self.junk = AC("junk", [1, 8], F32)
        self.fvals = AC("fvals", [128, 8], F32)
        nc.const_aps.register(F32, 1e-6, self.fvals.ap()[:, 0:1])
        nc.const_aps.register(F32, 1.0, self.fvals.ap()[:, 1:2])
        nc.const_aps.register(F32, 1e-5, self.fvals.ap()[:, 2:3])
        nc.const_aps.register(F32, 1e-12, self.fvals.ap()[:, 3:4])
        self._guards_io = []
        if guarded:
            def AIO(name, shape, dt):
                g = nc.sbuf_tensor(name, shape, dt)
                h = g.__enter__()
                self._guards_io.append(g)
                return h
        else:
            AIO = nc.alloc_sbuf_tensor
        self.q_tok = AIO("q_tok", [128, NT * 64 * H], BF16)
        self.k_tok = AIO("k_tok", [128, NT * 64 * H], BF16)
        self.w_tok = AIO("w_tok", [128, NT * 64 * H], BF16)
        self.v65 = AIO("v65", [128, NT * 65 * H], BF16)
        self.bcol = AIO("bcol", [128, NT * H], F32)
        self.negG3_tm = AIO("negG3_tm", [128, NT * 3 * H], BF16)
        self.xtile = A("xtile", [128, 512], BF16)
        self.wblk = A("wblk", [128, 512], BF16)
        self.w1t = A("w1t", [128, 16 * 32], BF16)
        self.w2t = A("w2t", [32, 512], BF16)
        self.bgTw = A("bgTw", [128, 16 * 64], BF16)
        self.bgb = A("bgb", [64, 1], F32)
        self.qgrow = A("qgrow", [1, 512], F32)
        self.kgrow = A("kgrow", [1, 512], F32)
        self.grep = A("grep", [128, 512], F32)
        self.convs = A("convs", [128, 16], F32)
        self.sqbuf = A("sqbuf", [128, 512], F32)
        self.rstd8 = A("rstd8", [128, 8], F32)
        self.var8 = A("var8", [128, 8], F32)
        self.w0T = A("w0T", [32, T], BF16)
        self.wpre = A("wpre", [128, 2052], BF16)
        self.wconv = A("wconv", [128, 2048], F32)
        self.nrm2 = A("nrm2", [2, 512], F32)
        self.bgB = A("bgB", [40, 512], F32)
        self.bgS = A("bgS", [40, 512], F32)
        self.bgS2 = A("bgS2", [40, 512], F32)
        self.gtok = A("gtok", [128, 32], F32)
        self.Gtok = A("Gtok", [128, 32], F32)
        self.gcarry = A("gcarry", [1, 8], F32)
        self.msplit = A("msplit", [128, 32], F32)
        self.spl = A("spl", [128, 3 * 32], BF16)
        self.splf = A("splf", [128, 32], F32)
        self.stage24 = A("stage24", [24, 128], BF16)
        self.psA = AP_("psA", [128, 512], F32)
        self.psM = AP_("psM", [128, 512], F32)
        self.psQ = AP_("psQ", [128, 512], F32)
        self.big1 = AP_("big1", [128, 512], F32)
        self.big2 = AP_("big2", [128, 512], F32)
        self.psPV = AP_("psPV", [128, 512], F32)
        self.psT = AP_("psT", [128, 2048], BF16)

    def alloc_attn(self):
        nc = self.nc
        if self._A_guarded:
            def A(name, shape, dt):
                g = nc.sbuf_tensor(name, shape, dt)
                h = g.__enter__()
                self._guards_attn.append(g)
                return h
        else:
            A = nc.alloc_sbuf_tensor
        self.ch = [ChainCtx(A, nc, 0, None), ChainCtx(A, nc, 1, None)]
        self.oT = A("oT", [128, T], BF16)
        self.ch[0].seq = self.psA.ap()
        self.ch[0].big = self.psM.ap()
        self.ch[0].pvw = self.psQ.ap()
        self.ch[0].tp = self.psT.ap()[:, 0:1024]
        self.ch[1].seq = self.big1.ap()
        self.ch[1].big = self.big2.ap()
        self.ch[1].pvw = self.psPV.ap()
        self.ch[1].tp = self.psT.ap()[:, 1024:2048]

    def release_attn(self):
        for g in reversed(self._guards_attn):
            g.__exit__(None, None, None)
        self._guards_attn = []
        for g in reversed(self._guards_io):
            g.__exit__(None, None, None)
        self._guards_io = []


def release_guards(ctx):
    for g in reversed(ctx._guards):
        g.__exit__(None, None, None)
    ctx._guards = []


def emit_load_consts(P, ctx, dram):
    pairs = [("c_idb", ctx.idb), ("c_idf", ctx.idf), ("c_mask_slneg", ctx.mask_slneg),
             ("c_mask_su", ctx.mask_su),
             ("c_maskA", ctx.maskA), ("c_maskB", ctx.maskB), ("c_ones3", ctx.ones3),
             ("c_ones1_64", ctx.ones1_64), ("c_onesblk", ctx.onesblk),
             ("c_onesblkT", ctx.onesblkT), ("c_trilf", ctx.trilf),
             ("c_onescol", ctx.onescol), ("c_onesrow1", ctx.onesrow1),
             ("c_fvals", ctx.fvals)]
    for name, dst in pairs:
        if name in dram:
            P.s(lambda e, name=name, dst=dst: e.dma_start(out=dst.ap(), in_=dram[name]))


def emit_rms_apply(P, ctx, ps, grep, dst_tok, tt, eps):
    P.a(lambda e, ps=ps: e.activation(ctx.sqbuf.ap(), ps, AF.Square))
    P.v(lambda e: e.tensor_reduce(ctx.var8.ap(),
                                  ctx.sqbuf.ap().rearrange("p (h d) -> p h d", d=64),
                                  AX.X, AL.add))
    P.a(lambda e, eps=eps: e.activation(ctx.rstd8.ap(), ctx.var8.ap(), AF.Ln,
                                        scale=1.0 / 64, bias=eps))
    P.brk2()
    P.a(lambda e: e.activation(ctx.var8.ap(), ctx.rstd8.ap(), AF.Exp, scale=-0.5))
    for seg in range(8):
        P.v(lambda e, ps=ps, seg=seg, tt=tt: e.scalar_tensor_tensor(
            dst_tok.ap()[:, tt * 512 + seg * 64: tt * 512 + (seg + 1) * 64],
            ps[:, seg * 64:(seg + 1) * 64], ctx.var8.ap()[:, seg:seg + 1],
            grep.ap()[:, seg * 64:(seg + 1) * 64], AL.mult, AL.mult))


def emit_prep(P, ctx, dram):
    nc = P.nc
    psA = ctx.psA.ap()
    banks = [ctx.psA.ap(), ctx.psM.ap(), ctx.psQ.ap(), ctx.big2.ap()]
    big1, big2, psPV, psT = ctx.big1.ap(), ctx.big2.ap(), ctx.psPV.ap(), ctx.psT.ap()

    P.g(lambda e: e.dma_start(out=ctx.w1t.ap().rearrange("p (k n) -> p k n", n=32),
                              in_=dram["W1T"].rearrange("(k p) n -> p k n", p=128)))
    P.g(lambda e: e.dma_start(out=ctx.w2t.ap(), in_=dram["W2T"]))
    P.g(lambda e: e.dma_start(out=ctx.bgTw.ap().rearrange("p (k n) -> p k n", n=64),
                              in_=dram["bgT"].rearrange("(k p) n -> p k n", p=128)))
    P.s(lambda e: e.dma_start(out=ctx.bgb.ap(), in_=dram["bgb"]))
    P.s(lambda e: e.dma_start(out=ctx.convs.ap(), in_=dram["convs"]))
    P.s(lambda e: e.dma_start(out=ctx.qgrow.ap(), in_=dram["qgrow"]))
    P.s(lambda e: e.dma_start(out=ctx.kgrow.ap(), in_=dram["kgrow"]))
    P.v(lambda e: e.memset(ctx.v65.ap(), 1.0))
    P.v(lambda e: e.memset(ctx.gcarry.ap(), 0.0))
    P.v(lambda e: e.memset(ctx.wpre.ap()[:, 0:4], 0.0))

    for which, wname, grow, dst in [("q", "WqT", ctx.qgrow, ctx.q_tok),
                                    ("k", "WkT", ctx.kgrow, ctx.k_tok),
                                    ("v", "WvT", None, ctx.v65)]:
        if grow is not None:
            P.t(lambda e, grow=grow: e.matmul(
                psA[:, 0:512], ctx.onesrow1.ap(), grow.ap(),
                start=True, stop=True, skip_group_check=True))
            P.a(lambda e: e.copy(ctx.grep.ap(), psA[:, 0:512]))
        for tg in range(8):
            for kt in range(16):
                P.g(lambda e, tg=tg, kt=kt: e.dma_start(
                    out=ctx.xtile.ap(),
                    in_=dram["xT"][kt * 128:(kt + 1) * 128, tg * 512:(tg + 1) * 512]))
                P.g(lambda e, wname=wname, kt=kt: e.dma_start(
                    out=ctx.wblk.ap(),
                    in_=dram[wname][kt * 128:(kt + 1) * 128, :]))
                for sub in range(4):
                    P.t(lambda e, kt=kt, sub=sub: e.matmul(
                        banks[sub][:, 0:512],
                        ctx.xtile.ap()[:, sub * 128:(sub + 1) * 128],
                        ctx.wblk.ap(), start=(kt == 0), stop=(kt == 15),
                        skip_group_check=True))
                if which == "v":
                    P.t(lambda e, kt=kt: e.matmul(
                        psPV[0:32, 0:512], ctx.w1t.ap()[:, kt * 32:(kt + 1) * 32],
                        ctx.xtile.ap()[:, 0:512],
                        start=(kt == 0), stop=(kt == 15), skip_group_check=True))
                    P.t(lambda e, kt=kt: e.matmul(
                        big1[0:64, 0:512], ctx.bgTw.ap()[:, kt * 64:(kt + 1) * 64],
                        ctx.xtile.ap()[:, 0:512],
                        start=(kt == 0), stop=(kt == 15), skip_group_check=True))
            if which in ("q", "k"):
                for sub in range(4):
                    emit_rms_apply(P, ctx, banks[sub][:, 0:512], ctx.grep, dst,
                                   tg * 4 + sub, 1e-6)
            else:
                for sub in range(4):
                    tt = tg * 4 + sub
                    P.v(lambda e, sub=sub, tt=tt: e.tensor_copy(
                        ctx.v65.ap()[:, tt * 520: tt * 520 + 520]
                        .rearrange("p (h c) -> p h c", c=65)[:, :, 0:64],
                        banks[sub][:, 0:512].rearrange("p (h d) -> p h d", d=64)))
                P.a(lambda e, tg=tg: e.copy(ctx.w0T.ap()[:, tg * 512:(tg + 1) * 512],
                                            psPV[0:32, 0:512]))
                emit_bg(P, ctx, tg)

    emit_wchain(P, ctx)


def emit_bg(P, ctx, tg):
    big1, big2 = ctx.big1.ap(), ctx.big2.ap()
    psT = ctx.psT.ap()
    P.a(lambda e: e.activation(ctx.bgS.ap()[0:8, :], big1[0:8, 0:512], AF.Sigmoid,
                               bias=ctx.bgb.ap()[0:8, :]))
    P.v(lambda e: e.tensor_scalar_mul(ctx.bgB.ap()[0:8, :], ctx.bgS.ap()[0:8, :], 2.0))
    for sub in range(4):
        P.t(lambda e, sub=sub: e.transpose(big2[:, 0:8],
                                           ctx.bgB.ap()[0:8, sub * 128:(sub + 1) * 128],
                                           ctx.idf.ap()[0:8, 0:8]))
        P.a(lambda e, sub=sub, tg=tg: e.copy(
            ctx.bcol.ap()[:, (tg * 4 + sub) * 8:(tg * 4 + sub) * 8 + 8], big2[:, 0:8]))
    P.v(lambda e: e.tensor_scalar_add(ctx.bgB.ap()[32:40, :], big1[32:40, 0:512],
                                      ctx.bgb.ap()[32:40, :]))
    P.a(lambda e: e.activation(ctx.bgS.ap()[32:40, :], ctx.bgB.ap()[32:40, :],
                               AF.Relu, scale=-1.0))
    P.a(lambda e: e.activation(ctx.bgS2.ap()[32:40, :], ctx.bgB.ap()[32:40, :], AF.Abs))
    P.brk2()
    P.a(lambda e: e.activation(ctx.bgB.ap()[32:40, :], ctx.bgS2.ap()[32:40, :],
                               AF.Exp, scale=-1.0))
    P.brk2()
    P.a(lambda e: e.activation(ctx.bgS2.ap()[32:40, :], ctx.bgB.ap()[32:40, :],
                               AF.Ln, bias=1.0))
    P.v(lambda e: e.scalar_tensor_tensor(ctx.bgB.ap()[32:40, :],
                                         ctx.bgS2.ap()[32:40, :], -1.0,
                                         ctx.bgS.ap()[32:40, :], AL.mult, AL.subtract))
    for sub in range(4):
        P.t(lambda e, sub=sub: e.transpose(big2[:, 0:8],
                                           ctx.bgB.ap()[32:40, sub * 128:(sub + 1) * 128],
                                           ctx.idf.ap()[32:40, 32:40]))
        P.a(lambda e, sub=sub: e.copy(ctx.gtok.ap()[:, sub * 8:(sub + 1) * 8],
                                      big2[:, 0:8]))
    for sub in range(4):
        P.t(lambda e, sub=sub: e.matmul(big2[:, 16:24], ctx.trilf.ap(),
                                        ctx.gtok.ap()[:, sub * 8:(sub + 1) * 8],
                                        start=True, stop=False, skip_group_check=True))
        P.t(lambda e: e.matmul(big2[:, 16:24], ctx.onesrow1.ap(), ctx.gcarry.ap(),
                               start=False, stop=True, skip_group_check=True))
        P.a(lambda e, sub=sub: e.copy(ctx.Gtok.ap()[:, sub * 8:(sub + 1) * 8],
                                      big2[:, 16:24]))
        P.t(lambda e, sub=sub: e.matmul(big2[0:1, 24:32], ctx.onescol.ap(),
                                        ctx.gtok.ap()[:, sub * 8:(sub + 1) * 8],
                                        start=True, stop=False, skip_group_check=True))
        P.t(lambda e: e.matmul(big2[0:1, 24:32], ctx.onesrow1.ap()[:, 0:1],
                               ctx.gcarry.ap(), start=False, stop=True,
                               skip_group_check=True))
        P.v(lambda e: e.tensor_copy(ctx.gcarry.ap(), big2[0:1, 24:32]))
        P.brk()
    P.v(lambda e: e.tensor_scalar_mul(ctx.msplit.ap(), ctx.Gtok.ap(), -1.0))
    P.a(lambda e: e.copy(ctx.spl.ap()[:, 0:32], ctx.msplit.ap()))
    P.v(lambda e: e.tensor_copy(ctx.splf.ap(), ctx.spl.ap()[:, 0:32]))
    P.brk()
    P.v(lambda e: e.tensor_tensor(ctx.msplit.ap(), ctx.msplit.ap(), ctx.splf.ap(),
                                  AL.subtract))
    P.a(lambda e: e.copy(ctx.spl.ap()[:, 32:64], ctx.msplit.ap()))
    P.v(lambda e: e.tensor_copy(ctx.splf.ap(), ctx.spl.ap()[:, 32:64]))
    P.brk()
    P.v(lambda e: e.tensor_tensor(ctx.msplit.ap(), ctx.msplit.ap(), ctx.splf.ap(),
                                  AL.subtract))
    P.a(lambda e: e.copy(ctx.spl.ap()[:, 64:96], ctx.msplit.ap()))
    for sub in range(4):
        tt = tg * 4 + sub
        P.v(lambda e, sub=sub, tt=tt: e.tensor_copy(
            ctx.negG3_tm.ap()[:, tt * 24: tt * 24 + 24]
            .rearrange("p (h s) -> p h s", s=3),
            ctx.spl.ap().rearrange("p (s q) -> p s q", q=32)[:, :, sub * 8:(sub + 1) * 8]
            .rearrange("p s h -> p h s")))
    for sub in range(4):
        tt = tg * 4 + sub
        P.t(lambda e, tt=tt: e.transpose(ctx.psT.ap()[0:24, 0:128],
                                         ctx.negG3_tm.ap()[:, tt * 24: tt * 24 + 24],
                                         ctx.idb.ap()))
        P.a(lambda e: e.copy(ctx.stage24.ap(), ctx.psT.ap()[0:24, 0:128]))
        P.s(lambda e, tt=tt: e.dma_start(
            out=P.negG3_hbm.rearrange("h s t -> (h s) t")[:, tt * 128:(tt + 1) * 128],
            in_=ctx.stage24.ap()))


def emit_wchain(P, ctx):
    psA, big1, big2, psT = ctx.psA.ap(), ctx.big1.ap(), ctx.big2.ap(), ctx.psT.ap()
    for dt in range(4):
        P.v(lambda e: e.memset(ctx.wpre.ap()[:, 0:4], 0.0))
        for half in range(2):
            for tgh in range(4):
                P.t(lambda e, dt=dt, half=half, tgh=tgh: e.matmul(
                    psA[:, 0:512], ctx.w2t.ap()[:, dt * 128:(dt + 1) * 128],
                    ctx.w0T.ap()[:, half * 2048 + tgh * 512: half * 2048 + (tgh + 1) * 512],
                    start=True, stop=True, skip_group_check=True))
                P.a(lambda e, tgh=tgh: e.copy(
                    ctx.wpre.ap()[:, 4 + tgh * 512: 4 + (tgh + 1) * 512], psA[:, 0:512]))
            P.v(lambda e, dt=dt: e.tensor_scalar_mul(
                ctx.wconv.ap(), ctx.wpre.ap()[:, 1:2049], ctx.convs.ap()[:, dt * 4: dt * 4 + 1]))
            for k in range(1, 4):
                P.brk()
                P.v(lambda e, dt=dt, k=k: e.scalar_tensor_tensor(
                    ctx.wconv.ap(), ctx.wpre.ap()[:, 1 + k: 2049 + k],
                    ctx.convs.ap()[:, dt * 4 + k: dt * 4 + k + 1], ctx.wconv.ap(),
                    AL.mult, AL.add))
            P.brk()
            P.v(lambda e: e.tensor_copy(ctx.wpre.ap()[:, 0:4], ctx.wpre.ap()[:, 2048:2052]))
            P.a(lambda e: e.activation(ctx.wpre.ap()[:, 4:2052], ctx.wconv.ap(), AF.Square))
            for tgh in range(4):
                P.t(lambda e, tgh=tgh: e.matmul(
                    big1[0:2, 0:512], ctx.onesblk.ap(),
                    ctx.wpre.ap()[:, 4 + tgh * 512: 4 + (tgh + 1) * 512],
                    start=True, stop=True, skip_group_check=True))
                P.v(lambda e: e.tensor_scalar_add(ctx.nrm2.ap(), big1[0:2, 0:512], 1e-12))
                P.a(lambda e: e.activation(ctx.bgS.ap()[0:2, :], ctx.nrm2.ap(), AF.Ln))
                P.brk2()
                P.a(lambda e: e.activation(ctx.nrm2.ap(), ctx.bgS.ap()[0:2, :],
                                           AF.Exp, scale=-0.5))
                P.t(lambda e: e.matmul(big2[:, 0:512], ctx.onesblkT.ap(), ctx.nrm2.ap(),
                                       start=True, stop=True, skip_group_check=True))
                P.v(lambda e, tgh=tgh: e.tensor_tensor(
                    ctx.wpre.ap()[:, 4 + tgh * 512: 4 + (tgh + 1) * 512],
                    ctx.wconv.ap()[:, tgh * 512:(tgh + 1) * 512], big2[:, 0:512], AL.mult))
                for s4 in range(4):
                    tt = half * 16 + tgh * 4 + s4
                    P.t(lambda e, tgh=tgh, s4=s4: e.transpose(
                        psT[:, 0:128],
                        ctx.wpre.ap()[:, 4 + tgh * 512 + s4 * 128: 4 + tgh * 512 + (s4 + 1) * 128],
                        ctx.idb.ap()))
                    P.a(lambda e, tt=tt, dt=dt: e.copy(
                        ctx.w_tok.ap()[:, tt * 512 + dt * 128: tt * 512 + (dt + 1) * 128],
                        psT[:, 0:128]))


def emit_head(P, ctx, ch, h, dram, win=WIN):
    P.chain = ch.cid
    P.s(lambda e: e.dma_start(out=ch.Kd.ap()[64:67, :],
                              in_=dram["c_ones3T"][:, 0:T]))
    P.s(lambda e, h=h: e.dma_start(out=ch.Kd.ap()[67:70, :], in_=P.negG3_hbm[h]))
    P.s(lambda e: e.dma_start(out=ch.q70.ap()[67:70, :],
                              in_=dram["c_ones3T"][:, 0:256]))
    P.v(lambda e: e.memset(ch.q70.ap()[64:67, :], 0.0))


def emit_chunk(P, ctx, ch, h, c, win=WIN):
    P.chain = ch.cid
    t0 = c * CHUNK
    tt = c
    r = c % win
    wstart = (c // win) * win * CHUNK

    qs = ctx.q_tok.ap()[:, tt * 64 * H + 64 * h: tt * 64 * H + 64 * (h + 1)]
    ks = ctx.k_tok.ap()[:, tt * 64 * H + 64 * h: tt * 64 * H + 64 * (h + 1)]
    ws = ctx.w_tok.ap()[:, tt * 64 * H + 64 * h: tt * 64 * H + 64 * (h + 1)]
    bc = ctx.bcol.ap()[:, tt * H + h: tt * H + h + 1]
    gsl = ctx.negG3_tm.ap()[:, tt * 3 * H + 3 * h: tt * 3 * H + 3 * (h + 1)]

    seq, big, pvw, tp = ch.seq, ch.big, ch.pvw, ch.tp
    QbT = ch.QbT.ap()[0:64, 0:128]
    KbT = ch.QbT.ap()[0:64, 128:256]
    WbT = ch.QbT.ap()[0:64, 256:384]

    P.t(lambda e: e.transpose(tp[0:64, 0:128], qs, ctx.idb.ap()))
    P.t(lambda e: e.transpose(tp[0:64, 128:256], ks, ctx.idb.ap()))
    P.t(lambda e: e.transpose(tp[0:64, 256:384], ws, ctx.idb.ap()))
    P.t(lambda e: e.transpose(tp[0:3, 384:512], gsl, ctx.idb.ap()))
    P.a(lambda e: e.copy(ch.QbT.ap()[0:64, 0:384], tp[0:64, 0:384]))
    P.a(lambda e: e.copy(ch.negG3r.ap(), tp[0:3, 384:512]))
    P.a(lambda e: e.mul(ch.QbTs.ap(), QbT, SCALE))
    P.v(lambda e: e.tensor_scalar(ch.negbWb.ap(), ws, bc, -1.0, AL.mult, AL.mult))

    P.t(lambda e: e.matmul(seq[:, 0:128], WbT, WbT, start=True, stop=True,
                           skip_group_check=True))
    P.t(lambda e: e.matmul(seq[:, 128:256], WbT, KbT, start=True, stop=True,
                           skip_group_check=True))
    P.t(lambda e: e.matmul(seq[:, 256:384], WbT, QbT, start=True, stop=True,
                           skip_group_check=True))
    P.v(lambda e: e.scalar_tensor_tensor(ch.Xp.ap()[:, 0:128], seq[:, 0:128], bc,
                                         ctx.mask_slneg.ap(), AL.mult, AL.mult))
    P.v(lambda e: e.tensor_tensor(ch.X_u.ap(), seq[:, 0:128], ctx.mask_su.ap(),
                                  AL.mult))
    P.v(lambda e: e.tensor_tensor(ch.negsWK.ap(), seq[:, 128:256],
                                  ctx.mask_slneg.ap(), AL.mult))
    P.v(lambda e: e.tensor_tensor(ch.QWt_m.ap(), seq[:, 256:384],
                                  ctx.trilf.ap(), AL.mult))
    P.t(lambda e: e.transpose(seq[:, 384:512], ch.Xp.ap()[:, 0:128], ctx.idf.ap()))
    P.a(lambda e: e.copy(ch.Xp.ap()[:, 128:256], seq[:, 384:512]))

    P.v(lambda e: e.tensor_tensor(ch.Mp.ap()[:, 0:128], ch.Xp.ap()[:, 0:128],
                                  ctx.idf.ap(), AL.add))
    P.v(lambda e: e.tensor_tensor(ch.Mp.ap()[:, 128:256], ch.Xp.ap()[:, 128:256],
                                  ctx.idf.ap(), AL.add))
    for lev in range(6):
        P.t(lambda e: e.matmul(seq[:, 0:128], ch.Xp.ap()[:, 128:256],
                               ch.Xp.ap()[:, 0:128], start=True, stop=True,
                               skip_group_check=True))
        P.t(lambda e: e.matmul(seq[:, 128:256], ch.Xp.ap()[:, 0:128],
                               ch.Xp.ap()[:, 128:256], start=True, stop=True,
                               skip_group_check=True))
        P.a(lambda e: e.copy(ch.Xp.ap()[:, 0:256], seq[:, 0:256]))
        P.t(lambda e: e.matmul(seq[:, 256:384], ch.Mp.ap()[:, 128:256],
                               ch.Xp.ap()[:, 0:128], start=True, stop=True,
                               skip_group_check=True))
        P.t(lambda e: e.matmul(seq[:, 384:512], ch.Xp.ap()[:, 0:128],
                               ch.Mp.ap()[:, 128:256], start=True, stop=True,
                               skip_group_check=True))
        P.v(lambda e: e.tensor_tensor(ch.Mp.ap()[:, 0:256], ch.Mp.ap()[:, 0:256],
                                      seq[:, 256:512], AL.add))

    Minv = ch.Mp.ap()[:, 0:128]
    P.t(lambda e: e.matmul(seq[:, 0:128], Minv, ch.QWt_m.ap(), start=True,
                           stop=True, skip_group_check=True))
    P.a(lambda e: e.copy(ch.RT.ap(), seq[:, 0:128]))
    P.v(lambda e: e.tensor_scalar(ch.RbT.ap(), ch.RT.ap(), bc, None, AL.mult))
    P.a(lambda e: e.mul(ch.RbTs.ap(), ch.RbT.ap(), SCALE))
    P.t(lambda e: e.matmul(seq[0:64, 128:256], ch.negbWb.ap(), ch.RT.ap(),
                           start=True, stop=True, skip_group_check=True))
    P.v(lambda e: e.tensor_tensor(ch.QtT.ap(), seq[0:64, 128:256], QbT, AL.add))
    P.a(lambda e: e.mul(ch.q70.ap()[0:64, 128:256], ch.QtT.ap(), SCALE))
    if r == 0:
        P.a(lambda e: e.mul(ch.q70.ap()[0:64, 0:128], ch.QtT.ap(), SCALE))
    else:
        P.t(lambda e: e.matmul(seq[0:64, 256:384], ch.S.ap()[:, 0:64], ch.QtT.ap(),
                               start=True, stop=True, skip_group_check=True))
        P.a(lambda e: e.mul(ch.q70.ap()[0:64, 0:128], seq[0:64, 256:384], SCALE))

    groups = []
    for j0 in range(0, wstart, 512):
        groups.append((j0, min(512, wstart - j0), 0))
    if t0 > wstart:
        groups.append((wstart, t0 - wstart, 128))
    nmp = 0
    for (j0, wdt, voff) in groups:
        P.t(lambda e, j0=j0, wdt=wdt, voff=voff:
            e.matmul(big[:, 0:wdt], ch.q70.ap()[:, voff:voff + 128],
                     ch.Kd.ap()[:, j0:j0 + wdt], start=True, stop=True,
                     skip_group_check=True))
        P.v(lambda e, wdt=wdt, nmp=nmp:
            e.tensor_reduce(ch.mpart.ap()[:, nmp:nmp + 1], big[:, 0:wdt],
                            AX.X, AL.max))
        nmp += 1
    P.t(lambda e: e.matmul(big[:, 0:128], ch.QbTs.ap(), KbT, start=True,
                           stop=False, skip_group_check=True))
    P.t(lambda e: e.matmul(big[:, 0:128], ch.RbTs.ap(), ch.negsWK.ap(),
                           start=False, stop=False, skip_group_check=True))
    P.t(lambda e: e.matmul(big[:, 0:128], ctx.ones3.ap(), ch.negG3r.ap(),
                           start=False, stop=False, skip_group_check=True))
    P.t(lambda e: e.matmul(big[:, 0:128], ctx.maskA.ap(), ctx.idb.ap(),
                           start=False, stop=True, skip_group_check=True))
    P.v(lambda e, nmp=nmp: e.tensor_reduce(ch.mpart.ap()[:, nmp:nmp + 1],
                                           big[:, 0:128], AX.X, AL.max))
    nmp += 1
    P.brk()
    P.v(lambda e, nmp=nmp: e.tensor_reduce(ch.mcol.ap()[:, 0:1],
                                           ch.mpart.ap()[:, 0:nmp], AX.X, AL.max))
    P.a(lambda e: e.mul(ch.negm_sb.ap(), ch.mcol.ap()[:, 0:1], -1.0))
    P.t(lambda e: e.transpose(pvw[0:1, 128:256], ch.negm_sb.ap(), ctx.idf.ap()))
    P.a(lambda e: e.copy(ch.negmrow4.ap()[:, 0:128], pvw[0:1, 128:256]))
    for rr in range(1, 4):
        P.v(lambda e, rr=rr: e.tensor_copy(ch.negmrow4.ap()[:, rr * 128:(rr + 1) * 128],
                                           ch.negmrow4.ap()[:, 0:128]))

    n_jt = t0 // 128
    total_jt = n_jt + 1
    pv_count = [0]

    def emit_pv(jt, pslice):
        vsl = ctx.v65.ap()[:, jt * 65 * H + 65 * h: jt * 65 * H + 65 * h + 65]
        first = pv_count[0] == 0
        last = pv_count[0] == total_jt - 1
        P.t(lambda e, vsl=vsl, pslice=pslice, first=first, last=last: e.matmul(
            pvw[0:65, 0:128], vsl, pslice, start=first, stop=last,
            skip_group_check=True))
        pv_count[0] += 1

    pos = 0
    pbswap = 0
    while pos < n_jt:
        grp = []
        while pos < n_jt and len(grp) < 4:
            grp.append(pos)
            pos += 1
        for gi, jt in enumerate(grp):
            voff = 0 if (jt * 128) < wstart else 128
            P.t(lambda e, jt=jt, gi=gi, voff=voff:
                e.matmul(big[:, gi * 128:(gi + 1) * 128],
                         ch.Kd.ap()[:, jt * 128:(jt + 1) * 128],
                         ch.q70.ap()[:, voff:voff + 128],
                         start=(gi == 0), stop=False, skip_group_check=True))
        nc_ = len(grp) * 128
        P.t(lambda e, nc_=nc_: e.matmul(
            big[:, 0:nc_], ctx.onesrow1.ap(), ch.negmrow4.ap()[:, 0:nc_],
            start=False, stop=True, skip_group_check=True))
        pb = ch.P_buf.ap()[:, (pbswap % 2) * 512:(pbswap % 2) * 512 + nc_]
        P.a(lambda e, pb=pb, nc_=nc_: e.activation(pb, big[:, 0:nc_], AF.Exp))
        for gi, jt in enumerate(grp):
            emit_pv(jt, ch.P_buf.ap()[:, (pbswap % 2) * 512 + gi * 128:
                                      (pbswap % 2) * 512 + (gi + 1) * 128])
        pbswap += 1
    P.t(lambda e: e.matmul(big[:, 0:128], KbT, ch.QbTs.ap(), start=True,
                           stop=False, skip_group_check=True))
    P.t(lambda e: e.matmul(big[:, 0:128], ch.negsWK.ap(), ch.RbTs.ap(),
                           start=False, stop=False, skip_group_check=True))
    P.t(lambda e: e.matmul(big[:, 0:128], ch.negG3r.ap(), ctx.ones3.ap(),
                           start=False, stop=False, skip_group_check=True))
    P.t(lambda e: e.matmul(big[:, 0:128], ctx.onesrow1.ap(),
                           ch.negmrow4.ap()[:, 0:128], start=False, stop=False,
                           skip_group_check=True))
    P.t(lambda e: e.matmul(big[:, 0:128], ctx.maskB.ap(), ctx.idb.ap(),
                           start=False, stop=True, skip_group_check=True))
    P.a(lambda e: e.activation(ch.P_buf.ap()[:, 896:1024], big[:, 0:128], AF.Exp))
    emit_pv(n_jt, ch.P_buf.ap()[:, 896:1024])

    P.v(lambda e: e.reciprocal(ch.zinv.ap(), pvw[64:65, 0:128]))
    P.v(lambda e: e.tensor_copy(ch.ot_raw.ap(), pvw[0:64, 0:128]))
    P.t(lambda e: e.matmul(pvw[0:64, 256:384], ctx.ones1_64.ap(), ch.zinv.ap(),
                           start=True, stop=True, skip_group_check=True))
    P.a(lambda e: e.copy(ch.zrep.ap(), pvw[0:64, 256:384]))
    ob = ctx.oT.ap()[ch.cid * 64: ch.cid * 64 + 64, t0:t0 + CHUNK]
    P.v(lambda e, ob=ob: e.tensor_tensor(ob, ch.ot_raw.ap(), ch.zrep.ap(), AL.mult))

    P.t(lambda e: e.matmul(seq[:, 0:128], Minv, ch.X_u.ap(), start=True,
                           stop=True, skip_group_check=True))
    P.a(lambda e: e.copy(ch.RwT.ap(), seq[:, 0:128]))
    P.t(lambda e: e.matmul(seq[:, 128:192], ch.RwT.ap(), ch.negbWb.ap(),
                           start=True, stop=True, skip_group_check=True))
    P.v(lambda e: e.tensor_tensor(ch.V_sb.ap(), seq[:, 128:192], ws, AL.add))
    P.t(lambda e: e.matmul(seq[0:64, 192:256], ch.V_sb.ap(), ch.negbWb.ap(),
                           start=True, stop=True, skip_group_check=True))
    P.v(lambda e: e.tensor_tensor(ch.McT.ap()[0:64, 0:64], seq[0:64, 192:256],
                                  ctx.idf.ap()[0:64, 0:64], AL.add))
    P.v(lambda e: e.tensor_scalar(ch.bsWK.ap(), ch.negsWK.ap(), bc, -1.0,
                                  AL.mult, AL.mult))
    P.t(lambda e: e.matmul(seq[:, 256:384], ch.RwT.ap(), ch.bsWK.ap(),
                           start=True, stop=True, skip_group_check=True))
    P.v(lambda e: e.scalar_tensor_tensor(ch.D_sb.ap(), seq[:, 256:384], -1.0,
                                         ch.negsWK.ap(), AL.mult, AL.subtract))
    if t0 > wstart:
        wdt = t0 - wstart
        P.t(lambda e, wdt=wdt: e.matmul(pvw[0:64, 0:wdt], ch.McT.ap()[0:64, 0:64],
                                        ch.Kd.ap()[0:64, wstart:t0],
                                        start=True, stop=True, skip_group_check=True))
        P.a(lambda e, wdt=wdt: e.copy(ch.Kd.ap()[0:64, wstart:t0], pvw[0:64, 0:wdt]))
    P.t(lambda e: e.matmul(seq[0:64, 384:512], ch.negbWb.ap(), ch.D_sb.ap(),
                           start=True, stop=True, skip_group_check=True))
    P.v(lambda e: e.tensor_tensor(ch.Kd.ap()[0:64, t0:t0 + CHUNK],
                                  seq[0:64, 384:512], KbT, AL.add))
    if r == 0:
        P.t(lambda e: e.transpose(tp[0:64, 0:64], ch.McT.ap()[0:64, 0:64],
                                  ctx.idb.ap()[0:64, 0:64]))
        P.a(lambda e: e.copy(ch.S.ap()[:, 0:64], tp[0:64, 0:64]))
        P.a(lambda e: e.copy(ch.S.ap()[:, 64:128], ch.McT.ap()[0:64, 0:64]))
    else:
        P.t(lambda e: e.matmul(pvw[0:64, 384:448], ch.McT.ap()[0:64, 0:64],
                               ch.S.ap()[:, 0:64], start=True, stop=True,
                               skip_group_check=True))
        P.t(lambda e: e.matmul(pvw[0:64, 448:512], ch.S.ap()[:, 0:64],
                               ch.McT.ap()[0:64, 0:64], start=True, stop=True,
                               skip_group_check=True))
        P.a(lambda e: e.copy(ch.S.ap()[:, 0:128], pvw[0:64, 384:512]))
    if r == win - 1 and wstart > 0:
        for j0 in range(0, wstart, 512):
            wdt = min(512, wstart - j0)
            P.t(lambda e, j0=j0, wdt=wdt: e.matmul(
                pvw[0:64, 0:wdt], ch.S.ap()[:, 64:128],
                ch.Kd.ap()[0:64, j0:j0 + wdt], start=True, stop=True,
                skip_group_check=True))
            P.a(lambda e, j0=j0, wdt=wdt: e.copy(ch.Kd.ap()[0:64, j0:j0 + wdt],
                                                 pvw[0:64, 0:wdt]))


class BCtx:
    def __init__(self, nc, actx=None):
        self._guards = []
        if actx is not None:
            def A(name, shape, dt):
                g = nc.sbuf_tensor(name, shape, dt)
                h = g.__enter__()
                self._guards.append(g)
                return h
            AP_ = None
        else:
            A = nc.alloc_sbuf_tensor
            AP_ = nc.alloc_psum_tensor
        self._actx = actx
        self.moe = A("moe", [128, 8 * 2048], F32)
        self.x2T = A("x2T", [128, 16 * 1024], BF16)
        self.he = A("he", [128, 11 * 1024], BF16)
        self.gblk = A("gblk", [128, 2 * 16 * 256], BF16)
        self.ublk = A("ublk", [128, 2 * 16 * 256], BF16)
        self.dblk = A("dblk", [128, 2 * 11 * 512], BF16)
        self.otk = A("otk", [128, 2 * 2048], BF16)
        self.xbuf = A("xbuf", [128, 2048], F32)
        self.x2b = A("x2b", [128, 2048], BF16)
        self.crep = A("crep", [128, 1024], BF16)
        self.rwT = A("rwT", [128, 16 * 8], BF16)
        self.comb = A("comb", [128, 64], F32)
        self.combb = A("combb", [128, 64], BF16)
        self.combT = A("combT", [8, 1024], BF16)
        self.sel = A("sel", [8, 1024], BF16)
        self.sgb = A("sgb", [128, 22], F32)
        self.sub = A("sub", [128, 22], F32)
        if actx is not None:
            self.idb, self.idf = actx.idb, actx.idf
            self.onesrow1, self.junk, self.fvals = actx.onesrow1, actx.junk, actx.fvals
        else:
            self.idb = A("idb", [128, 128], BF16)
            self.idf = A("idf", [128, 128], F32)
            self.onesrow1 = A("onesrow1", [1, 128], F32)
            self.junk = A("junk", [1, 8], F32)
            self.fvals = A("fvals", [128, 8], F32)
            nc.const_aps.register(F32, 1e-6, self.fvals.ap()[:, 0:1])
            nc.const_aps.register(F32, 1.0, self.fvals.ap()[:, 1:2])
            nc.const_aps.register(F32, 1e-5, self.fvals.ap()[:, 2:3])
            nc.const_aps.register(F32, 1e-12, self.fvals.ap()[:, 3:4])
        self.r1 = A("r1", [128, 1], F32)
        self.r2 = A("r2", [128, 1], F32)
        self.r3 = A("r3", [128, 1], F32)
        self.r4 = A("r4", [128, 1], F32)
        self.r5 = A("r5", [128, 8], F32)
        self.r6 = A("r6", [128, 8], F32)
        self.sg_s = A("sg_s", [128, 512], BF16)
        self.uc_s = A("uc_s", [128, 512], BF16)
        if actx is not None:
            self.ps = [actx.psA, actx.psM, actx.psQ, actx.big1, actx.big2,
                       actx.psPV, actx.psA]
            self.psT = actx.psT
        else:
            self.ps = [AP_(f"ps{i}", [128, 512], F32) for i in range(7)]
            self.psT = AP_("psTb", [128, 1024], BF16)


def emit_ln(P, ctx, src_ap, tile_cols, out_ap):
    P.v(lambda e: e.tensor_reduce(ctx.r1.ap(), src_ap, AX.X, AL.add))
    P.a(lambda e: e.activation(ctx.xbuf.ap(), src_ap, AF.Square))
    P.v(lambda e: e.tensor_reduce(ctx.r2.ap(), ctx.xbuf.ap(), AX.X, AL.add))
    P.a(lambda e: e.mul(ctx.r3.ap(), ctx.r1.ap(), 1.0 / 2048))
    P.brk2()
    P.a(lambda e: e.activation(ctx.r1.ap(), ctx.r3.ap(), AF.Square))
    P.v(lambda e: e.scalar_tensor_tensor(ctx.r2.ap(), ctx.r2.ap(), 1.0 / 2048,
                                         ctx.r1.ap(), AL.mult, AL.subtract))
    P.a(lambda e: e.activation(ctx.r4.ap(), ctx.r2.ap(), AF.Ln, bias=1e-5))
    P.brk2()
    P.a(lambda e: e.activation(ctx.r2.ap(), ctx.r4.ap(), AF.Exp, scale=-0.5))
    P.a(lambda e: e.mul(ctx.r4.ap(), ctx.r3.ap(), -1.0))
    P.v(lambda e: e.tensor_scalar(out_ap, src_ap, ctx.r4.ap(), ctx.r2.ap(),
                                  AL.add, AL.mult))


def emit_launchB(P, ctx, dram, fused=False):
    nc = P.nc
    ps = [p.ap() for p in ctx.ps]
    psT = ctx.psT.ap()

    def otk_load(e, tt, par):
        dst = ctx.otk.ap()[:, par * 2048:(par + 1) * 2048] \
            .rearrange("p (k n) -> p k n", n=128)
        if fused:
            if "tq" not in P.regs:
                pid = e.partition_id()
                tq = e.scalar_reg_alu(AL.mod, pid, 4)
                P.regs["tq"] = e.scalar_reg_alu(AL.mult, tq, TQ)
            src3 = dram["oTq"].rearrange("(k p) t -> p k t", p=128)
            return e.dma_start(out=dst,
                               in_=src3[:, :, bass.ds(P.regs["tq"] + tt * 128, 128)])
        return e.dma_start(out=dst, in_=dram["oTq"][:, tt * 128:(tt + 1) * 128]
                           .rearrange("(k p) n -> p k n", p=128))

    n_rwT = P.gd(lambda e: e.dma_start(
        out=ctx.rwT.ap().rearrange("p (k n) -> p k n", n=8),
        in_=dram["rwT"].rearrange("(k p) n -> p k n", p=128)))
    P.s(lambda e: e.dma_start(out=ctx.sgb.ap(), in_=dram["sgb"]))
    P.s(lambda e: e.dma_start(out=ctx.sub.ap(), in_=dram["sub"]))
    P.s(lambda e: e.dma_start(out=ctx.idb.ap(), in_=dram["c_idb"]))
    P.s(lambda e: e.dma_start(out=ctx.idf.ap(), in_=dram["c_idf"]))
    P.s(lambda e: e.dma_start(out=ctx.onesrow1.ap(), in_=dram["c_onesrow1"]))
    P.s(lambda e: e.dma_start(out=ctx.fvals.ap(), in_=dram["c_fvals"]))
    P.s(lambda e: e.dma_start(out=ctx.sel.ap(), in_=dram["sel"]))

    wo_slab = [ctx.gblk.ap()[:, 0:8192], ctx.ublk.ap()[:, 0:8192]]
    wo_ord = {}
    for og in range(2):
        wo_ord[og] = P.gd(lambda e, og=og: e.dma_start(
            out=wo_slab[og % 2].rearrange("p (k n) -> p k n", n=512),
            in_=dram["WoT"][:, og * 512:(og + 1) * 512]
            .rearrange("(k p) n -> p k n", p=128)))
    cc_tok = getattr(P, "cc_tok", None)
    otk_ord = {}
    otk_ord[0] = P.gd(lambda e: otk_load(e, 0, 0), wait=cc_tok)
    wo_tok = {}
    otk_tok = {}
    for og in range(4):
        for tt in range(8):
            bk = ps[tt % 2]
            idx = og * 8 + tt
            for kt in range(16):
                def mmfn(e, kt=kt, bk=bk, og=og, idx=idx):
                    if kt == 0:
                        P.gwait(wo_ord[og])(e)
                        P.gwait(otk_ord[idx])(e)
                    return e.matmul(
                        bk[:, 0:512],
                        ctx.otk.ap()[:, (idx % 2) * 2048 + kt * 128:
                                     (idx % 2) * 2048 + (kt + 1) * 128],
                        wo_slab[og % 2][:, kt * 512:(kt + 1) * 512],
                        start=(kt == 0), stop=(kt == 15), skip_group_check=True)
                P.t(mmfn)
            P.a(lambda e, tt=tt, og=og, bk=bk: e.copy(
                ctx.moe.ap()[:, tt * 2048 + og * 512: tt * 2048 + (og + 1) * 512],
                bk[:, 0:512]))
            otk_tok[idx] = P.token()
            nxt = idx + 1
            if nxt < 32:
                wt = otk_tok.get(nxt - 2) or cc_tok
                otk_ord[nxt] = P.gd(
                    lambda e, nxt=nxt: otk_load(e, nxt % 8, nxt % 2), wait=wt)
        wo_tok[og] = otk_tok[og * 8 + 7]
        if og + 2 < 4:
            wo_ord[og + 2] = P.gd(lambda e, og=og: e.dma_start(
                out=wo_slab[og % 2].rearrange("p (k n) -> p k n", n=512),
                in_=dram["WoT"][:, (og + 2) * 512:(og + 3) * 512]
                .rearrange("(k p) n -> p k n", p=128)), wait=wo_tok[og])

    for tt in range(8):
        mt = ctx.moe.ap()[:, tt * 2048:(tt + 1) * 2048]
        P.s(lambda e, tt=tt: e.dma_start(out=ctx.xbuf.ap(),
                                         in_=dram["x_tok"][tt * 128:(tt + 1) * 128, :]))
        P.v(lambda e, mt=mt: e.tensor_tensor(mt, mt, ctx.xbuf.ap(), AL.add))
        P.brk()
        emit_ln(P, ctx, mt, 2048, ctx.xbuf.ap())
        P.brk()
        P.v(lambda e: e.tensor_copy(ctx.x2b.ap(), ctx.xbuf.ap()))
        P.a(lambda e, mt=mt: e.copy(mt, ctx.xbuf.ap()))
        for kt in range(16):
            P.t(lambda e, kt=kt: e.transpose(psT[:, 0:128],
                                             ctx.x2b.ap()[:, kt * 128:(kt + 1) * 128],
                                             ctx.idb.ap()))
            P.a(lambda e, kt=kt, tt=tt: e.copy(
                ctx.x2T.ap()[:, kt * 1024 + tt * 128: kt * 1024 + (tt + 1) * 128],
                psT[:, 0:128]))

    for tt in range(8):
        bk = ps[2]
        for kt in range(16):
            def rfn(e, kt=kt, tt=tt, bk=bk):
                if kt == 0 and tt == 0:
                    P.gwait(n_rwT)(e)
                return e.matmul(
                    bk[:, 0:8],
                    ctx.x2T.ap()[:, kt * 1024 + tt * 128: kt * 1024 + (tt + 1) * 128],
                    ctx.rwT.ap()[:, kt * 8:(kt + 1) * 8],
                    start=(kt == 0), stop=(kt == 15), skip_group_check=True)
            P.t(rfn)
        P.v(lambda e, bk=bk: e.tensor_reduce(ctx.r1.ap(), bk[:, 0:8], AX.X, AL.max))
        P.a(lambda e: e.mul(ctx.r2.ap(), ctx.r1.ap(), -1.0))
        P.brk2()
        P.a(lambda e, bk=bk: e.activation(ctx.r5.ap(), bk[:, 0:8], AF.Exp,
                                          bias=ctx.r2.ap()))
        P.brk()
        P.v(lambda e: e.tensor_reduce(ctx.r1.ap(), ctx.r5.ap(), AX.X, AL.add))
        P.brk()
        P.v(lambda e: e.reciprocal(ctx.r2.ap(), ctx.r1.ap()))
        P.brk()
        P.v(lambda e: e.tensor_scalar(ctx.r5.ap(), ctx.r5.ap(), ctx.r2.ap(), None,
                                      AL.mult))
        P.brk()
        P.v(lambda e: e.tensor_reduce(ctx.r1.ap(), ctx.r5.ap(), AX.X, AL.max))
        P.brk()
        P.v(lambda e: e.tensor_scalar(ctx.r6.ap(), ctx.r5.ap(), ctx.r1.ap(), None,
                                      AL.is_ge))
        P.brk()
        P.v(lambda e: e.scalar_tensor_tensor(ctx.r6.ap(), ctx.r6.ap(), -2.0,
                                             ctx.r5.ap(), AL.mult, AL.add))
        P.brk()
        P.v(lambda e: e.tensor_reduce(ctx.r2.ap(), ctx.r6.ap(), AX.X, AL.max))
        P.brk()
        P.v(lambda e: e.tensor_tensor(ctx.r3.ap(), ctx.r1.ap(), ctx.r2.ap(), AL.add))
        P.brk()
        P.v(lambda e: e.reciprocal(ctx.r4.ap(), ctx.r3.ap()))
        P.v(lambda e: e.tensor_scalar(ctx.r6.ap(), ctx.r5.ap(), ctx.r2.ap(), None,
                                      AL.is_ge))
        P.brk()
        P.v(lambda e, tt=tt: e.scalar_tensor_tensor(
            ctx.comb.ap()[:, tt * 8:(tt + 1) * 8], ctx.r5.ap(), 1.0, ctx.r6.ap(),
            AL.mult, AL.mult))
        P.brk()
        P.v(lambda e, tt=tt: e.tensor_scalar(ctx.comb.ap()[:, tt * 8:(tt + 1) * 8],
                                             ctx.comb.ap()[:, tt * 8:(tt + 1) * 8],
                                             ctx.r4.ap(), None, AL.mult))
        P.brk()
        P.a(lambda e, tt=tt: e.copy(ctx.combb.ap()[:, tt * 8:(tt + 1) * 8],
                                    ctx.comb.ap()[:, tt * 8:(tt + 1) * 8]))
        P.t(lambda e, tt=tt: e.transpose(psT[0:8, 0:128],
                                         ctx.combb.ap()[:, tt * 8:(tt + 1) * 8],
                                         ctx.idb.ap()))
        P.a(lambda e, tt=tt: e.copy(ctx.combT.ap()[:, tt * 128:(tt + 1) * 128],
                                    psT[0:8, 0:128]))

    ex_blocks = [(0, 2), (2, 2), (4, 2), (6, 2), (8, 2), (10, 1)]
    sh_blocks = [(0, 2), (2, 2), (4, 2), (6, 2), (8, 2), (10, 1),
                 (11, 2), (13, 2), (15, 2), (17, 2), (19, 2), (21, 1)]
    items = []
    for ex in range(E):
        for (it0, itn) in ex_blocks:
            items.append(("gu", ("gate_w", ex), ("up_w", ex), it0, itn))
        for og in range(4):
            items.append(("d", ("down_w", ex), 0, og))
    for bi, (it0, itn) in enumerate(sh_blocks):
        items.append(("gu", ("s_gate_w", None), ("s_up_w", None), it0, itn))
        if (it0 + itn) in (11, 22):
            halfE = 0 if (it0 + itn) == 11 else 1
            for og in range(4):
                items.append(("d", ("s_down_w", None), halfE * 11 * 128, og))

    def wsrc(spec):
        name, ex = spec
        return dram[name][ex] if ex is not None else dram[name]

    def load_item(e, it):
        kind = it[0]
        if kind == "gu2":
            pass
        if kind == "gu":
            _, gsrc, usrc, it0, itn = it
            return None
        return None

    ords = {}
    toks = {}

    def emit_load(i, wait):
        it = items[i]
        if it[0] == "gu":
            _, gsrc, usrc, it0, itn = it
            par = i % 2
            o1 = P.gd(lambda e, gsrc=gsrc, it0=it0, itn=itn, par=par: e.dma_start(
                out=ctx.gblk.ap()[:, par * 4096: par * 4096 + 16 * itn * 128]
                .rearrange("p (k n) -> p k n", n=itn * 128),
                in_=wsrc(gsrc)[:, it0 * 128:(it0 + itn) * 128]
                .rearrange("(k p) n -> p k n", p=128)), wait=wait)
            o2 = P.gd(lambda e, usrc=usrc, it0=it0, itn=itn, par=par: e.dma_start(
                out=ctx.ublk.ap()[:, par * 4096: par * 4096 + 16 * itn * 128]
                .rearrange("p (k n) -> p k n", n=itn * 128),
                in_=wsrc(usrc)[:, it0 * 128:(it0 + itn) * 128]
                .rearrange("(k p) n -> p k n", p=128)), wait=None)
            ords[i] = o2
        else:
            _, dsrc, rows0, og = it
            par = i % 2
            ords[i] = P.gd(lambda e, dsrc=dsrc, rows0=rows0, og=og, par=par:
                           e.dma_start(
                out=ctx.dblk.ap()[:, par * 5632: par * 5632 + 11 * 512]
                .rearrange("p (k n) -> p k n", n=512),
                in_=wsrc(dsrc)[rows0: rows0 + 11 * 128, og * 512:(og + 1) * 512]
                .rearrange("(k p) n -> p k n", p=128)), wait=wait)

    emit_load(0, wo_tok[2])
    emit_load(1, wo_tok[3])
    crep_done = -1
    for i, it in enumerate(items):
        par = i % 2
        if it[0] == "gu":
            _, gsrc, usrc, it0, itn = it
            shared = gsrc[1] is None
            ex = gsrc[1]
            if not shared and ex != crep_done and it0 == 0:
                for half in range(2):
                    P.t(lambda e, ex=ex, half=half: e.matmul(
                        ps[6][:, 0:512], ctx.sel.ap()[:, ex * 128:(ex + 1) * 128],
                        ctx.combT.ap()[:, half * 512:(half + 1) * 512],
                        start=True, stop=True, skip_group_check=True))
                    P.a(lambda e, half=half: e.copy(
                        ctx.crep.ap()[:, half * 512:(half + 1) * 512],
                        ps[6][:, 0:512]))
                crep_done = ex
            for itl in range(itn):
                itg = it0 + itl
                ith = itg % 11
                for tg2 in range(2):
                    bkg, bku = ps[(itl + tg2) % 2], ps[2 + (itl + tg2) % 2]
                    for kt in range(16):
                        def gfn(e, kt=kt, itl=itl, itn=itn, tg2=tg2, bkg=bkg,
                                par=par, i=i):
                            if kt == 0:
                                P.gwait(ords[i])(e)
                            return e.matmul(
                                bkg[:, 0:512],
                                ctx.gblk.ap()[:, par * 4096 + kt * itn * 128 + itl * 128:
                                              par * 4096 + kt * itn * 128 + (itl + 1) * 128],
                                ctx.x2T.ap()[:, kt * 1024 + tg2 * 512:
                                             kt * 1024 + (tg2 + 1) * 512],
                                start=(kt == 0), stop=(kt == 15),
                                skip_group_check=True)
                        P.t(gfn)
                        P.t(lambda e, kt=kt, itl=itl, itn=itn, tg2=tg2, bku=bku,
                            par=par: e.matmul(
                            bku[:, 0:512],
                            ctx.ublk.ap()[:, par * 4096 + kt * itn * 128 + itl * 128:
                                          par * 4096 + kt * itn * 128 + (itl + 1) * 128],
                            ctx.x2T.ap()[:, kt * 1024 + tg2 * 512:
                                         kt * 1024 + (tg2 + 1) * 512],
                            start=(kt == 0), stop=(kt == 15), skip_group_check=True))
                    if shared:
                        P.a(lambda e, bkg=bkg, itg=itg: e.activation(
                            ctx.sg_s.ap(), bkg[:, 0:512], AF.Silu,
                            bias=ctx.sgb.ap()[:, itg:itg + 1]))
                        P.v(lambda e, bku=bku, itg=itg: e.tensor_scalar_add(
                            ctx.uc_s.ap(), bku[:, 0:512], ctx.sub.ap()[:, itg:itg + 1]))
                    else:
                        P.a(lambda e, bkg=bkg: e.activation(ctx.sg_s.ap(),
                                                            bkg[:, 0:512], AF.Silu))
                        P.v(lambda e, bku=bku, tg2=tg2: e.tensor_tensor(
                            ctx.uc_s.ap(), bku[:, 0:512],
                            ctx.crep.ap()[:, tg2 * 512:(tg2 + 1) * 512], AL.mult))
                    P.brk()
                    P.v(lambda e, ith=ith, tg2=tg2: e.tensor_tensor(
                        ctx.he.ap()[:, ith * 1024 + tg2 * 512:
                                    ith * 1024 + (tg2 + 1) * 512],
                        ctx.sg_s.ap(), ctx.uc_s.ap(), AL.mult))
            toks[i] = P.token()
        else:
            _, dsrc, rows0, og = it
            for tt in range(8):
                bk = ps[4 + tt % 2]
                for kt in range(11):
                    def dfn(e, kt=kt, tt=tt, bk=bk, par=par, i=i):
                        if kt == 0 and tt == 0:
                            P.gwait(ords[i])(e)
                        return e.matmul(
                            bk[:, 0:512],
                            ctx.he.ap()[:, kt * 1024 + tt * 128:
                                        kt * 1024 + tt * 128 + 128],
                            ctx.dblk.ap()[:, par * 5632 + kt * 512:
                                          par * 5632 + (kt + 1) * 512],
                            start=(kt == 0), stop=(kt == 10),
                            skip_group_check=True)
                    P.t(dfn)
                P.v(lambda e, tt=tt, og=og, bk=bk: e.tensor_tensor(
                    ctx.moe.ap()[:, tt * 2048 + og * 512: tt * 2048 + (og + 1) * 512],
                    ctx.moe.ap()[:, tt * 2048 + og * 512: tt * 2048 + (og + 1) * 512],
                    bk[:, 0:512], AL.add))
            toks[i] = P.token()
        if i + 2 < len(items):
            emit_load(i + 2, toks.get(i))

    P.s(lambda e: e.dma_start(out=ctx.xbuf.ap()[0:1, :], in_=dram["sdb"]))
    for og in range(4):
        P.t(lambda e, og=og: e.matmul(ps[6][:, 0:512], ctx.onesrow1.ap(),
                                      ctx.xbuf.ap()[0:1, og * 512:(og + 1) * 512],
                                      start=True, stop=True, skip_group_check=True))
        for tt in range(8):
            P.v(lambda e, tt=tt, og=og: e.tensor_tensor(
                ctx.moe.ap()[:, tt * 2048 + og * 512: tt * 2048 + (og + 1) * 512],
                ctx.moe.ap()[:, tt * 2048 + og * 512: tt * 2048 + (og + 1) * 512],
                ps[6][:, 0:512], AL.add))
    for tt in range(8):
        mt = ctx.moe.ap()[:, tt * 2048:(tt + 1) * 2048]
        emit_ln(P, ctx, mt, 2048, ctx.xbuf.ap())
        P.s(lambda e, tt=tt: e.dma_start(out=P.out_q[tt * 128:(tt + 1) * 128, :],
                                         in_=ctx.xbuf.ap()))
        P.brk()


def build_launchB():
    nc = bass.Bass("TRN2", target_bir_lowering=False)
    dram = {}
    for name in ["c_idb", "c_idf", "c_onesrow1", "c_fvals"]:
        arr = CONST_SPECS[name]
        dt = BF16 if arr.dtype == BF else F32
        dram[name] = nc.dram_tensor(name, list(arr.shape), dt, kind="ExternalInput").ap()
    spec = [("oTq", [D, TQ], BF16), ("x_tok", [TQ, D], F32), ("WoT", [D, D], F32),
            ("rwT", [D, 8], F32), ("sel", [8, 1024], BF16),
            ("s_gate_w", [D, SI], F32), ("s_up_w", [D, SI], F32),
            ("s_down_w", [SI, D], F32), ("sgb", [128, 22], F32),
            ("sub", [128, 22], F32), ("sdb", [1, 2048], F32)]
    for name, shape, dt in spec:
        dram[name] = nc.dram_tensor("d_" + name, shape, dt, kind="ExternalInput").ap()
    dram["gate_w"] = nc.dram_tensor("d_gate_w", [E, D, I_FF], F32, kind="ExternalInput").ap()
    dram["up_w"] = nc.dram_tensor("d_up_w", [E, D, I_FF], F32, kind="ExternalInput").ap()
    dram["down_w"] = nc.dram_tensor("d_down_w", [E, I_FF, D], F32, kind="ExternalInput").ap()
    out_q = nc.dram_tensor("out_q", [TQ, D], F32, kind="ExternalOutput").ap()

    ctx = BCtx(nc)
    with nc.Block() as block:
        P = Prog(nc, nbar=8)
        P._junk = ctx.junk.ap()
        P.out_q = out_q
        emit_launchB(P, ctx, dram)
        P.v(lambda e: e.engine_nop())
        P.lower(block)
    return nc


def build_fused():
    nc = bass.Bass("TRN2", target_bir_lowering=False)
    dram = {}
    for name, arr in CONST_SPECS.items():
        dt = BF16 if arr.dtype == BF else F32
        dram[name] = nc.dram_tensor(name, list(arr.shape), dt, kind="ExternalInput").ap()
    insA = [("xT", [D, T], F32), ("WqT", [D, 512], F32), ("WkT", [D, 512], F32),
            ("WvT", [D, 512], F32), ("W1T", [D, 32], F32), ("W2T", [32, 512], F32),
            ("bgT", [D, 64], F32), ("bgb", [64, 1], F32), ("convs", [128, 16], F32),
            ("qgrow", [1, 512], F32), ("kgrow", [1, 512], F32)]
    insB = [("x_tok", [TQ, D], F32), ("WoT", [D, D], F32),
            ("rwT", [D, 8], F32), ("sel", [8, 1024], BF16),
            ("s_gate_w", [D, SI], F32), ("s_up_w", [D, SI], F32),
            ("s_down_w", [SI, D], F32), ("sgb", [128, 22], F32),
            ("sub", [128, 22], F32), ("sdb", [1, 2048], F32),
            ("gate_w", [E, D, I_FF], F32), ("up_w", [E, D, I_FF], F32),
            ("down_w", [E, I_FF, D], F32)]
    for name, shape, dt in insA + insB:
        dram[name] = nc.dram_tensor("d_" + name, shape, dt, kind="ExternalInput").ap()
    out_q = nc.dram_tensor("out_q", [TQ, D], F32, kind="ExternalOutput").ap()
    negG3_hbm = nc.dram_tensor("negG3_scr", [H, 3, T], BF16, kind="Internal").ap()
    cc_in = nc.dram_tensor("cc_in", [512, T], BF16, kind="Internal").ap()
    cc_out = nc.dram_tensor("cc_out", [D, T], BF16, kind="Internal").ap()

    ctx = ACtx(nc, guarded=True)
    with nc.Block() as block:
        P = Prog(nc, nbar=8)
        P._junk = ctx.junk.ap()
        P.negG3_hbm = negG3_hbm
        P.out_q = out_q
        emit_load_consts(P, ctx, dram)
        emit_prep(P, ctx, dram)
        tokP = P.token()
        release_guards(ctx)
        ctx.alloc_attn()
        P.join(1, tokP)
        for pair in range(4):
            hA, hB = 2 * pair, 2 * pair + 1
            emit_head(P, ctx, ctx.ch[0], hA, dram)
            emit_head(P, ctx, ctx.ch[1], hB, dram)
            for c in range(NCH):
                emit_chunk(P, ctx, ctx.ch[0], hA, c)
                emit_chunk(P, ctx, ctx.ch[1], hB, c)
            P.chain = 0
            P.s(lambda e, hA=hA: e.dma_start(
                out=cc_in[hA * 64: hA * 64 + 64, :], in_=ctx.oT.ap()[0:64, :]))
            P.chain = 1
            P.s(lambda e, hB=hB: e.dma_start(
                out=cc_in[hB * 64: hB * 64 + 64, :], in_=ctx.oT.ap()[64:128, :]))
        tok1 = P.token()
        P.chain = 0
        P.join(0, tok1)
        ctx.release_attn()
        ctxB = BCtx(nc, actx=ctx)
        P.cc(lambda e: e.collective_compute(
            "AllGather", AL.bypass, [[0, 1, 2, 3], [4, 5, 6, 7]],
            [cc_in], [cc_out]))
        P.cc_tok = P.token()
        emit_launchB(P, ctxB, dict(dram, oTq=cc_out), fused=True)
        P.v(lambda e: e.engine_nop())
        P.lower(block)
    return nc


def kernel(**inputs):
    consts = host_consts()
    x = np.asarray(inputs["L0_attn_hidden_states"], dtype=np.float32)

    nc = build_fused()
    selc = np.zeros((8, 1024), np.float32)
    for ex in range(8):
        selc[ex, ex * 128:(ex + 1) * 128] = 1.0
    gate_w = np.ascontiguousarray(np.asarray(inputs["L0_ffn_gate_w"], np.float32))
    up_w = np.ascontiguousarray(np.asarray(inputs["L0_ffn_up_w"], np.float32))
    down_w = np.ascontiguousarray(np.asarray(inputs["L0_ffn_down_w"], np.float32))
    s_gate_w = np.ascontiguousarray(np.asarray(inputs["L0_ffn_s_gate_w"], np.float32))
    s_up_w = np.ascontiguousarray(np.asarray(inputs["L0_ffn_s_up_w"], np.float32))
    s_down_w = np.ascontiguousarray(np.asarray(inputs["L0_ffn_s_down_w"], np.float32))
    WoT = np.ascontiguousarray(np.asarray(inputs["L0_attn_Wo"]).T.astype(np.float32))
    rwT = np.ascontiguousarray(np.asarray(inputs["L0_ffn_router_w"]).T.astype(np.float32))
    W1T = np.ascontiguousarray(np.asarray(inputs["L0_attn_W1"]).T.astype(np.float32))
    sgb_l = np.ascontiguousarray(
        np.asarray(inputs["L0_ffn_s_gate_b"], np.float32).reshape(22, 128).T)
    sub_l = np.ascontiguousarray(
        np.asarray(inputs["L0_ffn_s_up_b"], np.float32).reshape(22, 128).T)
    sdb_l = np.asarray(inputs["L0_ffn_s_down_b"], np.float32)[None, :]
    xT = [np.ascontiguousarray(x[b].T) for b in range(B)]

    in_maps = []
    for c in range(N_CORES):
        b = c // 4
        hg = c % 4
        tq = c % 4
        hs = slice(hg * 8 * HD, (hg + 1) * 8 * HD)
        hh = slice(hg * 8, (hg + 1) * 8)
        tsl = slice(tq * TQ, (tq + 1) * TQ)
        m = dict(consts)
        m["d_xT"] = xT[b]
        m["d_WqT"] = np.ascontiguousarray(np.asarray(inputs["L0_attn_Wq"])[hs].T.astype(np.float32))
        m["d_WkT"] = np.ascontiguousarray(np.asarray(inputs["L0_attn_Wk"])[hs].T.astype(np.float32))
        m["d_WvT"] = np.ascontiguousarray(np.asarray(inputs["L0_attn_Wv"])[hs].T.astype(np.float32))
        m["d_W1T"] = W1T
        m["d_W2T"] = np.ascontiguousarray(np.asarray(inputs["L0_attn_W2"]).T[:, hs].astype(np.float32))
        bg = np.zeros((D, 64), np.float32)
        bg[:, 0:8] = np.asarray(inputs["L0_attn_bt_w"])[hh].T
        bg[:, 32:40] = np.asarray(inputs["L0_attn_g_w"])[hh].T
        m["d_bgT"] = bg
        bgb = np.zeros((64, 1), np.float32)
        bgb[0:8, 0] = np.asarray(inputs["L0_attn_bt_b"])[hh]
        bgb[32:40, 0] = np.asarray(inputs["L0_attn_g_b"])[hh]
        m["d_bgb"] = bgb
        cw = np.asarray(inputs["L0_attn_conv_w"])[hs]
        m["d_convs"] = np.ascontiguousarray(
            cw.reshape(4, 128, 4).transpose(1, 0, 2).reshape(128, 16)).astype(np.float32)
        m["d_qgrow"] = np.tile(np.asarray(inputs["L0_attn_q_norm_w"], np.float32), 8)[None, :]
        m["d_kgrow"] = np.tile(np.asarray(inputs["L0_attn_k_norm_w"], np.float32), 8)[None, :]
        m["d_x_tok"] = np.ascontiguousarray(x[b][tsl])
        m["d_WoT"] = WoT
        m["d_rwT"] = rwT
        m["d_sel"] = selc.astype(BF)
        m["d_gate_w"] = gate_w
        m["d_up_w"] = up_w
        m["d_down_w"] = down_w
        m["d_s_gate_w"] = s_gate_w
        m["d_s_up_w"] = s_up_w
        m["d_s_down_w"] = s_down_w
        m["d_sgb"] = sgb_l
        m["d_sub"] = sub_l
        m["d_sdb"] = sdb_l
        in_maps.append(m)

    res = run_bass_kernel_spmd(nc, in_maps, core_ids=list(range(N_CORES)))
    out = np.zeros((B, T, D), dtype=np.float32)
    for c in range(N_CORES):
        b = c // 4
        tq = c % 4
        out[b, tq * TQ:(tq + 1) * TQ] = res.results[c]["out_q"]
    return out


# version 08
# speedup vs baseline: 1.6436x; 1.4458x over previous
import numpy as np
import ml_dtypes
import concourse.bass as bass
import concourse.mybir as mybir
from concourse.bass_utils import run_bass_kernel_spmd

F32 = mybir.dt.float32
BF16 = mybir.dt.bfloat16
AF = mybir.ActivationFunctionType
AL = mybir.AluOpType
AX = mybir.AxisListType
BF = ml_dtypes.bfloat16

BIGNEG = -8.0e37
DMA_ENGS = ("sync", "gpsimd")

B, T, D, H_ALL, HD = 2, 4096, 2048, 32, 64
H = 8
NT = T // 128
CHUNK = 128
NCH = T // CHUNK
WIN = 4
SCALE = HD ** (-0.5)
E, I_FF, SI = 8, 1408, 2816
TQ = 1024
N_CORES = 8


class Prog:
    ENGS = ("tensor", "vector", "scalar", "gpsimd", "sync")

    def __init__(self, nc, nbar=8):
        self.nc = nc
        self.steps = []
        self.nbar = nbar
        self._junk = None
        self.regs = {}
        self._tok_requests = []
        self._gd_count = 0
        self.gsem = None
        self.chain = 0
        self.nchains = 2
        self._cap = None
        self._sems = None
        self._ccsem = None

    class Tok:
        __slots__ = ("chain", "slot", "val")
        def __init__(self):
            self.chain = None
            self.slot = None
            self.val = None

    def token(self):
        t = Prog.Tok()
        self._tok_requests.append((len(self.steps) - 1, t))
        return t

    def join(self, chain, tok):
        def fn(e, tok=tok):
            e.wait_ge(self._semof(tok), tok.val)
            return e.memzero(self._junk)
        self.steps.append(("scalar", fn, "n", chain))

    def _semof(self, tok):
        if tok.chain == "cc":
            return self._ccsem
        return self._sems[tok.chain][tok.slot]

    def gd(self, fn, wait=None):
        n = self._gd_count
        self._gd_count += 1
        self.steps.append(("gpsimd", fn, ("gd", wait), 0))
        return n

    def gwait(self, n):
        def w(e):
            e.wait_ge(self.gsem, 16 * (n + 1))
        return w

    def _emit(self, step):
        if self._cap is not None:
            self._cap.append(step)
        else:
            self.steps.append(step)

    def capture_begin(self):
        self._cap = []

    def capture_end(self):
        l = self._cap
        self._cap = None
        return l

    def interleave(self, LA, LB):
        def groups(L):
            gs = []
            for s in L:
                if (gs and gs[-1][-1][0] == s[0] and gs[-1][-1][3] == s[3]
                        and s[2] == "n" and gs[-1][-1][2] == "n"):
                    gs[-1].append(s)
                else:
                    gs.append([s])
            return gs
        ga, gb = groups(LA), groups(LB)
        for i in range(max(len(ga), len(gb))):
            if i < len(ga):
                self.steps.extend(ga[i])
            if i < len(gb):
                self.steps.extend(gb[i])

    def brk(self):
        self._emit(("scalar", lambda e: e.memzero(self._junk), "n", self.chain))

    def brk2(self):
        self._emit(("vector", lambda e: e.memzero(self._junk), "n", self.chain))

    def t(self, fn): self._emit(("tensor", fn, "n", self.chain))
    def v(self, fn): self._emit(("vector", fn, "n", self.chain))
    def a(self, fn): self._emit(("scalar", fn, "n", self.chain))
    def g(self, fn): self._emit(("gpsimd", fn, "n", self.chain))
    def s(self, fn): self._emit(("sync", fn, "n", self.chain))
    def cc(self, fn): self.steps.append(("gpsimd", fn, "cc", 0))

    def lower(self, block):
        nc = self.nc
        NB = self.nbar
        sems = [[nc.alloc_semaphore(f"bar{c}_{i}") for i in range(NB)]
                for c in range(self.nchains)]
        cc_sem = nc.alloc_semaphore("cc_sem")
        self.gsem = nc.alloc_semaphore("gsem")
        self._sems = sems
        self._ccsem = cc_sem
        runs = []
        step_run = {}
        for si, (eng, fn, kind, ch) in enumerate(self.steps):
            if isinstance(kind, tuple) and kind[0] == "gd":
                runs.append([eng, [fn], "gd", kind[1], 0])
            elif (kind == "n" and runs and runs[-1][0] == eng
                  and runs[-1][2] == "n" and runs[-1][4] == ch):
                runs[-1][1].append(fn)
            else:
                runs.append([eng, [fn], kind, None, ch])
            step_run[si] = len(runs) - 1
        nruns = len(runs)
        sem_tot = [[0] * NB for _ in range(self.nchains)]
        run_wait = [None] * nruns
        cum_after = [None] * nruns
        prev_chain_run = [None] * self.nchains
        chain_idx = [0] * self.nchains
        for i, (eng, fns, kind, extra, ch) in enumerate(runs):
            if kind == "gd":
                continue
            if prev_chain_run[ch] is not None:
                run_wait[i] = cum_after[prev_chain_run[ch]]
            if kind == "cc":
                cum_after[i] = ("cc", 0, 1)
            else:
                s = chain_idx[ch] % NB
                amt = 16 * len(fns) if eng in DMA_ENGS else 1
                sem_tot[ch][s] += amt
                cum_after[i] = (ch, s, sem_tot[ch][s])
            prev_chain_run[ch] = i
            chain_idx[ch] += 1
        for si, tok in self._tok_requests:
            ri = step_run[si]
            assert cum_after[ri] is not None, "token on a non-chained step"
            tok.chain, tok.slot, tok.val = cum_after[ri]
        per_eng = {e: [] for e in self.ENGS}
        for i, r in enumerate(runs):
            per_eng[r[0]].append((i, r[0], r[1], r[2], r[3], r[4]))

        def semof(marker):
            ch, s, v = marker
            return (cc_sem if ch == "cc" else sems[ch][s]), v

        def make_body(items):
            def body(eng):
                for (i, ename, fns, kind, extra, ch) in items:
                    if kind == "gd":
                        if extra is not None:
                            sm, v = semof((extra.chain, extra.slot, extra.val))
                            eng.wait_ge(sm, v)
                        fns[0](eng).then_inc(self.gsem, 16)
                        continue
                    if run_wait[i] is not None:
                        sm, v = semof(run_wait[i])
                        eng.wait_ge(sm, v)
                    if kind == "cc":
                        fns[0](eng).then_inc(cc_sem)
                        continue
                    _c, s, _v = cum_after[i]
                    if ename in DMA_ENGS:
                        for fn in fns:
                            fn(eng).then_inc(sems[ch][s], 16)
                    else:
                        last = None
                        for fn in fns:
                            rr = fn(eng)
                            if rr is not None:
                                last = rr
                        last.then_inc(sems[ch][s], 1)
            return body
        for e in self.ENGS:
            if per_eng[e]:
                getattr(block, e)(make_body(per_eng[e]))


def cdiv(a, b):
    return (a + b - 1) // b


def host_consts():
    r = np.arange(128)
    c = {}
    c["c_idb"] = np.eye(128, dtype=np.float32).astype(BF)
    c["c_idf"] = np.eye(128, dtype=np.float32)
    c["c_mask_slneg"] = np.where(r[:, None] > r[None, :], -1.0, 0.0).astype(np.float32)
    c["c_mask_su"] = np.where(r[:, None] < r[None, :], 1.0, 0.0).astype(np.float32)
    c["c_mask_le"] = np.where(r[:, None] <= r[None, :], 1.0, 0.0).astype(np.float32)
    c["c_maskA"] = np.where(r[:, None] > r[None, :], BIGNEG, 0.0).astype(np.float32).astype(BF)
    c["c_maskB"] = np.where(r[:, None] < r[None, :], BIGNEG, 0.0).astype(np.float32).astype(BF)
    c["c_ones3"] = np.ones((3, 128), dtype=np.float32).astype(BF)
    c["c_ones1_64"] = np.ones((1, 64), dtype=np.float32)
    c["c_ones3T"] = np.ones((3, 4096), dtype=np.float32).astype(BF)
    c["c_onesblk"] = np.stack([(r < 64).astype(np.float32),
                               (r >= 64).astype(np.float32)], axis=1).astype(BF)
    c["c_onesblkT"] = np.ascontiguousarray(
        np.stack([(r < 64), (r >= 64)], axis=0).astype(np.float32))
    c["c_trilf"] = np.where(r[:, None] <= r[None, :], 1.0, 0.0).astype(np.float32)
    c["c_onescol"] = np.ones((128, 1), dtype=np.float32)
    c["c_fvals"] = np.tile(np.array([[1e-6, 1.0, 1e-5, 1e-12, 0, 0, 0, 0]],
                                    dtype=np.float32), (128, 1))
    c["c_onesrow1"] = np.ones((1, 128), dtype=np.float32)
    return c


CONST_SPECS = {k: v for k, v in host_consts().items()}


class ChainCtx:
    def __init__(self, A, nc, cid, psT):
        self.cid = cid
        sfx = f"_c{cid}"
        self.Kd = A("Kd70" + sfx, [70, T], BF16)
        self.q70 = A("q70" + sfx, [70, 256], BF16)
        self.S = A("S_sb" + sfx, [64, 128], BF16)
        self.QbT = A("QbT" + sfx, [64, 384], BF16)
        self.QbTs = A("QbTs" + sfx, [64, 128], BF16)
        self.negbWb = A("negbWb" + sfx, [128, 64], BF16)
        self.Xp = A("Xp" + sfx, [128, 256], F32)
        self.Mp = A("Mp" + sfx, [128, 256], F32)
        self.QWt_m = A("QWt_m" + sfx, [128, 128], F32)
        self.RT = A("RT" + sfx, [128, 128], BF16)
        self.RbT = A("RbT" + sfx, [128, 128], BF16)
        self.RbTs = A("RbTs" + sfx, [128, 128], BF16)
        self.QtT = A("QtT" + sfx, [64, 128], BF16)
        self.negsWK = A("negsWK" + sfx, [128, 128], BF16)
        self.bsWK = A("bsWK" + sfx, [128, 128], BF16)
        self.X_u = A("X_u" + sfx, [128, 128], F32)
        self.RwT = A("RwT" + sfx, [128, 128], BF16)
        self.V_sb = A("V_sb" + sfx, [128, 64], BF16)
        self.McT = A("McT" + sfx, [64, 128], BF16)
        self.D_sb = A("D_sb" + sfx, [128, 128], BF16)
        self.negG3r = A("negG3r" + sfx, [3, 128], BF16)
        self.mpart = A("mpart" + sfx, [128, 12], F32)
        self.mcol = A("mcol" + sfx, [128, 1], F32)
        self.negm_sb = A("negm_sb" + sfx, [128, 1], F32)
        self.negmrow4 = A("negmrow4" + sfx, [1, 512], F32)
        self.zinv = A("zinv" + sfx, [1, 128], F32)
        self.zrep = A("zrep" + sfx, [64, 128], F32)
        self.ot_raw = A("ot_raw" + sfx, [64, 128], F32)
        self.P_buf = A("P_buf" + sfx, [128, 1024], BF16)
        self.seq = None
        self.big = None
        self.pvw = None
        self.tp = psT


class ACtx:
    def __init__(self, nc, guarded=False):
        self._guards = []
        self._guards_attn = []
        self.nc = nc
        if guarded:
            def A(name, shape, dt):
                g = nc.sbuf_tensor(name, shape, dt)
                h = g.__enter__()
                self._guards.append(g)
                return h
        else:
            A = nc.alloc_sbuf_tensor
        self._A_guarded = guarded
        AP_ = nc.alloc_psum_tensor
        AC = nc.alloc_sbuf_tensor
        self.idb = AC("idb", [128, 128], BF16)
        self.idf = AC("idf", [128, 128], F32)
        self.mask_slneg = AC("mask_slneg", [128, 128], F32)
        self.mask_su = AC("mask_su", [128, 128], F32)
        self.maskA = AC("maskA", [128, 128], BF16)
        self.maskB = AC("maskB", [128, 128], BF16)
        self.ones3 = AC("ones3", [3, 128], BF16)
        self.ones1_64 = AC("ones1_64", [1, 64], F32)
        self.onesblk = AC("onesblk", [128, 2], BF16)
        self.onesblkT = AC("onesblkT", [2, 128], F32)
        self.trilf = AC("trilf", [128, 128], F32)
        self.onescol = AC("onescol", [128, 1], F32)
        self.onesrow1 = AC("onesrow1", [1, 128], F32)
        self.junk = AC("junk", [1, 8], F32)
        self.fvals = AC("fvals", [128, 8], F32)
        nc.const_aps.register(F32, 1e-6, self.fvals.ap()[:, 0:1])
        nc.const_aps.register(F32, 1.0, self.fvals.ap()[:, 1:2])
        nc.const_aps.register(F32, 1e-5, self.fvals.ap()[:, 2:3])
        nc.const_aps.register(F32, 1e-12, self.fvals.ap()[:, 3:4])
        self._guards_io = []
        if guarded:
            def AIO(name, shape, dt):
                g = nc.sbuf_tensor(name, shape, dt)
                h = g.__enter__()
                self._guards_io.append(g)
                return h
        else:
            AIO = nc.alloc_sbuf_tensor
        self.q_tok = AIO("q_tok", [128, NT * 64 * H], BF16)
        self.k_tok = AIO("k_tok", [128, NT * 64 * H], BF16)
        self.w_tok = AIO("w_tok", [128, NT * 64 * H], BF16)
        self.v65 = AIO("v65", [128, NT * 65 * H], BF16)
        self.bcol = AIO("bcol", [128, NT * H], F32)
        self.negG3_tm = AIO("negG3_tm", [128, NT * 3 * H], BF16)
        self.xtile = A("xtile", [128, 512], BF16)
        self.wblk = A("wblk", [128, 512], BF16)
        self.w1t = A("w1t", [128, 16 * 32], BF16)
        self.w2t = A("w2t", [32, 512], BF16)
        self.bgTw = A("bgTw", [128, 16 * 64], BF16)
        self.bgb = A("bgb", [64, 1], F32)
        self.qgrow = A("qgrow", [1, 512], F32)
        self.kgrow = A("kgrow", [1, 512], F32)
        self.grep = A("grep", [128, 512], F32)
        self.convs = A("convs", [128, 16], F32)
        self.sqbuf = A("sqbuf", [128, 512], F32)
        self.rstd8 = A("rstd8", [128, 8], F32)
        self.var8 = A("var8", [128, 8], F32)
        self.w0T = A("w0T", [32, T], BF16)
        self.wpre = A("wpre", [128, 2052], BF16)
        self.wconv = A("wconv", [128, 2048], F32)
        self.nrm2 = A("nrm2", [2, 512], F32)
        self.bgB = A("bgB", [40, 512], F32)
        self.bgS = A("bgS", [40, 512], F32)
        self.bgS2 = A("bgS2", [40, 512], F32)
        self.gtok = A("gtok", [128, 32], F32)
        self.Gtok = A("Gtok", [128, 32], F32)
        self.gcarry = A("gcarry", [1, 8], F32)
        self.msplit = A("msplit", [128, 32], F32)
        self.spl = A("spl", [128, 3 * 32], BF16)
        self.splf = A("splf", [128, 32], F32)
        self.stage24 = A("stage24", [24, 128], BF16)
        self.psA = AP_("psA", [128, 512], F32)
        self.psM = AP_("psM", [128, 512], F32)
        self.psQ = AP_("psQ", [128, 512], F32)
        self.big1 = AP_("big1", [128, 512], F32)
        self.big2 = AP_("big2", [128, 512], F32)
        self.psPV = AP_("psPV", [128, 512], F32)
        self.psT = AP_("psT", [128, 2048], BF16)

    def alloc_attn(self):
        nc = self.nc
        if self._A_guarded:
            def A(name, shape, dt):
                g = nc.sbuf_tensor(name, shape, dt)
                h = g.__enter__()
                self._guards_attn.append(g)
                return h
        else:
            A = nc.alloc_sbuf_tensor
        self.ch = [ChainCtx(A, nc, 0, None), ChainCtx(A, nc, 1, None)]
        self.oT = A("oT", [128, T], BF16)
        self.ch[0].seq = self.psA.ap()
        self.ch[0].big = self.psM.ap()
        self.ch[0].pvw = self.psQ.ap()
        self.ch[0].tp = self.psT.ap()[:, 0:1024]
        self.ch[1].seq = self.big1.ap()
        self.ch[1].big = self.big2.ap()
        self.ch[1].pvw = self.psPV.ap()
        self.ch[1].tp = self.psT.ap()[:, 1024:2048]

    def release_attn(self):
        for g in reversed(self._guards_attn):
            g.__exit__(None, None, None)
        self._guards_attn = []
        for g in reversed(self._guards_io):
            g.__exit__(None, None, None)
        self._guards_io = []


def release_guards(ctx):
    for g in reversed(ctx._guards):
        g.__exit__(None, None, None)
    ctx._guards = []


def emit_load_consts(P, ctx, dram):
    pairs = [("c_idb", ctx.idb), ("c_idf", ctx.idf), ("c_mask_slneg", ctx.mask_slneg),
             ("c_mask_su", ctx.mask_su),
             ("c_maskA", ctx.maskA), ("c_maskB", ctx.maskB), ("c_ones3", ctx.ones3),
             ("c_ones1_64", ctx.ones1_64), ("c_onesblk", ctx.onesblk),
             ("c_onesblkT", ctx.onesblkT), ("c_trilf", ctx.trilf),
             ("c_onescol", ctx.onescol), ("c_onesrow1", ctx.onesrow1),
             ("c_fvals", ctx.fvals)]
    for name, dst in pairs:
        if name in dram:
            P.s(lambda e, name=name, dst=dst: e.dma_start(out=dst.ap(), in_=dram[name]))


def emit_rms_apply(P, ctx, ps, grep, dst_tok, tt, eps):
    P.a(lambda e, ps=ps: e.activation(ctx.sqbuf.ap(), ps, AF.Square))
    P.v(lambda e: e.tensor_reduce(ctx.var8.ap(),
                                  ctx.sqbuf.ap().rearrange("p (h d) -> p h d", d=64),
                                  AX.X, AL.add))
    P.a(lambda e, eps=eps: e.activation(ctx.rstd8.ap(), ctx.var8.ap(), AF.Ln,
                                        scale=1.0 / 64, bias=eps))
    P.brk2()
    P.a(lambda e: e.activation(ctx.var8.ap(), ctx.rstd8.ap(), AF.Exp, scale=-0.5))
    for seg in range(8):
        P.v(lambda e, ps=ps, seg=seg, tt=tt: e.scalar_tensor_tensor(
            dst_tok.ap()[:, tt * 512 + seg * 64: tt * 512 + (seg + 1) * 64],
            ps[:, seg * 64:(seg + 1) * 64], ctx.var8.ap()[:, seg:seg + 1],
            grep.ap()[:, seg * 64:(seg + 1) * 64], AL.mult, AL.mult))


def emit_prep(P, ctx, dram):
    nc = P.nc
    psA = ctx.psA.ap()
    banks = [ctx.psA.ap(), ctx.psM.ap(), ctx.psQ.ap(), ctx.big2.ap()]
    big1, big2, psPV, psT = ctx.big1.ap(), ctx.big2.ap(), ctx.psPV.ap(), ctx.psT.ap()

    P.g(lambda e: e.dma_start(out=ctx.w1t.ap().rearrange("p (k n) -> p k n", n=32),
                              in_=dram["W1T"].rearrange("(k p) n -> p k n", p=128)))
    P.g(lambda e: e.dma_start(out=ctx.w2t.ap(), in_=dram["W2T"]))
    P.g(lambda e: e.dma_start(out=ctx.bgTw.ap().rearrange("p (k n) -> p k n", n=64),
                              in_=dram["bgT"].rearrange("(k p) n -> p k n", p=128)))
    P.s(lambda e: e.dma_start(out=ctx.bgb.ap(), in_=dram["bgb"]))
    P.s(lambda e: e.dma_start(out=ctx.convs.ap(), in_=dram["convs"]))
    P.s(lambda e: e.dma_start(out=ctx.qgrow.ap(), in_=dram["qgrow"]))
    P.s(lambda e: e.dma_start(out=ctx.kgrow.ap(), in_=dram["kgrow"]))
    P.v(lambda e: e.memset(ctx.v65.ap(), 1.0))
    P.v(lambda e: e.memset(ctx.gcarry.ap(), 0.0))
    P.v(lambda e: e.memset(ctx.wpre.ap()[:, 0:4], 0.0))

    for which, wname, grow, dst in [("q", "WqT", ctx.qgrow, ctx.q_tok),
                                    ("k", "WkT", ctx.kgrow, ctx.k_tok),
                                    ("v", "WvT", None, ctx.v65)]:
        if grow is not None:
            P.t(lambda e, grow=grow: e.matmul(
                psA[:, 0:512], ctx.onesrow1.ap(), grow.ap(),
                start=True, stop=True, skip_group_check=True))
            P.a(lambda e: e.copy(ctx.grep.ap(), psA[:, 0:512]))
        for tg in range(8):
            for kt in range(16):
                P.g(lambda e, tg=tg, kt=kt: e.dma_start(
                    out=ctx.xtile.ap(),
                    in_=dram["xT"][kt * 128:(kt + 1) * 128, tg * 512:(tg + 1) * 512]))
                P.g(lambda e, wname=wname, kt=kt: e.dma_start(
                    out=ctx.wblk.ap(),
                    in_=dram[wname][kt * 128:(kt + 1) * 128, :]))
                for sub in range(4):
                    P.t(lambda e, kt=kt, sub=sub: e.matmul(
                        banks[sub][:, 0:512],
                        ctx.xtile.ap()[:, sub * 128:(sub + 1) * 128],
                        ctx.wblk.ap(), start=(kt == 0), stop=(kt == 15),
                        skip_group_check=True))
                if which == "v":
                    P.t(lambda e, kt=kt: e.matmul(
                        psPV[0:32, 0:512], ctx.w1t.ap()[:, kt * 32:(kt + 1) * 32],
                        ctx.xtile.ap()[:, 0:512],
                        start=(kt == 0), stop=(kt == 15), skip_group_check=True))
                    P.t(lambda e, kt=kt: e.matmul(
                        big1[0:64, 0:512], ctx.bgTw.ap()[:, kt * 64:(kt + 1) * 64],
                        ctx.xtile.ap()[:, 0:512],
                        start=(kt == 0), stop=(kt == 15), skip_group_check=True))
            if which in ("q", "k"):
                for sub in range(4):
                    emit_rms_apply(P, ctx, banks[sub][:, 0:512], ctx.grep, dst,
                                   tg * 4 + sub, 1e-6)
            else:
                for sub in range(4):
                    tt = tg * 4 + sub
                    P.v(lambda e, sub=sub, tt=tt: e.tensor_copy(
                        ctx.v65.ap()[:, tt * 520: tt * 520 + 520]
                        .rearrange("p (h c) -> p h c", c=65)[:, :, 0:64],
                        banks[sub][:, 0:512].rearrange("p (h d) -> p h d", d=64)))
                P.a(lambda e, tg=tg: e.copy(ctx.w0T.ap()[:, tg * 512:(tg + 1) * 512],
                                            psPV[0:32, 0:512]))
                emit_bg(P, ctx, tg)

    emit_wchain(P, ctx)


def emit_bg(P, ctx, tg):
    big1, big2 = ctx.big1.ap(), ctx.big2.ap()
    psT = ctx.psT.ap()
    P.a(lambda e: e.activation(ctx.bgS.ap()[0:8, :], big1[0:8, 0:512], AF.Sigmoid,
                               bias=ctx.bgb.ap()[0:8, :]))
    P.v(lambda e: e.tensor_scalar_mul(ctx.bgB.ap()[0:8, :], ctx.bgS.ap()[0:8, :], 2.0))
    for sub in range(4):
        P.t(lambda e, sub=sub: e.transpose(big2[:, 0:8],
                                           ctx.bgB.ap()[0:8, sub * 128:(sub + 1) * 128],
                                           ctx.idf.ap()[0:8, 0:8]))
        P.a(lambda e, sub=sub, tg=tg: e.copy(
            ctx.bcol.ap()[:, (tg * 4 + sub) * 8:(tg * 4 + sub) * 8 + 8], big2[:, 0:8]))
    P.v(lambda e: e.tensor_scalar_add(ctx.bgB.ap()[32:40, :], big1[32:40, 0:512],
                                      ctx.bgb.ap()[32:40, :]))
    P.a(lambda e: e.activation(ctx.bgS.ap()[32:40, :], ctx.bgB.ap()[32:40, :],
                               AF.Relu, scale=-1.0))
    P.a(lambda e: e.activation(ctx.bgS2.ap()[32:40, :], ctx.bgB.ap()[32:40, :], AF.Abs))
    P.brk2()
    P.a(lambda e: e.activation(ctx.bgB.ap()[32:40, :], ctx.bgS2.ap()[32:40, :],
                               AF.Exp, scale=-1.0))
    P.brk2()
    P.a(lambda e: e.activation(ctx.bgS2.ap()[32:40, :], ctx.bgB.ap()[32:40, :],
                               AF.Ln, bias=1.0))
    P.v(lambda e: e.scalar_tensor_tensor(ctx.bgB.ap()[32:40, :],
                                         ctx.bgS2.ap()[32:40, :], -1.0,
                                         ctx.bgS.ap()[32:40, :], AL.mult, AL.subtract))
    for sub in range(4):
        P.t(lambda e, sub=sub: e.transpose(big2[:, 0:8],
                                           ctx.bgB.ap()[32:40, sub * 128:(sub + 1) * 128],
                                           ctx.idf.ap()[32:40, 32:40]))
        P.a(lambda e, sub=sub: e.copy(ctx.gtok.ap()[:, sub * 8:(sub + 1) * 8],
                                      big2[:, 0:8]))
    for sub in range(4):
        P.t(lambda e, sub=sub: e.matmul(big2[:, 16:24], ctx.trilf.ap(),
                                        ctx.gtok.ap()[:, sub * 8:(sub + 1) * 8],
                                        start=True, stop=False, skip_group_check=True))
        P.t(lambda e: e.matmul(big2[:, 16:24], ctx.onesrow1.ap(), ctx.gcarry.ap(),
                               start=False, stop=True, skip_group_check=True))
        P.a(lambda e, sub=sub: e.copy(ctx.Gtok.ap()[:, sub * 8:(sub + 1) * 8],
                                      big2[:, 16:24]))
        P.t(lambda e, sub=sub: e.matmul(big2[0:1, 24:32], ctx.onescol.ap(),
                                        ctx.gtok.ap()[:, sub * 8:(sub + 1) * 8],
                                        start=True, stop=False, skip_group_check=True))
        P.t(lambda e: e.matmul(big2[0:1, 24:32], ctx.onesrow1.ap()[:, 0:1],
                               ctx.gcarry.ap(), start=False, stop=True,
                               skip_group_check=True))
        P.v(lambda e: e.tensor_copy(ctx.gcarry.ap(), big2[0:1, 24:32]))
        P.brk()
    P.v(lambda e: e.tensor_scalar_mul(ctx.msplit.ap(), ctx.Gtok.ap(), -1.0))
    P.a(lambda e: e.copy(ctx.spl.ap()[:, 0:32], ctx.msplit.ap()))
    P.v(lambda e: e.tensor_copy(ctx.splf.ap(), ctx.spl.ap()[:, 0:32]))
    P.brk()
    P.v(lambda e: e.tensor_tensor(ctx.msplit.ap(), ctx.msplit.ap(), ctx.splf.ap(),
                                  AL.subtract))
    P.a(lambda e: e.copy(ctx.spl.ap()[:, 32:64], ctx.msplit.ap()))
    P.v(lambda e: e.tensor_copy(ctx.splf.ap(), ctx.spl.ap()[:, 32:64]))
    P.brk()
    P.v(lambda e: e.tensor_tensor(ctx.msplit.ap(), ctx.msplit.ap(), ctx.splf.ap(),
                                  AL.subtract))
    P.a(lambda e: e.copy(ctx.spl.ap()[:, 64:96], ctx.msplit.ap()))
    for sub in range(4):
        tt = tg * 4 + sub
        P.v(lambda e, sub=sub, tt=tt: e.tensor_copy(
            ctx.negG3_tm.ap()[:, tt * 24: tt * 24 + 24]
            .rearrange("p (h s) -> p h s", s=3),
            ctx.spl.ap().rearrange("p (s q) -> p s q", q=32)[:, :, sub * 8:(sub + 1) * 8]
            .rearrange("p s h -> p h s")))
    for sub in range(4):
        tt = tg * 4 + sub
        P.t(lambda e, tt=tt: e.transpose(ctx.psT.ap()[0:24, 0:128],
                                         ctx.negG3_tm.ap()[:, tt * 24: tt * 24 + 24],
                                         ctx.idb.ap()))
        P.a(lambda e: e.copy(ctx.stage24.ap(), ctx.psT.ap()[0:24, 0:128]))
        P.s(lambda e, tt=tt: e.dma_start(
            out=P.negG3_hbm.rearrange("h s t -> (h s) t")[:, tt * 128:(tt + 1) * 128],
            in_=ctx.stage24.ap()))


def emit_wchain(P, ctx):
    psA, big1, big2, psT = ctx.psA.ap(), ctx.big1.ap(), ctx.big2.ap(), ctx.psT.ap()
    for dt in range(4):
        P.v(lambda e: e.memset(ctx.wpre.ap()[:, 0:4], 0.0))
        for half in range(2):
            for tgh in range(4):
                P.t(lambda e, dt=dt, half=half, tgh=tgh: e.matmul(
                    psA[:, 0:512], ctx.w2t.ap()[:, dt * 128:(dt + 1) * 128],
                    ctx.w0T.ap()[:, half * 2048 + tgh * 512: half * 2048 + (tgh + 1) * 512],
                    start=True, stop=True, skip_group_check=True))
                P.a(lambda e, tgh=tgh: e.copy(
                    ctx.wpre.ap()[:, 4 + tgh * 512: 4 + (tgh + 1) * 512], psA[:, 0:512]))
            P.v(lambda e, dt=dt: e.tensor_scalar_mul(
                ctx.wconv.ap(), ctx.wpre.ap()[:, 1:2049], ctx.convs.ap()[:, dt * 4: dt * 4 + 1]))
            for k in range(1, 4):
                P.brk()
                P.v(lambda e, dt=dt, k=k: e.scalar_tensor_tensor(
                    ctx.wconv.ap(), ctx.wpre.ap()[:, 1 + k: 2049 + k],
                    ctx.convs.ap()[:, dt * 4 + k: dt * 4 + k + 1], ctx.wconv.ap(),
                    AL.mult, AL.add))
            P.brk()
            P.v(lambda e: e.tensor_copy(ctx.wpre.ap()[:, 0:4], ctx.wpre.ap()[:, 2048:2052]))
            P.a(lambda e: e.activation(ctx.wpre.ap()[:, 4:2052], ctx.wconv.ap(), AF.Square))
            for tgh in range(4):
                P.t(lambda e, tgh=tgh: e.matmul(
                    big1[0:2, 0:512], ctx.onesblk.ap(),
                    ctx.wpre.ap()[:, 4 + tgh * 512: 4 + (tgh + 1) * 512],
                    start=True, stop=True, skip_group_check=True))
                P.v(lambda e: e.tensor_scalar_add(ctx.nrm2.ap(), big1[0:2, 0:512], 1e-12))
                P.a(lambda e: e.activation(ctx.bgS.ap()[0:2, :], ctx.nrm2.ap(), AF.Ln))
                P.brk2()
                P.a(lambda e: e.activation(ctx.nrm2.ap(), ctx.bgS.ap()[0:2, :],
                                           AF.Exp, scale=-0.5))
                P.t(lambda e: e.matmul(big2[:, 0:512], ctx.onesblkT.ap(), ctx.nrm2.ap(),
                                       start=True, stop=True, skip_group_check=True))
                P.v(lambda e, tgh=tgh: e.tensor_tensor(
                    ctx.wpre.ap()[:, 4 + tgh * 512: 4 + (tgh + 1) * 512],
                    ctx.wconv.ap()[:, tgh * 512:(tgh + 1) * 512], big2[:, 0:512], AL.mult))
                for s4 in range(4):
                    tt = half * 16 + tgh * 4 + s4
                    P.t(lambda e, tgh=tgh, s4=s4: e.transpose(
                        psT[:, 0:128],
                        ctx.wpre.ap()[:, 4 + tgh * 512 + s4 * 128: 4 + tgh * 512 + (s4 + 1) * 128],
                        ctx.idb.ap()))
                    P.a(lambda e, tt=tt, dt=dt: e.copy(
                        ctx.w_tok.ap()[:, tt * 512 + dt * 128: tt * 512 + (dt + 1) * 128],
                        psT[:, 0:128]))


def emit_head(P, ctx, ch, h, dram, win=WIN):
    P.chain = ch.cid
    P.s(lambda e: e.dma_start(out=ch.Kd.ap()[64:67, :],
                              in_=dram["c_ones3T"][:, 0:T]))
    P.s(lambda e, h=h: e.dma_start(out=ch.Kd.ap()[67:70, :], in_=P.negG3_hbm[h]))
    P.s(lambda e: e.dma_start(out=ch.q70.ap()[67:70, :],
                              in_=dram["c_ones3T"][:, 0:256]))
    P.v(lambda e: e.memset(ch.q70.ap()[64:67, :], 0.0))


def emit_chunk(P, ctx, ch, h, c, win=WIN):
    P.chain = ch.cid
    t0 = c * CHUNK
    tt = c
    r = c % win
    wstart = (c // win) * win * CHUNK

    qs = ctx.q_tok.ap()[:, tt * 64 * H + 64 * h: tt * 64 * H + 64 * (h + 1)]
    ks = ctx.k_tok.ap()[:, tt * 64 * H + 64 * h: tt * 64 * H + 64 * (h + 1)]
    ws = ctx.w_tok.ap()[:, tt * 64 * H + 64 * h: tt * 64 * H + 64 * (h + 1)]
    bc = ctx.bcol.ap()[:, tt * H + h: tt * H + h + 1]
    gsl = ctx.negG3_tm.ap()[:, tt * 3 * H + 3 * h: tt * 3 * H + 3 * (h + 1)]

    seq, big, pvw, tp = ch.seq, ch.big, ch.pvw, ch.tp
    QbT = ch.QbT.ap()[0:64, 0:128]
    KbT = ch.QbT.ap()[0:64, 128:256]
    WbT = ch.QbT.ap()[0:64, 256:384]

    P.t(lambda e: e.transpose(tp[0:64, 0:128], qs, ctx.idb.ap()))
    P.t(lambda e: e.transpose(tp[0:64, 128:256], ks, ctx.idb.ap()))
    P.t(lambda e: e.transpose(tp[0:64, 256:384], ws, ctx.idb.ap()))
    P.t(lambda e: e.transpose(tp[0:3, 384:512], gsl, ctx.idb.ap()))
    P.a(lambda e: e.copy(ch.QbT.ap()[0:64, 0:384], tp[0:64, 0:384]))
    P.a(lambda e: e.copy(ch.negG3r.ap(), tp[0:3, 384:512]))
    P.a(lambda e: e.mul(ch.QbTs.ap(), QbT, SCALE))
    P.v(lambda e: e.tensor_scalar(ch.negbWb.ap(), ws, bc, -1.0, AL.mult, AL.mult))

    P.t(lambda e: e.matmul(seq[:, 0:128], WbT, WbT, start=True, stop=True,
                           skip_group_check=True))
    P.t(lambda e: e.matmul(seq[:, 128:256], WbT, KbT, start=True, stop=True,
                           skip_group_check=True))
    P.t(lambda e: e.matmul(seq[:, 256:384], WbT, QbT, start=True, stop=True,
                           skip_group_check=True))
    P.v(lambda e: e.scalar_tensor_tensor(ch.Xp.ap()[:, 0:128], seq[:, 0:128], bc,
                                         ctx.mask_slneg.ap(), AL.mult, AL.mult))
    P.v(lambda e: e.tensor_tensor(ch.X_u.ap(), seq[:, 0:128], ctx.mask_su.ap(),
                                  AL.mult))
    P.v(lambda e: e.tensor_tensor(ch.negsWK.ap(), seq[:, 128:256],
                                  ctx.mask_slneg.ap(), AL.mult))
    P.v(lambda e: e.tensor_tensor(ch.QWt_m.ap(), seq[:, 256:384],
                                  ctx.trilf.ap(), AL.mult))
    P.t(lambda e: e.transpose(seq[:, 384:512], ch.Xp.ap()[:, 0:128], ctx.idf.ap()))
    P.a(lambda e: e.copy(ch.Xp.ap()[:, 128:256], seq[:, 384:512]))

    P.v(lambda e: e.tensor_tensor(ch.Mp.ap()[:, 0:128], ch.Xp.ap()[:, 0:128],
                                  ctx.idf.ap(), AL.add))
    P.v(lambda e: e.tensor_tensor(ch.Mp.ap()[:, 128:256], ch.Xp.ap()[:, 128:256],
                                  ctx.idf.ap(), AL.add))
    for lev in range(6):
        P.t(lambda e: e.matmul(seq[:, 0:128], ch.Xp.ap()[:, 128:256],
                               ch.Xp.ap()[:, 0:128], start=True, stop=True,
                               skip_group_check=True))
        P.t(lambda e: e.matmul(seq[:, 128:256], ch.Xp.ap()[:, 0:128],
                               ch.Xp.ap()[:, 128:256], start=True, stop=True,
                               skip_group_check=True))
        P.a(lambda e: e.copy(ch.Xp.ap()[:, 0:256], seq[:, 0:256]))
        P.t(lambda e: e.matmul(seq[:, 256:384], ch.Mp.ap()[:, 128:256],
                               ch.Xp.ap()[:, 0:128], start=True, stop=True,
                               skip_group_check=True))
        P.t(lambda e: e.matmul(seq[:, 384:512], ch.Xp.ap()[:, 0:128],
                               ch.Mp.ap()[:, 128:256], start=True, stop=True,
                               skip_group_check=True))
        P.v(lambda e: e.tensor_tensor(ch.Mp.ap()[:, 0:256], ch.Mp.ap()[:, 0:256],
                                      seq[:, 256:512], AL.add))

    Minv = ch.Mp.ap()[:, 0:128]
    P.t(lambda e: e.matmul(seq[:, 0:128], Minv, ch.QWt_m.ap(), start=True,
                           stop=True, skip_group_check=True))
    P.a(lambda e: e.copy(ch.RT.ap(), seq[:, 0:128]))
    P.v(lambda e: e.tensor_scalar(ch.RbT.ap(), ch.RT.ap(), bc, None, AL.mult))
    P.a(lambda e: e.mul(ch.RbTs.ap(), ch.RbT.ap(), SCALE))
    P.t(lambda e: e.matmul(seq[0:64, 128:256], ch.negbWb.ap(), ch.RT.ap(),
                           start=True, stop=True, skip_group_check=True))
    P.v(lambda e: e.tensor_tensor(ch.QtT.ap(), seq[0:64, 128:256], QbT, AL.add))
    P.a(lambda e: e.mul(ch.q70.ap()[0:64, 128:256], ch.QtT.ap(), SCALE))
    if r == 0:
        P.a(lambda e: e.mul(ch.q70.ap()[0:64, 0:128], ch.QtT.ap(), SCALE))
    else:
        P.t(lambda e: e.matmul(seq[0:64, 256:384], ch.S.ap()[:, 0:64], ch.QtT.ap(),
                               start=True, stop=True, skip_group_check=True))
        P.a(lambda e: e.mul(ch.q70.ap()[0:64, 0:128], seq[0:64, 256:384], SCALE))

    groups = []
    for j0 in range(0, wstart, 512):
        groups.append((j0, min(512, wstart - j0), 0))
    if t0 > wstart:
        groups.append((wstart, t0 - wstart, 128))
    nmp = 0
    for (j0, wdt, voff) in groups:
        P.t(lambda e, j0=j0, wdt=wdt, voff=voff:
            e.matmul(big[:, 0:wdt], ch.q70.ap()[:, voff:voff + 128],
                     ch.Kd.ap()[:, j0:j0 + wdt], start=True, stop=True,
                     skip_group_check=True))
        P.v(lambda e, wdt=wdt, nmp=nmp:
            e.tensor_reduce(ch.mpart.ap()[:, nmp:nmp + 1], big[:, 0:wdt],
                            AX.X, AL.max))
        nmp += 1
    P.t(lambda e: e.matmul(big[:, 0:128], ch.QbTs.ap(), KbT, start=True,
                           stop=False, skip_group_check=True))
    P.t(lambda e: e.matmul(big[:, 0:128], ch.RbTs.ap(), ch.negsWK.ap(),
                           start=False, stop=False, skip_group_check=True))
    P.t(lambda e: e.matmul(big[:, 0:128], ctx.ones3.ap(), ch.negG3r.ap(),
                           start=False, stop=False, skip_group_check=True))
    P.t(lambda e: e.matmul(big[:, 0:128], ctx.maskA.ap(), ctx.idb.ap(),
                           start=False, stop=True, skip_group_check=True))
    P.v(lambda e, nmp=nmp: e.tensor_reduce(ch.mpart.ap()[:, nmp:nmp + 1],
                                           big[:, 0:128], AX.X, AL.max))
    nmp += 1
    P.brk()
    P.v(lambda e, nmp=nmp: e.tensor_reduce(ch.mcol.ap()[:, 0:1],
                                           ch.mpart.ap()[:, 0:nmp], AX.X, AL.max))
    P.a(lambda e: e.mul(ch.negm_sb.ap(), ch.mcol.ap()[:, 0:1], -1.0))
    P.t(lambda e: e.transpose(pvw[0:1, 128:256], ch.negm_sb.ap(), ctx.idf.ap()))
    P.a(lambda e: e.copy(ch.negmrow4.ap()[:, 0:128], pvw[0:1, 128:256]))
    for rr in range(1, 4):
        P.v(lambda e, rr=rr: e.tensor_copy(ch.negmrow4.ap()[:, rr * 128:(rr + 1) * 128],
                                           ch.negmrow4.ap()[:, 0:128]))

    n_jt = t0 // 128
    total_jt = n_jt + 1
    pv_count = [0]

    def emit_pv(jt, pslice):
        vsl = ctx.v65.ap()[:, jt * 65 * H + 65 * h: jt * 65 * H + 65 * h + 65]
        first = pv_count[0] == 0
        last = pv_count[0] == total_jt - 1
        P.t(lambda e, vsl=vsl, pslice=pslice, first=first, last=last: e.matmul(
            pvw[0:65, 0:128], vsl, pslice, start=first, stop=last,
            skip_group_check=True))
        pv_count[0] += 1

    pos = 0
    pbswap = 0
    while pos < n_jt:
        grp = []
        while pos < n_jt and len(grp) < 4:
            grp.append(pos)
            pos += 1
        for gi, jt in enumerate(grp):
            voff = 0 if (jt * 128) < wstart else 128
            P.t(lambda e, jt=jt, gi=gi, voff=voff:
                e.matmul(big[:, gi * 128:(gi + 1) * 128],
                         ch.Kd.ap()[:, jt * 128:(jt + 1) * 128],
                         ch.q70.ap()[:, voff:voff + 128],
                         start=(gi == 0), stop=False, skip_group_check=True))
        nc_ = len(grp) * 128
        P.t(lambda e, nc_=nc_: e.matmul(
            big[:, 0:nc_], ctx.onesrow1.ap(), ch.negmrow4.ap()[:, 0:nc_],
            start=False, stop=True, skip_group_check=True))
        pb = ch.P_buf.ap()[:, (pbswap % 2) * 512:(pbswap % 2) * 512 + nc_]
        P.a(lambda e, pb=pb, nc_=nc_: e.activation(pb, big[:, 0:nc_], AF.Exp))
        for gi, jt in enumerate(grp):
            emit_pv(jt, ch.P_buf.ap()[:, (pbswap % 2) * 512 + gi * 128:
                                      (pbswap % 2) * 512 + (gi + 1) * 128])
        pbswap += 1
    P.t(lambda e: e.matmul(big[:, 0:128], KbT, ch.QbTs.ap(), start=True,
                           stop=False, skip_group_check=True))
    P.t(lambda e: e.matmul(big[:, 0:128], ch.negsWK.ap(), ch.RbTs.ap(),
                           start=False, stop=False, skip_group_check=True))
    P.t(lambda e: e.matmul(big[:, 0:128], ch.negG3r.ap(), ctx.ones3.ap(),
                           start=False, stop=False, skip_group_check=True))
    P.t(lambda e: e.matmul(big[:, 0:128], ctx.onesrow1.ap(),
                           ch.negmrow4.ap()[:, 0:128], start=False, stop=False,
                           skip_group_check=True))
    P.t(lambda e: e.matmul(big[:, 0:128], ctx.maskB.ap(), ctx.idb.ap(),
                           start=False, stop=True, skip_group_check=True))
    P.a(lambda e: e.activation(ch.P_buf.ap()[:, 896:1024], big[:, 0:128], AF.Exp))
    emit_pv(n_jt, ch.P_buf.ap()[:, 896:1024])

    P.v(lambda e: e.reciprocal(ch.zinv.ap(), pvw[64:65, 0:128]))
    P.v(lambda e: e.tensor_copy(ch.ot_raw.ap(), pvw[0:64, 0:128]))
    P.t(lambda e: e.matmul(pvw[0:64, 256:384], ctx.ones1_64.ap(), ch.zinv.ap(),
                           start=True, stop=True, skip_group_check=True))
    P.a(lambda e: e.copy(ch.zrep.ap(), pvw[0:64, 256:384]))
    ob = ctx.oT.ap()[ch.cid * 64: ch.cid * 64 + 64, t0:t0 + CHUNK]
    P.v(lambda e, ob=ob: e.tensor_tensor(ob, ch.ot_raw.ap(), ch.zrep.ap(), AL.mult))

    P.t(lambda e: e.matmul(seq[:, 0:128], Minv, ch.X_u.ap(), start=True,
                           stop=True, skip_group_check=True))
    P.a(lambda e: e.copy(ch.RwT.ap(), seq[:, 0:128]))
    P.t(lambda e: e.matmul(seq[:, 128:192], ch.RwT.ap(), ch.negbWb.ap(),
                           start=True, stop=True, skip_group_check=True))
    P.v(lambda e: e.tensor_tensor(ch.V_sb.ap(), seq[:, 128:192], ws, AL.add))
    P.t(lambda e: e.matmul(seq[0:64, 192:256], ch.V_sb.ap(), ch.negbWb.ap(),
                           start=True, stop=True, skip_group_check=True))
    P.v(lambda e: e.tensor_tensor(ch.McT.ap()[0:64, 0:64], seq[0:64, 192:256],
                                  ctx.idf.ap()[0:64, 0:64], AL.add))
    P.v(lambda e: e.tensor_scalar(ch.bsWK.ap(), ch.negsWK.ap(), bc, -1.0,
                                  AL.mult, AL.mult))
    P.t(lambda e: e.matmul(seq[:, 256:384], ch.RwT.ap(), ch.bsWK.ap(),
                           start=True, stop=True, skip_group_check=True))
    P.v(lambda e: e.scalar_tensor_tensor(ch.D_sb.ap(), seq[:, 256:384], -1.0,
                                         ch.negsWK.ap(), AL.mult, AL.subtract))
    if t0 > wstart:
        wdt = t0 - wstart
        P.t(lambda e, wdt=wdt: e.matmul(pvw[0:64, 0:wdt], ch.McT.ap()[0:64, 0:64],
                                        ch.Kd.ap()[0:64, wstart:t0],
                                        start=True, stop=True, skip_group_check=True))
        P.a(lambda e, wdt=wdt: e.copy(ch.Kd.ap()[0:64, wstart:t0], pvw[0:64, 0:wdt]))
    P.t(lambda e: e.matmul(seq[0:64, 384:512], ch.negbWb.ap(), ch.D_sb.ap(),
                           start=True, stop=True, skip_group_check=True))
    P.v(lambda e: e.tensor_tensor(ch.Kd.ap()[0:64, t0:t0 + CHUNK],
                                  seq[0:64, 384:512], KbT, AL.add))
    if r == 0:
        P.t(lambda e: e.transpose(tp[0:64, 0:64], ch.McT.ap()[0:64, 0:64],
                                  ctx.idb.ap()[0:64, 0:64]))
        P.a(lambda e: e.copy(ch.S.ap()[:, 0:64], tp[0:64, 0:64]))
        P.a(lambda e: e.copy(ch.S.ap()[:, 64:128], ch.McT.ap()[0:64, 0:64]))
    else:
        P.t(lambda e: e.matmul(pvw[0:64, 384:448], ch.McT.ap()[0:64, 0:64],
                               ch.S.ap()[:, 0:64], start=True, stop=True,
                               skip_group_check=True))
        P.t(lambda e: e.matmul(pvw[0:64, 448:512], ch.S.ap()[:, 0:64],
                               ch.McT.ap()[0:64, 0:64], start=True, stop=True,
                               skip_group_check=True))
        P.a(lambda e: e.copy(ch.S.ap()[:, 0:128], pvw[0:64, 384:512]))
    if r == win - 1 and wstart > 0:
        for j0 in range(0, wstart, 512):
            wdt = min(512, wstart - j0)
            P.t(lambda e, j0=j0, wdt=wdt: e.matmul(
                pvw[0:64, 0:wdt], ch.S.ap()[:, 64:128],
                ch.Kd.ap()[0:64, j0:j0 + wdt], start=True, stop=True,
                skip_group_check=True))
            P.a(lambda e, j0=j0, wdt=wdt: e.copy(ch.Kd.ap()[0:64, j0:j0 + wdt],
                                                 pvw[0:64, 0:wdt]))


class BCtx:
    def __init__(self, nc, actx=None):
        self._guards = []
        if actx is not None:
            def A(name, shape, dt):
                g = nc.sbuf_tensor(name, shape, dt)
                h = g.__enter__()
                self._guards.append(g)
                return h
            AP_ = None
        else:
            A = nc.alloc_sbuf_tensor
            AP_ = nc.alloc_psum_tensor
        self._actx = actx
        self.moe = A("moe", [128, 8 * 2048], F32)
        self.x2T = A("x2T", [128, 16 * 1024], BF16)
        self.he = A("he", [128, 11 * 1024], BF16)
        self.gblk = A("gblk", [128, 2 * 16 * 256], BF16)
        self.ublk = A("ublk", [128, 2 * 16 * 256], BF16)
        self.dblk = A("dblk", [128, 2 * 11 * 512], BF16)
        self.otk = A("otk", [128, 2 * 2048], BF16)
        self.xbuf = A("xbuf", [128, 2048], F32)
        self.x2b = A("x2b", [128, 2048], BF16)
        self.crep = A("crep", [128, 1024], BF16)
        self.rwT = A("rwT", [128, 16 * 8], BF16)
        self.comb = A("comb", [128, 64], F32)
        self.combb = A("combb", [128, 64], BF16)
        self.combT = A("combT", [8, 1024], BF16)
        self.sel = A("sel", [8, 1024], BF16)
        self.sgb = A("sgb", [128, 22], F32)
        self.sub = A("sub", [128, 22], F32)
        if actx is not None:
            self.idb, self.idf = actx.idb, actx.idf
            self.onesrow1, self.junk, self.fvals = actx.onesrow1, actx.junk, actx.fvals
        else:
            self.idb = A("idb", [128, 128], BF16)
            self.idf = A("idf", [128, 128], F32)
            self.onesrow1 = A("onesrow1", [1, 128], F32)
            self.junk = A("junk", [1, 8], F32)
            self.fvals = A("fvals", [128, 8], F32)
            nc.const_aps.register(F32, 1e-6, self.fvals.ap()[:, 0:1])
            nc.const_aps.register(F32, 1.0, self.fvals.ap()[:, 1:2])
            nc.const_aps.register(F32, 1e-5, self.fvals.ap()[:, 2:3])
            nc.const_aps.register(F32, 1e-12, self.fvals.ap()[:, 3:4])
        self.r1 = A("r1", [128, 1], F32)
        self.r2 = A("r2", [128, 1], F32)
        self.r3 = A("r3", [128, 1], F32)
        self.r4 = A("r4", [128, 1], F32)
        self.r5 = A("r5", [128, 8], F32)
        self.r6 = A("r6", [128, 8], F32)
        self.sg_s = A("sg_s", [128, 512], BF16)
        self.uc_s = A("uc_s", [128, 512], BF16)
        if actx is not None:
            self.ps = [actx.psA, actx.psM, actx.psQ, actx.big1, actx.big2,
                       actx.psPV, actx.psA]
            self.psT = actx.psT
        else:
            self.ps = [AP_(f"ps{i}", [128, 512], F32) for i in range(7)]
            self.psT = AP_("psTb", [128, 1024], BF16)


def emit_ln(P, ctx, src_ap, tile_cols, out_ap):
    P.v(lambda e: e.tensor_reduce(ctx.r1.ap(), src_ap, AX.X, AL.add))
    P.a(lambda e: e.activation(ctx.xbuf.ap(), src_ap, AF.Square))
    P.v(lambda e: e.tensor_reduce(ctx.r2.ap(), ctx.xbuf.ap(), AX.X, AL.add))
    P.a(lambda e: e.mul(ctx.r3.ap(), ctx.r1.ap(), 1.0 / 2048))
    P.brk2()
    P.a(lambda e: e.activation(ctx.r1.ap(), ctx.r3.ap(), AF.Square))
    P.v(lambda e: e.scalar_tensor_tensor(ctx.r2.ap(), ctx.r2.ap(), 1.0 / 2048,
                                         ctx.r1.ap(), AL.mult, AL.subtract))
    P.a(lambda e: e.activation(ctx.r4.ap(), ctx.r2.ap(), AF.Ln, bias=1e-5))
    P.brk2()
    P.a(lambda e: e.activation(ctx.r2.ap(), ctx.r4.ap(), AF.Exp, scale=-0.5))
    P.a(lambda e: e.mul(ctx.r4.ap(), ctx.r3.ap(), -1.0))
    P.v(lambda e: e.tensor_scalar(out_ap, src_ap, ctx.r4.ap(), ctx.r2.ap(),
                                  AL.add, AL.mult))


def emit_launchB(P, ctx, dram, fused=False):
    nc = P.nc
    ps = [p.ap() for p in ctx.ps]
    psT = ctx.psT.ap()

    def otk_load(e, tt, par):
        dst = ctx.otk.ap()[:, par * 2048:(par + 1) * 2048] \
            .rearrange("p (k n) -> p k n", n=128)
        if fused:
            if "tq" not in P.regs:
                pid = e.partition_id()
                tq = e.scalar_reg_alu(AL.mod, pid, 4)
                P.regs["tq"] = e.scalar_reg_alu(AL.mult, tq, TQ)
            src3 = dram["oTq"].rearrange("(k p) t -> p k t", p=128)
            return e.dma_start(out=dst,
                               in_=src3[:, :, bass.ds(P.regs["tq"] + tt * 128, 128)])
        return e.dma_start(out=dst, in_=dram["oTq"][:, tt * 128:(tt + 1) * 128]
                           .rearrange("(k p) n -> p k n", p=128))

    n_rwT = P.gd(lambda e: e.dma_start(
        out=ctx.rwT.ap().rearrange("p (k n) -> p k n", n=8),
        in_=dram["rwT"].rearrange("(k p) n -> p k n", p=128)))
    P.s(lambda e: e.dma_start(out=ctx.sgb.ap(), in_=dram["sgb"]))
    P.s(lambda e: e.dma_start(out=ctx.sub.ap(), in_=dram["sub"]))
    P.s(lambda e: e.dma_start(out=ctx.idb.ap(), in_=dram["c_idb"]))
    P.s(lambda e: e.dma_start(out=ctx.idf.ap(), in_=dram["c_idf"]))
    P.s(lambda e: e.dma_start(out=ctx.onesrow1.ap(), in_=dram["c_onesrow1"]))
    P.s(lambda e: e.dma_start(out=ctx.fvals.ap(), in_=dram["c_fvals"]))
    P.s(lambda e: e.dma_start(out=ctx.sel.ap(), in_=dram["sel"]))

    wo_slab = [ctx.gblk.ap()[:, 0:8192], ctx.ublk.ap()[:, 0:8192]]
    wo_ord = {}
    for og in range(2):
        wo_ord[og] = P.gd(lambda e, og=og: e.dma_start(
            out=wo_slab[og % 2].rearrange("p (k n) -> p k n", n=512),
            in_=dram["WoT"][:, og * 512:(og + 1) * 512]
            .rearrange("(k p) n -> p k n", p=128)))
    cc_tok = getattr(P, "cc_tok", None)
    otk_ord = {}
    otk_ord[0] = P.gd(lambda e: otk_load(e, 0, 0), wait=cc_tok)
    wo_tok = {}
    otk_tok = {}
    for og in range(4):
        for tt in range(8):
            bk = ps[tt % 2]
            idx = og * 8 + tt
            for kt in range(16):
                def mmfn(e, kt=kt, bk=bk, og=og, idx=idx):
                    if kt == 0:
                        P.gwait(wo_ord[og])(e)
                        P.gwait(otk_ord[idx])(e)
                    return e.matmul(
                        bk[:, 0:512],
                        ctx.otk.ap()[:, (idx % 2) * 2048 + kt * 128:
                                     (idx % 2) * 2048 + (kt + 1) * 128],
                        wo_slab[og % 2][:, kt * 512:(kt + 1) * 512],
                        start=(kt == 0), stop=(kt == 15), skip_group_check=True)
                P.t(mmfn)
            P.a(lambda e, tt=tt, og=og, bk=bk: e.copy(
                ctx.moe.ap()[:, tt * 2048 + og * 512: tt * 2048 + (og + 1) * 512],
                bk[:, 0:512]))
            otk_tok[idx] = P.token()
            nxt = idx + 1
            if nxt < 32:
                wt = otk_tok.get(nxt - 2) or cc_tok
                otk_ord[nxt] = P.gd(
                    lambda e, nxt=nxt: otk_load(e, nxt % 8, nxt % 2), wait=wt)
        wo_tok[og] = otk_tok[og * 8 + 7]
        if og + 2 < 4:
            wo_ord[og + 2] = P.gd(lambda e, og=og: e.dma_start(
                out=wo_slab[og % 2].rearrange("p (k n) -> p k n", n=512),
                in_=dram["WoT"][:, (og + 2) * 512:(og + 3) * 512]
                .rearrange("(k p) n -> p k n", p=128)), wait=wo_tok[og])

    for tt in range(8):
        mt = ctx.moe.ap()[:, tt * 2048:(tt + 1) * 2048]
        P.s(lambda e, tt=tt: e.dma_start(out=ctx.xbuf.ap(),
                                         in_=dram["x_tok"][tt * 128:(tt + 1) * 128, :]))
        P.v(lambda e, mt=mt: e.tensor_tensor(mt, mt, ctx.xbuf.ap(), AL.add))
        P.brk()
        emit_ln(P, ctx, mt, 2048, ctx.xbuf.ap())
        P.brk()
        P.v(lambda e: e.tensor_copy(ctx.x2b.ap(), ctx.xbuf.ap()))
        P.a(lambda e, mt=mt: e.copy(mt, ctx.xbuf.ap()))
        for kt in range(16):
            P.t(lambda e, kt=kt: e.transpose(psT[:, 0:128],
                                             ctx.x2b.ap()[:, kt * 128:(kt + 1) * 128],
                                             ctx.idb.ap()))
            P.a(lambda e, kt=kt, tt=tt: e.copy(
                ctx.x2T.ap()[:, kt * 1024 + tt * 128: kt * 1024 + (tt + 1) * 128],
                psT[:, 0:128]))

    for tt in range(8):
        bk = ps[2]
        for kt in range(16):
            def rfn(e, kt=kt, tt=tt, bk=bk):
                if kt == 0 and tt == 0:
                    P.gwait(n_rwT)(e)
                return e.matmul(
                    bk[:, 0:8],
                    ctx.x2T.ap()[:, kt * 1024 + tt * 128: kt * 1024 + (tt + 1) * 128],
                    ctx.rwT.ap()[:, kt * 8:(kt + 1) * 8],
                    start=(kt == 0), stop=(kt == 15), skip_group_check=True)
            P.t(rfn)
        P.v(lambda e, bk=bk: e.tensor_reduce(ctx.r1.ap(), bk[:, 0:8], AX.X, AL.max))
        P.a(lambda e: e.mul(ctx.r2.ap(), ctx.r1.ap(), -1.0))
        P.brk2()
        P.a(lambda e, bk=bk: e.activation(ctx.r5.ap(), bk[:, 0:8], AF.Exp,
                                          bias=ctx.r2.ap()))
        P.brk()
        P.v(lambda e: e.tensor_reduce(ctx.r1.ap(), ctx.r5.ap(), AX.X, AL.add))
        P.brk()
        P.v(lambda e: e.reciprocal(ctx.r2.ap(), ctx.r1.ap()))
        P.brk()
        P.v(lambda e: e.tensor_scalar(ctx.r5.ap(), ctx.r5.ap(), ctx.r2.ap(), None,
                                      AL.mult))
        P.brk()
        P.v(lambda e: e.tensor_reduce(ctx.r1.ap(), ctx.r5.ap(), AX.X, AL.max))
        P.brk()
        P.v(lambda e: e.tensor_scalar(ctx.r6.ap(), ctx.r5.ap(), ctx.r1.ap(), None,
                                      AL.is_ge))
        P.brk()
        P.v(lambda e: e.scalar_tensor_tensor(ctx.r6.ap(), ctx.r6.ap(), -2.0,
                                             ctx.r5.ap(), AL.mult, AL.add))
        P.brk()
        P.v(lambda e: e.tensor_reduce(ctx.r2.ap(), ctx.r6.ap(), AX.X, AL.max))
        P.brk()
        P.v(lambda e: e.tensor_tensor(ctx.r3.ap(), ctx.r1.ap(), ctx.r2.ap(), AL.add))
        P.brk()
        P.v(lambda e: e.reciprocal(ctx.r4.ap(), ctx.r3.ap()))
        P.v(lambda e: e.tensor_scalar(ctx.r6.ap(), ctx.r5.ap(), ctx.r2.ap(), None,
                                      AL.is_ge))
        P.brk()
        P.v(lambda e, tt=tt: e.scalar_tensor_tensor(
            ctx.comb.ap()[:, tt * 8:(tt + 1) * 8], ctx.r5.ap(), 1.0, ctx.r6.ap(),
            AL.mult, AL.mult))
        P.brk()
        P.v(lambda e, tt=tt: e.tensor_scalar(ctx.comb.ap()[:, tt * 8:(tt + 1) * 8],
                                             ctx.comb.ap()[:, tt * 8:(tt + 1) * 8],
                                             ctx.r4.ap(), None, AL.mult))
        P.brk()
        P.a(lambda e, tt=tt: e.copy(ctx.combb.ap()[:, tt * 8:(tt + 1) * 8],
                                    ctx.comb.ap()[:, tt * 8:(tt + 1) * 8]))
        P.t(lambda e, tt=tt: e.transpose(psT[0:8, 0:128],
                                         ctx.combb.ap()[:, tt * 8:(tt + 1) * 8],
                                         ctx.idb.ap()))
        P.a(lambda e, tt=tt: e.copy(ctx.combT.ap()[:, tt * 128:(tt + 1) * 128],
                                    psT[0:8, 0:128]))

    ex_blocks = [(0, 2), (2, 2), (4, 2), (6, 2), (8, 2), (10, 1)]
    sh_blocks = [(0, 2), (2, 2), (4, 2), (6, 2), (8, 2), (10, 1),
                 (11, 2), (13, 2), (15, 2), (17, 2), (19, 2), (21, 1)]
    items = []
    for ex in range(E):
        for (it0, itn) in ex_blocks:
            items.append(("gu", ("gate_w", ex), ("up_w", ex), it0, itn))
        for og in range(4):
            items.append(("d", ("down_w", ex), 0, og))
    for bi, (it0, itn) in enumerate(sh_blocks):
        items.append(("gu", ("s_gate_w", None), ("s_up_w", None), it0, itn))
        if (it0 + itn) in (11, 22):
            halfE = 0 if (it0 + itn) == 11 else 1
            for og in range(4):
                items.append(("d", ("s_down_w", None), halfE * 11 * 128, og))

    def wsrc(spec):
        name, ex = spec
        return dram[name][ex] if ex is not None else dram[name]

    def load_item(e, it):
        kind = it[0]
        if kind == "gu2":
            pass
        if kind == "gu":
            _, gsrc, usrc, it0, itn = it
            return None
        return None

    ords = {}
    toks = {}

    def emit_load(i, wait):
        it = items[i]
        if it[0] == "gu":
            _, gsrc, usrc, it0, itn = it
            par = i % 2
            o1 = P.gd(lambda e, gsrc=gsrc, it0=it0, itn=itn, par=par: e.dma_start(
                out=ctx.gblk.ap()[:, par * 4096: par * 4096 + 16 * itn * 128]
                .rearrange("p (k n) -> p k n", n=itn * 128),
                in_=wsrc(gsrc)[:, it0 * 128:(it0 + itn) * 128]
                .rearrange("(k p) n -> p k n", p=128)), wait=wait)
            o2 = P.gd(lambda e, usrc=usrc, it0=it0, itn=itn, par=par: e.dma_start(
                out=ctx.ublk.ap()[:, par * 4096: par * 4096 + 16 * itn * 128]
                .rearrange("p (k n) -> p k n", n=itn * 128),
                in_=wsrc(usrc)[:, it0 * 128:(it0 + itn) * 128]
                .rearrange("(k p) n -> p k n", p=128)), wait=None)
            ords[i] = o2
        else:
            _, dsrc, rows0, og = it
            par = i % 2
            ords[i] = P.gd(lambda e, dsrc=dsrc, rows0=rows0, og=og, par=par:
                           e.dma_start(
                out=ctx.dblk.ap()[:, par * 5632: par * 5632 + 11 * 512]
                .rearrange("p (k n) -> p k n", n=512),
                in_=wsrc(dsrc)[rows0: rows0 + 11 * 128, og * 512:(og + 1) * 512]
                .rearrange("(k p) n -> p k n", p=128)), wait=wait)

    emit_load(0, wo_tok[2])
    emit_load(1, wo_tok[3])
    crep_done = -1
    for i, it in enumerate(items):
        par = i % 2
        if it[0] == "gu":
            _, gsrc, usrc, it0, itn = it
            shared = gsrc[1] is None
            ex = gsrc[1]
            if not shared and ex != crep_done and it0 == 0:
                for half in range(2):
                    P.t(lambda e, ex=ex, half=half: e.matmul(
                        ps[6][:, 0:512], ctx.sel.ap()[:, ex * 128:(ex + 1) * 128],
                        ctx.combT.ap()[:, half * 512:(half + 1) * 512],
                        start=True, stop=True, skip_group_check=True))
                    P.a(lambda e, half=half: e.copy(
                        ctx.crep.ap()[:, half * 512:(half + 1) * 512],
                        ps[6][:, 0:512]))
                crep_done = ex
            for itl in range(itn):
                itg = it0 + itl
                ith = itg % 11
                for tg2 in range(2):
                    bkg, bku = ps[(itl + tg2) % 2], ps[2 + (itl + tg2) % 2]
                    for kt in range(16):
                        def gfn(e, kt=kt, itl=itl, itn=itn, tg2=tg2, bkg=bkg,
                                par=par, i=i):
                            if kt == 0:
                                P.gwait(ords[i])(e)
                            return e.matmul(
                                bkg[:, 0:512],
                                ctx.gblk.ap()[:, par * 4096 + kt * itn * 128 + itl * 128:
                                              par * 4096 + kt * itn * 128 + (itl + 1) * 128],
                                ctx.x2T.ap()[:, kt * 1024 + tg2 * 512:
                                             kt * 1024 + (tg2 + 1) * 512],
                                start=(kt == 0), stop=(kt == 15),
                                skip_group_check=True)
                        P.t(gfn)
                        P.t(lambda e, kt=kt, itl=itl, itn=itn, tg2=tg2, bku=bku,
                            par=par: e.matmul(
                            bku[:, 0:512],
                            ctx.ublk.ap()[:, par * 4096 + kt * itn * 128 + itl * 128:
                                          par * 4096 + kt * itn * 128 + (itl + 1) * 128],
                            ctx.x2T.ap()[:, kt * 1024 + tg2 * 512:
                                         kt * 1024 + (tg2 + 1) * 512],
                            start=(kt == 0), stop=(kt == 15), skip_group_check=True))
                    if shared:
                        P.a(lambda e, bkg=bkg, itg=itg: e.activation(
                            ctx.sg_s.ap(), bkg[:, 0:512], AF.Silu,
                            bias=ctx.sgb.ap()[:, itg:itg + 1]))
                        P.v(lambda e, bku=bku, itg=itg: e.tensor_scalar_add(
                            ctx.uc_s.ap(), bku[:, 0:512], ctx.sub.ap()[:, itg:itg + 1]))
                    else:
                        P.a(lambda e, bkg=bkg: e.activation(ctx.sg_s.ap(),
                                                            bkg[:, 0:512], AF.Silu))
                        P.v(lambda e, bku=bku, tg2=tg2: e.tensor_tensor(
                            ctx.uc_s.ap(), bku[:, 0:512],
                            ctx.crep.ap()[:, tg2 * 512:(tg2 + 1) * 512], AL.mult))
                    P.brk()
                    P.v(lambda e, ith=ith, tg2=tg2: e.tensor_tensor(
                        ctx.he.ap()[:, ith * 1024 + tg2 * 512:
                                    ith * 1024 + (tg2 + 1) * 512],
                        ctx.sg_s.ap(), ctx.uc_s.ap(), AL.mult))
            toks[i] = P.token()
        else:
            _, dsrc, rows0, og = it
            for tt in range(8):
                bk = ps[4 + tt % 2]
                for kt in range(11):
                    def dfn(e, kt=kt, tt=tt, bk=bk, par=par, i=i):
                        if kt == 0 and tt == 0:
                            P.gwait(ords[i])(e)
                        return e.matmul(
                            bk[:, 0:512],
                            ctx.he.ap()[:, kt * 1024 + tt * 128:
                                        kt * 1024 + tt * 128 + 128],
                            ctx.dblk.ap()[:, par * 5632 + kt * 512:
                                          par * 5632 + (kt + 1) * 512],
                            start=(kt == 0), stop=(kt == 10),
                            skip_group_check=True)
                    P.t(dfn)
                P.v(lambda e, tt=tt, og=og, bk=bk: e.tensor_tensor(
                    ctx.moe.ap()[:, tt * 2048 + og * 512: tt * 2048 + (og + 1) * 512],
                    ctx.moe.ap()[:, tt * 2048 + og * 512: tt * 2048 + (og + 1) * 512],
                    bk[:, 0:512], AL.add))
            toks[i] = P.token()
        if i + 2 < len(items):
            emit_load(i + 2, toks.get(i))

    P.s(lambda e: e.dma_start(out=ctx.xbuf.ap()[0:1, :], in_=dram["sdb"]))
    for og in range(4):
        P.t(lambda e, og=og: e.matmul(ps[6][:, 0:512], ctx.onesrow1.ap(),
                                      ctx.xbuf.ap()[0:1, og * 512:(og + 1) * 512],
                                      start=True, stop=True, skip_group_check=True))
        for tt in range(8):
            P.v(lambda e, tt=tt, og=og: e.tensor_tensor(
                ctx.moe.ap()[:, tt * 2048 + og * 512: tt * 2048 + (og + 1) * 512],
                ctx.moe.ap()[:, tt * 2048 + og * 512: tt * 2048 + (og + 1) * 512],
                ps[6][:, 0:512], AL.add))
    for tt in range(8):
        mt = ctx.moe.ap()[:, tt * 2048:(tt + 1) * 2048]
        emit_ln(P, ctx, mt, 2048, ctx.xbuf.ap())
        P.s(lambda e, tt=tt: e.dma_start(out=P.out_q[tt * 128:(tt + 1) * 128, :],
                                         in_=ctx.xbuf.ap()))
        P.brk()


def build_launchB():
    nc = bass.Bass("TRN2", target_bir_lowering=False)
    dram = {}
    for name in ["c_idb", "c_idf", "c_onesrow1", "c_fvals"]:
        arr = CONST_SPECS[name]
        dt = BF16 if arr.dtype == BF else F32
        dram[name] = nc.dram_tensor(name, list(arr.shape), dt, kind="ExternalInput").ap()
    spec = [("oTq", [D, TQ], BF16), ("x_tok", [TQ, D], F32), ("WoT", [D, D], F32),
            ("rwT", [D, 8], F32), ("sel", [8, 1024], BF16),
            ("s_gate_w", [D, SI], F32), ("s_up_w", [D, SI], F32),
            ("s_down_w", [SI, D], F32), ("sgb", [128, 22], F32),
            ("sub", [128, 22], F32), ("sdb", [1, 2048], F32)]
    for name, shape, dt in spec:
        dram[name] = nc.dram_tensor("d_" + name, shape, dt, kind="ExternalInput").ap()
    dram["gate_w"] = nc.dram_tensor("d_gate_w", [E, D, I_FF], F32, kind="ExternalInput").ap()
    dram["up_w"] = nc.dram_tensor("d_up_w", [E, D, I_FF], F32, kind="ExternalInput").ap()
    dram["down_w"] = nc.dram_tensor("d_down_w", [E, I_FF, D], F32, kind="ExternalInput").ap()
    out_q = nc.dram_tensor("out_q", [TQ, D], F32, kind="ExternalOutput").ap()

    ctx = BCtx(nc)
    with nc.Block() as block:
        P = Prog(nc, nbar=8)
        P._junk = ctx.junk.ap()
        P.out_q = out_q
        emit_launchB(P, ctx, dram)
        P.v(lambda e: e.engine_nop())
        P.lower(block)
    return nc


def build_fused():
    nc = bass.Bass("TRN2", target_bir_lowering=False)
    dram = {}
    for name, arr in CONST_SPECS.items():
        dt = BF16 if arr.dtype == BF else F32
        dram[name] = nc.dram_tensor(name, list(arr.shape), dt, kind="ExternalInput").ap()
    insA = [("xT", [D, T], F32), ("WqT", [D, 512], F32), ("WkT", [D, 512], F32),
            ("WvT", [D, 512], F32), ("W1T", [D, 32], F32), ("W2T", [32, 512], F32),
            ("bgT", [D, 64], F32), ("bgb", [64, 1], F32), ("convs", [128, 16], F32),
            ("qgrow", [1, 512], F32), ("kgrow", [1, 512], F32)]
    insB = [("x_tok", [TQ, D], F32), ("WoT", [D, D], F32),
            ("rwT", [D, 8], F32), ("sel", [8, 1024], BF16),
            ("s_gate_w", [D, SI], F32), ("s_up_w", [D, SI], F32),
            ("s_down_w", [SI, D], F32), ("sgb", [128, 22], F32),
            ("sub", [128, 22], F32), ("sdb", [1, 2048], F32),
            ("gate_w", [E, D, I_FF], F32), ("up_w", [E, D, I_FF], F32),
            ("down_w", [E, I_FF, D], F32)]
    for name, shape, dt in insA + insB:
        dram[name] = nc.dram_tensor("d_" + name, shape, dt, kind="ExternalInput").ap()
    out_q = nc.dram_tensor("out_q", [TQ, D], F32, kind="ExternalOutput").ap()
    negG3_hbm = nc.dram_tensor("negG3_scr", [H, 3, T], BF16, kind="Internal").ap()
    cc_in = nc.dram_tensor("cc_in", [512, T], BF16, kind="Internal").ap()
    cc_out = nc.dram_tensor("cc_out", [D, T], BF16, kind="Internal").ap()

    ctx = ACtx(nc, guarded=True)
    with nc.Block() as block:
        P = Prog(nc, nbar=8)
        P._junk = ctx.junk.ap()
        P.negG3_hbm = negG3_hbm
        P.out_q = out_q
        emit_load_consts(P, ctx, dram)
        emit_prep(P, ctx, dram)
        tokP = P.token()
        release_guards(ctx)
        ctx.alloc_attn()
        P.join(1, tokP)
        for pair in range(4):
            hA, hB = 2 * pair, 2 * pair + 1
            emit_head(P, ctx, ctx.ch[0], hA, dram)
            emit_head(P, ctx, ctx.ch[1], hB, dram)
            for c in range(NCH):
                P.capture_begin()
                emit_chunk(P, ctx, ctx.ch[0], hA, c)
                LA = P.capture_end()
                P.capture_begin()
                emit_chunk(P, ctx, ctx.ch[1], hB, c)
                LB = P.capture_end()
                P.interleave(LA, LB)
            P.chain = 0
            P.s(lambda e, hA=hA: e.dma_start(
                out=cc_in[hA * 64: hA * 64 + 64, :], in_=ctx.oT.ap()[0:64, :]))
            P.chain = 1
            P.s(lambda e, hB=hB: e.dma_start(
                out=cc_in[hB * 64: hB * 64 + 64, :], in_=ctx.oT.ap()[64:128, :]))
        tok1 = P.token()
        P.chain = 0
        P.join(0, tok1)
        ctx.release_attn()
        ctxB = BCtx(nc, actx=ctx)
        P.cc(lambda e: e.collective_compute(
            "AllGather", AL.bypass, [[0, 1, 2, 3], [4, 5, 6, 7]],
            [cc_in], [cc_out]))
        P.cc_tok = P.token()
        emit_launchB(P, ctxB, dict(dram, oTq=cc_out), fused=True)
        P.v(lambda e: e.engine_nop())
        P.lower(block)
    return nc


def kernel(**inputs):
    consts = host_consts()
    x = np.asarray(inputs["L0_attn_hidden_states"], dtype=np.float32)

    nc = build_fused()
    selc = np.zeros((8, 1024), np.float32)
    for ex in range(8):
        selc[ex, ex * 128:(ex + 1) * 128] = 1.0
    gate_w = np.ascontiguousarray(np.asarray(inputs["L0_ffn_gate_w"], np.float32))
    up_w = np.ascontiguousarray(np.asarray(inputs["L0_ffn_up_w"], np.float32))
    down_w = np.ascontiguousarray(np.asarray(inputs["L0_ffn_down_w"], np.float32))
    s_gate_w = np.ascontiguousarray(np.asarray(inputs["L0_ffn_s_gate_w"], np.float32))
    s_up_w = np.ascontiguousarray(np.asarray(inputs["L0_ffn_s_up_w"], np.float32))
    s_down_w = np.ascontiguousarray(np.asarray(inputs["L0_ffn_s_down_w"], np.float32))
    WoT = np.ascontiguousarray(np.asarray(inputs["L0_attn_Wo"]).T.astype(np.float32))
    rwT = np.ascontiguousarray(np.asarray(inputs["L0_ffn_router_w"]).T.astype(np.float32))
    W1T = np.ascontiguousarray(np.asarray(inputs["L0_attn_W1"]).T.astype(np.float32))
    sgb_l = np.ascontiguousarray(
        np.asarray(inputs["L0_ffn_s_gate_b"], np.float32).reshape(22, 128).T)
    sub_l = np.ascontiguousarray(
        np.asarray(inputs["L0_ffn_s_up_b"], np.float32).reshape(22, 128).T)
    sdb_l = np.asarray(inputs["L0_ffn_s_down_b"], np.float32)[None, :]
    xT = [np.ascontiguousarray(x[b].T) for b in range(B)]

    in_maps = []
    for c in range(N_CORES):
        b = c // 4
        hg = c % 4
        tq = c % 4
        hs = slice(hg * 8 * HD, (hg + 1) * 8 * HD)
        hh = slice(hg * 8, (hg + 1) * 8)
        tsl = slice(tq * TQ, (tq + 1) * TQ)
        m = dict(consts)
        m["d_xT"] = xT[b]
        m["d_WqT"] = np.ascontiguousarray(np.asarray(inputs["L0_attn_Wq"])[hs].T.astype(np.float32))
        m["d_WkT"] = np.ascontiguousarray(np.asarray(inputs["L0_attn_Wk"])[hs].T.astype(np.float32))
        m["d_WvT"] = np.ascontiguousarray(np.asarray(inputs["L0_attn_Wv"])[hs].T.astype(np.float32))
        m["d_W1T"] = W1T
        m["d_W2T"] = np.ascontiguousarray(np.asarray(inputs["L0_attn_W2"]).T[:, hs].astype(np.float32))
        bg = np.zeros((D, 64), np.float32)
        bg[:, 0:8] = np.asarray(inputs["L0_attn_bt_w"])[hh].T
        bg[:, 32:40] = np.asarray(inputs["L0_attn_g_w"])[hh].T
        m["d_bgT"] = bg
        bgb = np.zeros((64, 1), np.float32)
        bgb[0:8, 0] = np.asarray(inputs["L0_attn_bt_b"])[hh]
        bgb[32:40, 0] = np.asarray(inputs["L0_attn_g_b"])[hh]
        m["d_bgb"] = bgb
        cw = np.asarray(inputs["L0_attn_conv_w"])[hs]
        m["d_convs"] = np.ascontiguousarray(
            cw.reshape(4, 128, 4).transpose(1, 0, 2).reshape(128, 16)).astype(np.float32)
        m["d_qgrow"] = np.tile(np.asarray(inputs["L0_attn_q_norm_w"], np.float32), 8)[None, :]
        m["d_kgrow"] = np.tile(np.asarray(inputs["L0_attn_k_norm_w"], np.float32), 8)[None, :]
        m["d_x_tok"] = np.ascontiguousarray(x[b][tsl])
        m["d_WoT"] = WoT
        m["d_rwT"] = rwT
        m["d_sel"] = selc.astype(BF)
        m["d_gate_w"] = gate_w
        m["d_up_w"] = up_w
        m["d_down_w"] = down_w
        m["d_s_gate_w"] = s_gate_w
        m["d_s_up_w"] = s_up_w
        m["d_s_down_w"] = s_down_w
        m["d_sgb"] = sgb_l
        m["d_sub"] = sub_l
        m["d_sdb"] = sdb_l
        in_maps.append(m)

    res = run_bass_kernel_spmd(nc, in_maps, core_ids=list(range(N_CORES)))
    out = np.zeros((B, T, D), dtype=np.float32)
    for c in range(N_CORES):
        b = c // 4
        tq = c % 4
        out[b, tq * TQ:(tq + 1) * TQ] = res.results[c]["out_q"]
    return out


# version 09
# speedup vs baseline: 1.8840x; 1.1463x over previous
import numpy as np
import ml_dtypes
import concourse.bass as bass
import concourse.mybir as mybir
from concourse.bass_utils import run_bass_kernel_spmd

F32 = mybir.dt.float32
BF16 = mybir.dt.bfloat16
AF = mybir.ActivationFunctionType
AL = mybir.AluOpType
AX = mybir.AxisListType
BF = ml_dtypes.bfloat16

BIGNEG = -8.0e37
DMA_ENGS = ("sync", "gpsimd")

B, T, D, H_ALL, HD = 2, 4096, 2048, 32, 64
H = 8
NT = T // 128
CHUNK = 128
NCH = T // CHUNK
WIN = 4
SCALE = HD ** (-0.5)
E, I_FF, SI = 8, 1408, 2816
TQ = 1024
N_CORES = 8


class Prog:
    ENGS = ("tensor", "vector", "scalar", "gpsimd", "sync")

    def __init__(self, nc, nbar=8):
        self.nc = nc
        self.steps = []
        self.nbar = nbar
        self._junk = None
        self.regs = {}
        self._tok_requests = []
        self._gd_count = 0
        self.gsem = None
        self.chain = 0
        self.nchains = 2
        self._cap = None
        self._sems = None
        self._ccsem = None

    class Tok:
        __slots__ = ("chain", "slot", "val")
        def __init__(self):
            self.chain = None
            self.slot = None
            self.val = None

    def token(self):
        t = Prog.Tok()
        self._tok_requests.append((len(self.steps) - 1, t))
        return t

    def join(self, chain, tok):
        def fn(e, tok=tok):
            e.wait_ge(self._semof(tok), tok.val)
            return e.memzero(self._junk)
        self.steps.append(("scalar", fn, "n", chain))

    def _semof(self, tok):
        if tok.chain == "cc":
            return self._ccsem
        return self._sems[tok.chain][tok.slot]

    def gd(self, fn, wait=None):
        n = self._gd_count
        self._gd_count += 1
        self.steps.append(("gpsimd", fn, ("gd", wait), 0))
        return n

    def gwait(self, n):
        def w(e):
            e.wait_ge(self.gsem, 16 * (n + 1))
        return w

    def _emit(self, step):
        if self._cap is not None:
            self._cap.append(step)
        else:
            self.steps.append(step)

    def capture_begin(self):
        self._cap = []

    def capture_end(self):
        l = self._cap
        self._cap = None
        return l

    def interleave(self, LA, LB):
        def groups(L):
            gs = []
            for s in L:
                if (gs and gs[-1][-1][0] == s[0] and gs[-1][-1][3] == s[3]
                        and s[2] == "n" and gs[-1][-1][2] == "n"):
                    gs[-1].append(s)
                else:
                    gs.append([s])
            return gs
        ga, gb = groups(LA), groups(LB)
        for i in range(max(len(ga), len(gb))):
            if i < len(ga):
                self.steps.extend(ga[i])
            if i < len(gb):
                self.steps.extend(gb[i])

    def brk(self):
        self._emit(("scalar", lambda e: e.memzero(self._junk), "n", self.chain))

    def brk2(self):
        self._emit(("vector", lambda e: e.memzero(self._junk), "n", self.chain))

    def t(self, fn): self._emit(("tensor", fn, "n", self.chain))
    def v(self, fn): self._emit(("vector", fn, "n", self.chain))
    def a(self, fn): self._emit(("scalar", fn, "n", self.chain))
    def g(self, fn): self._emit(("gpsimd", fn, "n", self.chain))
    def s(self, fn): self._emit(("sync", fn, "n", self.chain))
    def cc(self, fn): self.steps.append(("gpsimd", fn, "cc", 0))

    def lower(self, block):
        nc = self.nc
        NB = self.nbar
        sems = [[nc.alloc_semaphore(f"bar{c}_{i}") for i in range(NB)]
                for c in range(self.nchains)]
        cc_sem = nc.alloc_semaphore("cc_sem")
        self.gsem = nc.alloc_semaphore("gsem")
        self._sems = sems
        self._ccsem = cc_sem
        runs = []
        step_run = {}
        for si, (eng, fn, kind, ch) in enumerate(self.steps):
            if isinstance(kind, tuple) and kind[0] == "gd":
                runs.append([eng, [fn], "gd", kind[1], 0])
            elif (kind == "n" and runs and runs[-1][0] == eng
                  and runs[-1][2] == "n" and runs[-1][4] == ch):
                runs[-1][1].append(fn)
            else:
                runs.append([eng, [fn], kind, None, ch])
            step_run[si] = len(runs) - 1
        nruns = len(runs)
        sem_tot = [[0] * NB for _ in range(self.nchains)]
        run_wait = [None] * nruns
        cum_after = [None] * nruns
        prev_chain_run = [None] * self.nchains
        chain_idx = [0] * self.nchains
        for i, (eng, fns, kind, extra, ch) in enumerate(runs):
            if kind == "gd":
                continue
            if prev_chain_run[ch] is not None:
                run_wait[i] = cum_after[prev_chain_run[ch]]
            if kind == "cc":
                cum_after[i] = ("cc", 0, 1)
            else:
                s = chain_idx[ch] % NB
                amt = 16 * len(fns) if eng in DMA_ENGS else 1
                sem_tot[ch][s] += amt
                cum_after[i] = (ch, s, sem_tot[ch][s])
            prev_chain_run[ch] = i
            chain_idx[ch] += 1
        for si, tok in self._tok_requests:
            ri = step_run[si]
            assert cum_after[ri] is not None, "token on a non-chained step"
            tok.chain, tok.slot, tok.val = cum_after[ri]
        per_eng = {e: [] for e in self.ENGS}
        for i, r in enumerate(runs):
            per_eng[r[0]].append((i, r[0], r[1], r[2], r[3], r[4]))

        def semof(marker):
            ch, s, v = marker
            return (cc_sem if ch == "cc" else sems[ch][s]), v

        def make_body(items):
            def body(eng):
                for (i, ename, fns, kind, extra, ch) in items:
                    if kind == "gd":
                        if extra is not None:
                            sm, v = semof((extra.chain, extra.slot, extra.val))
                            eng.wait_ge(sm, v)
                        fns[0](eng).then_inc(self.gsem, 16)
                        continue
                    if run_wait[i] is not None:
                        sm, v = semof(run_wait[i])
                        eng.wait_ge(sm, v)
                    if kind == "cc":
                        fns[0](eng).then_inc(cc_sem)
                        continue
                    _c, s, _v = cum_after[i]
                    if ename in DMA_ENGS:
                        for fn in fns:
                            fn(eng).then_inc(sems[ch][s], 16)
                    else:
                        last = None
                        for fn in fns:
                            rr = fn(eng)
                            if rr is not None:
                                last = rr
                        last.then_inc(sems[ch][s], 1)
            return body
        for e in self.ENGS:
            if per_eng[e]:
                getattr(block, e)(make_body(per_eng[e]))


def cdiv(a, b):
    return (a + b - 1) // b


def host_consts():
    r = np.arange(128)
    c = {}
    c["c_idb"] = np.eye(128, dtype=np.float32).astype(BF)
    c["c_idf"] = np.eye(128, dtype=np.float32)
    c["c_mask_slneg"] = np.where(r[:, None] > r[None, :], -1.0, 0.0).astype(np.float32)
    c["c_mask_su"] = np.where(r[:, None] < r[None, :], 1.0, 0.0).astype(np.float32)
    c["c_mask_le"] = np.where(r[:, None] <= r[None, :], 1.0, 0.0).astype(np.float32)
    c["c_maskA"] = np.where(r[:, None] > r[None, :], BIGNEG, 0.0).astype(np.float32).astype(BF)
    c["c_maskB"] = np.where(r[:, None] < r[None, :], BIGNEG, 0.0).astype(np.float32).astype(BF)
    c["c_ones3"] = np.ones((3, 128), dtype=np.float32).astype(BF)
    c["c_ones1_64"] = np.ones((1, 64), dtype=np.float32)
    c["c_ones3T"] = np.ones((3, 4096), dtype=np.float32).astype(BF)
    c["c_onesblk"] = np.stack([(r < 64).astype(np.float32),
                               (r >= 64).astype(np.float32)], axis=1).astype(BF)
    c["c_onesblkT"] = np.ascontiguousarray(
        np.stack([(r < 64), (r >= 64)], axis=0).astype(np.float32))
    c["c_trilf"] = np.where(r[:, None] <= r[None, :], 1.0, 0.0).astype(np.float32)
    c["c_onescol"] = np.ones((128, 1), dtype=np.float32)
    c["c_fvals"] = np.tile(np.array([[1e-6, 1.0, 1e-5, 1e-12, 0, 0, 0, 0]],
                                    dtype=np.float32), (128, 1))
    c["c_onesrow1"] = np.ones((1, 128), dtype=np.float32)
    return c


CONST_SPECS = {k: v for k, v in host_consts().items()}


class ChainCtx:
    def __init__(self, A, nc, cid, psT):
        self.cid = cid
        sfx = f"_c{cid}"
        self.Kd = A("Kd70" + sfx, [70, T], BF16)
        self.q70 = A("q70" + sfx, [70, 256], BF16)
        self.S = A("S_sb" + sfx, [64, 128], BF16)
        self.QbT = A("QbT" + sfx, [64, 384], BF16)
        self.QbTs = A("QbTs" + sfx, [64, 128], BF16)
        self.negbWb = A("negbWb" + sfx, [128, 64], BF16)
        self.Xp = A("Xp" + sfx, [128, 256], F32)
        self.Mp = A("Mp" + sfx, [128, 128], F32)
        self.QWt_m = A("QWt_m" + sfx, [128, 128], F32)
        self.RT = A("RT" + sfx, [128, 128], BF16)
        self.RbT = A("RbT" + sfx, [128, 128], BF16)
        self.RbTs = A("RbTs" + sfx, [128, 128], BF16)
        self.QtT = A("QtT" + sfx, [64, 128], BF16)
        self.negsWK = A("negsWK" + sfx, [128, 128], BF16)
        self.bsWK = A("bsWK" + sfx, [128, 128], BF16)
        self.X_u = A("X_u" + sfx, [128, 128], F32)
        self.RwT = A("RwT" + sfx, [128, 128], BF16)
        self.V_sb = A("V_sb" + sfx, [128, 64], BF16)
        self.McT = A("McT" + sfx, [64, 128], BF16)
        self.D_sb = A("D_sb" + sfx, [128, 128], BF16)
        self.negG3r = A("negG3r" + sfx, [3, 128], BF16)
        self.mpart = A("mpart" + sfx, [128, 12], F32)
        self.mcol = A("mcol" + sfx, [128, 1], F32)
        self.negm_sb = A("negm_sb" + sfx, [128, 1], F32)
        self.sarow = A("sarow" + sfx, [1, 128], F32)
        self.srow_b = A("srow_b" + sfx, [1, 128], BF16)
        self.zinv = A("zinv" + sfx, [1, 128], F32)
        self.zrep = A("zrep" + sfx, [64, 128], F32)
        self.ot_raw = A("ot_raw" + sfx, [64, 128], F32)
        self.P_buf = A("P_buf" + sfx, [128, 1024], BF16)
        self.seq = None
        self.big = None
        self.pvw = None
        self.tp = psT


class ACtx:
    def __init__(self, nc, guarded=False):
        self._guards = []
        self._guards_attn = []
        self.nc = nc
        if guarded:
            def A(name, shape, dt):
                g = nc.sbuf_tensor(name, shape, dt)
                h = g.__enter__()
                self._guards.append(g)
                return h
        else:
            A = nc.alloc_sbuf_tensor
        self._A_guarded = guarded
        AP_ = nc.alloc_psum_tensor
        AC = nc.alloc_sbuf_tensor
        self.idb = AC("idb", [128, 128], BF16)
        self.idf = AC("idf", [128, 128], F32)
        self.mask_slneg = AC("mask_slneg", [128, 128], F32)
        self.mask_su = AC("mask_su", [128, 128], F32)
        self.maskA = AC("maskA", [128, 128], BF16)
        self.maskB = AC("maskB", [128, 128], BF16)
        self.ones3 = AC("ones3", [3, 128], BF16)
        self.ones1_64 = AC("ones1_64", [1, 64], F32)
        self.onesblk = AC("onesblk", [128, 2], BF16)
        self.onesblkT = AC("onesblkT", [2, 128], F32)
        self.trilf = AC("trilf", [128, 128], F32)
        self.onescol = AC("onescol", [128, 1], F32)
        self.onesrow1 = AC("onesrow1", [1, 128], F32)
        self.junk = AC("junk", [1, 8], F32)
        self.fvals = AC("fvals", [128, 8], F32)
        nc.const_aps.register(F32, 1e-6, self.fvals.ap()[:, 0:1])
        nc.const_aps.register(F32, 1.0, self.fvals.ap()[:, 1:2])
        nc.const_aps.register(F32, 1e-5, self.fvals.ap()[:, 2:3])
        nc.const_aps.register(F32, 1e-12, self.fvals.ap()[:, 3:4])
        self._guards_io = []
        if guarded:
            def AIO(name, shape, dt):
                g = nc.sbuf_tensor(name, shape, dt)
                h = g.__enter__()
                self._guards_io.append(g)
                return h
        else:
            AIO = nc.alloc_sbuf_tensor
        self.q_tok = AIO("q_tok", [128, NT * 64 * H], BF16)
        self.k_tok = AIO("k_tok", [128, NT * 64 * H], BF16)
        self.w_tok = AIO("w_tok", [128, NT * 64 * H], BF16)
        self.v65 = AIO("v65", [128, NT * 65 * H], BF16)
        self.bcol = AIO("bcol", [128, NT * H], F32)
        self.negG3_tm = AIO("negG3_tm", [128, NT * 3 * H], BF16)
        self.xtile = A("xtile", [128, 512], BF16)
        self.wblk = A("wblk", [128, 512], BF16)
        self.w1t = A("w1t", [128, 16 * 32], BF16)
        self.w2t = A("w2t", [32, 512], BF16)
        self.bgTw = A("bgTw", [128, 16 * 64], BF16)
        self.bgb = A("bgb", [64, 1], F32)
        self.qgrow = A("qgrow", [1, 512], F32)
        self.kgrow = A("kgrow", [1, 512], F32)
        self.grep = A("grep", [128, 512], F32)
        self.convs = A("convs", [128, 16], F32)
        self.sqbuf = A("sqbuf", [128, 512], F32)
        self.rstd8 = A("rstd8", [128, 8], F32)
        self.var8 = A("var8", [128, 8], F32)
        self.w0T = A("w0T", [32, T], BF16)
        self.wpre = A("wpre", [128, 2052], BF16)
        self.wconv = A("wconv", [128, 2048], F32)
        self.nrm2 = A("nrm2", [2, 512], F32)
        self.bgB = A("bgB", [40, 512], F32)
        self.bgS = A("bgS", [40, 512], F32)
        self.bgS2 = A("bgS2", [40, 512], F32)
        self.gtok = A("gtok", [128, 32], F32)
        self.Gtok = A("Gtok", [128, 32], F32)
        self.gcarry = A("gcarry", [1, 8], F32)
        self.msplit = A("msplit", [128, 32], F32)
        self.spl = A("spl", [128, 3 * 32], BF16)
        self.splf = A("splf", [128, 32], F32)
        self.stage24 = A("stage24", [24, 128], BF16)
        self.psA = AP_("psA", [128, 512], F32)
        self.psM = AP_("psM", [128, 512], F32)
        self.psQ = AP_("psQ", [128, 512], F32)
        self.big1 = AP_("big1", [128, 512], F32)
        self.big2 = AP_("big2", [128, 512], F32)
        self.psPV = AP_("psPV", [128, 512], F32)
        self.psT = AP_("psT", [128, 2048], BF16)

    def alloc_attn(self):
        nc = self.nc
        if self._A_guarded:
            def A(name, shape, dt):
                g = nc.sbuf_tensor(name, shape, dt)
                h = g.__enter__()
                self._guards_attn.append(g)
                return h
        else:
            A = nc.alloc_sbuf_tensor
        self.ch = [ChainCtx(A, nc, 0, None), ChainCtx(A, nc, 1, None)]
        self.oT = A("oT", [128, T], BF16)
        self.ch[0].seq = self.psA.ap()
        self.ch[0].big = self.psM.ap()
        self.ch[0].pvw = self.psQ.ap()
        self.ch[0].tp = self.psT.ap()[:, 0:1024]
        self.ch[1].seq = self.big1.ap()
        self.ch[1].big = self.big2.ap()
        self.ch[1].pvw = self.psPV.ap()
        self.ch[1].tp = self.psT.ap()[:, 1024:2048]

    def release_attn(self):
        for g in reversed(self._guards_attn):
            g.__exit__(None, None, None)
        self._guards_attn = []
        for g in reversed(self._guards_io):
            g.__exit__(None, None, None)
        self._guards_io = []


def release_guards(ctx):
    for g in reversed(ctx._guards):
        g.__exit__(None, None, None)
    ctx._guards = []


def emit_load_consts(P, ctx, dram):
    pairs = [("c_idb", ctx.idb), ("c_idf", ctx.idf), ("c_mask_slneg", ctx.mask_slneg),
             ("c_mask_su", ctx.mask_su),
             ("c_maskA", ctx.maskA), ("c_maskB", ctx.maskB), ("c_ones3", ctx.ones3),
             ("c_ones1_64", ctx.ones1_64), ("c_onesblk", ctx.onesblk),
             ("c_onesblkT", ctx.onesblkT), ("c_trilf", ctx.trilf),
             ("c_onescol", ctx.onescol), ("c_onesrow1", ctx.onesrow1),
             ("c_fvals", ctx.fvals)]
    for name, dst in pairs:
        if name in dram:
            P.s(lambda e, name=name, dst=dst: e.dma_start(out=dst.ap(), in_=dram[name]))


def emit_rms_apply(P, ctx, ps, grep, dst_tok, tt, eps):
    P.a(lambda e, ps=ps: e.activation(ctx.sqbuf.ap(), ps, AF.Square))
    P.v(lambda e: e.tensor_reduce(ctx.var8.ap(),
                                  ctx.sqbuf.ap().rearrange("p (h d) -> p h d", d=64),
                                  AX.X, AL.add))
    P.a(lambda e, eps=eps: e.activation(ctx.rstd8.ap(), ctx.var8.ap(), AF.Ln,
                                        scale=1.0 / 64, bias=eps))
    P.brk2()
    P.a(lambda e: e.activation(ctx.var8.ap(), ctx.rstd8.ap(), AF.Exp, scale=-0.5))
    for seg in range(8):
        P.v(lambda e, ps=ps, seg=seg, tt=tt: e.scalar_tensor_tensor(
            dst_tok.ap()[:, tt * 512 + seg * 64: tt * 512 + (seg + 1) * 64],
            ps[:, seg * 64:(seg + 1) * 64], ctx.var8.ap()[:, seg:seg + 1],
            grep.ap()[:, seg * 64:(seg + 1) * 64], AL.mult, AL.mult))


def emit_prep(P, ctx, dram):
    nc = P.nc
    psA = ctx.psA.ap()
    banks = [ctx.psA.ap(), ctx.psM.ap(), ctx.psQ.ap(), ctx.big2.ap()]
    big1, big2, psPV, psT = ctx.big1.ap(), ctx.big2.ap(), ctx.psPV.ap(), ctx.psT.ap()

    P.g(lambda e: e.dma_start(out=ctx.w1t.ap().rearrange("p (k n) -> p k n", n=32),
                              in_=dram["W1T"].rearrange("(k p) n -> p k n", p=128)))
    P.g(lambda e: e.dma_start(out=ctx.w2t.ap(), in_=dram["W2T"]))
    P.g(lambda e: e.dma_start(out=ctx.bgTw.ap().rearrange("p (k n) -> p k n", n=64),
                              in_=dram["bgT"].rearrange("(k p) n -> p k n", p=128)))
    P.s(lambda e: e.dma_start(out=ctx.bgb.ap(), in_=dram["bgb"]))
    P.s(lambda e: e.dma_start(out=ctx.convs.ap(), in_=dram["convs"]))
    P.s(lambda e: e.dma_start(out=ctx.qgrow.ap(), in_=dram["qgrow"]))
    P.s(lambda e: e.dma_start(out=ctx.kgrow.ap(), in_=dram["kgrow"]))
    P.v(lambda e: e.memset(ctx.v65.ap(), 1.0))
    P.v(lambda e: e.memset(ctx.gcarry.ap(), 0.0))
    P.v(lambda e: e.memset(ctx.wpre.ap()[:, 0:4], 0.0))

    for which, wname, grow, dst in [("q", "WqT", ctx.qgrow, ctx.q_tok),
                                    ("k", "WkT", ctx.kgrow, ctx.k_tok),
                                    ("v", "WvT", None, ctx.v65)]:
        if grow is not None:
            P.t(lambda e, grow=grow: e.matmul(
                psA[:, 0:512], ctx.onesrow1.ap(), grow.ap(),
                start=True, stop=True, skip_group_check=True))
            P.a(lambda e: e.copy(ctx.grep.ap(), psA[:, 0:512]))
        for tg in range(8):
            for kt in range(16):
                P.g(lambda e, tg=tg, kt=kt: e.dma_start(
                    out=ctx.xtile.ap(),
                    in_=dram["xT"][kt * 128:(kt + 1) * 128, tg * 512:(tg + 1) * 512]))
                P.g(lambda e, wname=wname, kt=kt: e.dma_start(
                    out=ctx.wblk.ap(),
                    in_=dram[wname][kt * 128:(kt + 1) * 128, :]))
                for sub in range(4):
                    P.t(lambda e, kt=kt, sub=sub: e.matmul(
                        banks[sub][:, 0:512],
                        ctx.xtile.ap()[:, sub * 128:(sub + 1) * 128],
                        ctx.wblk.ap(), start=(kt == 0), stop=(kt == 15),
                        skip_group_check=True))
                if which == "v":
                    P.t(lambda e, kt=kt: e.matmul(
                        psPV[0:32, 0:512], ctx.w1t.ap()[:, kt * 32:(kt + 1) * 32],
                        ctx.xtile.ap()[:, 0:512],
                        start=(kt == 0), stop=(kt == 15), skip_group_check=True))
                    P.t(lambda e, kt=kt: e.matmul(
                        big1[0:64, 0:512], ctx.bgTw.ap()[:, kt * 64:(kt + 1) * 64],
                        ctx.xtile.ap()[:, 0:512],
                        start=(kt == 0), stop=(kt == 15), skip_group_check=True))
            if which in ("q", "k"):
                for sub in range(4):
                    emit_rms_apply(P, ctx, banks[sub][:, 0:512], ctx.grep, dst,
                                   tg * 4 + sub, 1e-6)
            else:
                for sub in range(4):
                    tt = tg * 4 + sub
                    P.v(lambda e, sub=sub, tt=tt: e.tensor_copy(
                        ctx.v65.ap()[:, tt * 520: tt * 520 + 520]
                        .rearrange("p (h c) -> p h c", c=65)[:, :, 0:64],
                        banks[sub][:, 0:512].rearrange("p (h d) -> p h d", d=64)))
                P.a(lambda e, tg=tg: e.copy(ctx.w0T.ap()[:, tg * 512:(tg + 1) * 512],
                                            psPV[0:32, 0:512]))
                emit_bg(P, ctx, tg)

    emit_wchain(P, ctx)


def emit_bg(P, ctx, tg):
    big1, big2 = ctx.big1.ap(), ctx.big2.ap()
    psT = ctx.psT.ap()
    P.a(lambda e: e.activation(ctx.bgS.ap()[0:8, :], big1[0:8, 0:512], AF.Sigmoid,
                               bias=ctx.bgb.ap()[0:8, :]))
    P.v(lambda e: e.tensor_scalar_mul(ctx.bgB.ap()[0:8, :], ctx.bgS.ap()[0:8, :], 2.0))
    for sub in range(4):
        P.t(lambda e, sub=sub: e.transpose(big2[:, 0:8],
                                           ctx.bgB.ap()[0:8, sub * 128:(sub + 1) * 128],
                                           ctx.idf.ap()[0:8, 0:8]))
        P.a(lambda e, sub=sub, tg=tg: e.copy(
            ctx.bcol.ap()[:, (tg * 4 + sub) * 8:(tg * 4 + sub) * 8 + 8], big2[:, 0:8]))
    P.v(lambda e: e.tensor_scalar_add(ctx.bgB.ap()[32:40, :], big1[32:40, 0:512],
                                      ctx.bgb.ap()[32:40, :]))
    P.a(lambda e: e.activation(ctx.bgS.ap()[32:40, :], ctx.bgB.ap()[32:40, :],
                               AF.Relu, scale=-1.0))
    P.a(lambda e: e.activation(ctx.bgS2.ap()[32:40, :], ctx.bgB.ap()[32:40, :], AF.Abs))
    P.brk2()
    P.a(lambda e: e.activation(ctx.bgB.ap()[32:40, :], ctx.bgS2.ap()[32:40, :],
                               AF.Exp, scale=-1.0))
    P.brk2()
    P.a(lambda e: e.activation(ctx.bgS2.ap()[32:40, :], ctx.bgB.ap()[32:40, :],
                               AF.Ln, bias=1.0))
    P.v(lambda e: e.scalar_tensor_tensor(ctx.bgB.ap()[32:40, :],
                                         ctx.bgS2.ap()[32:40, :], -1.0,
                                         ctx.bgS.ap()[32:40, :], AL.mult, AL.subtract))
    for sub in range(4):
        P.t(lambda e, sub=sub: e.transpose(big2[:, 0:8],
                                           ctx.bgB.ap()[32:40, sub * 128:(sub + 1) * 128],
                                           ctx.idf.ap()[32:40, 32:40]))
        P.a(lambda e, sub=sub: e.copy(ctx.gtok.ap()[:, sub * 8:(sub + 1) * 8],
                                      big2[:, 0:8]))
    for sub in range(4):
        P.t(lambda e, sub=sub: e.matmul(big2[:, 16:24], ctx.trilf.ap(),
                                        ctx.gtok.ap()[:, sub * 8:(sub + 1) * 8],
                                        start=True, stop=False, skip_group_check=True))
        P.t(lambda e: e.matmul(big2[:, 16:24], ctx.onesrow1.ap(), ctx.gcarry.ap(),
                               start=False, stop=True, skip_group_check=True))
        P.a(lambda e, sub=sub: e.copy(ctx.Gtok.ap()[:, sub * 8:(sub + 1) * 8],
                                      big2[:, 16:24]))
        P.t(lambda e, sub=sub: e.matmul(big2[0:1, 24:32], ctx.onescol.ap(),
                                        ctx.gtok.ap()[:, sub * 8:(sub + 1) * 8],
                                        start=True, stop=False, skip_group_check=True))
        P.t(lambda e: e.matmul(big2[0:1, 24:32], ctx.onesrow1.ap()[:, 0:1],
                               ctx.gcarry.ap(), start=False, stop=True,
                               skip_group_check=True))
        P.v(lambda e: e.tensor_copy(ctx.gcarry.ap(), big2[0:1, 24:32]))
        P.brk()
    P.v(lambda e: e.tensor_scalar_mul(ctx.msplit.ap(), ctx.Gtok.ap(), -1.0))
    P.a(lambda e: e.copy(ctx.spl.ap()[:, 0:32], ctx.msplit.ap()))
    P.v(lambda e: e.tensor_copy(ctx.splf.ap(), ctx.spl.ap()[:, 0:32]))
    P.brk()
    P.v(lambda e: e.tensor_tensor(ctx.msplit.ap(), ctx.msplit.ap(), ctx.splf.ap(),
                                  AL.subtract))
    P.a(lambda e: e.copy(ctx.spl.ap()[:, 32:64], ctx.msplit.ap()))
    P.v(lambda e: e.tensor_copy(ctx.splf.ap(), ctx.spl.ap()[:, 32:64]))
    P.brk()
    P.v(lambda e: e.tensor_tensor(ctx.msplit.ap(), ctx.msplit.ap(), ctx.splf.ap(),
                                  AL.subtract))
    P.a(lambda e: e.copy(ctx.spl.ap()[:, 64:96], ctx.msplit.ap()))
    for sub in range(4):
        tt = tg * 4 + sub
        P.v(lambda e, sub=sub, tt=tt: e.tensor_copy(
            ctx.negG3_tm.ap()[:, tt * 24: tt * 24 + 24]
            .rearrange("p (h s) -> p h s", s=3),
            ctx.spl.ap().rearrange("p (s q) -> p s q", q=32)[:, :, sub * 8:(sub + 1) * 8]
            .rearrange("p s h -> p h s")))
    for sub in range(4):
        tt = tg * 4 + sub
        P.t(lambda e, tt=tt: e.transpose(ctx.psT.ap()[0:24, 0:128],
                                         ctx.negG3_tm.ap()[:, tt * 24: tt * 24 + 24],
                                         ctx.idb.ap()))
        P.a(lambda e: e.copy(ctx.stage24.ap(), ctx.psT.ap()[0:24, 0:128]))
        P.s(lambda e, tt=tt: e.dma_start(
            out=P.negG3_hbm.rearrange("h s t -> (h s) t")[:, tt * 128:(tt + 1) * 128],
            in_=ctx.stage24.ap()))


def emit_wchain(P, ctx):
    psA, big1, big2, psT = ctx.psA.ap(), ctx.big1.ap(), ctx.big2.ap(), ctx.psT.ap()
    for dt in range(4):
        P.v(lambda e: e.memset(ctx.wpre.ap()[:, 0:4], 0.0))
        for half in range(2):
            for tgh in range(4):
                P.t(lambda e, dt=dt, half=half, tgh=tgh: e.matmul(
                    psA[:, 0:512], ctx.w2t.ap()[:, dt * 128:(dt + 1) * 128],
                    ctx.w0T.ap()[:, half * 2048 + tgh * 512: half * 2048 + (tgh + 1) * 512],
                    start=True, stop=True, skip_group_check=True))
                P.a(lambda e, tgh=tgh: e.copy(
                    ctx.wpre.ap()[:, 4 + tgh * 512: 4 + (tgh + 1) * 512], psA[:, 0:512]))
            P.v(lambda e, dt=dt: e.tensor_scalar_mul(
                ctx.wconv.ap(), ctx.wpre.ap()[:, 1:2049], ctx.convs.ap()[:, dt * 4: dt * 4 + 1]))
            for k in range(1, 4):
                P.brk()
                P.v(lambda e, dt=dt, k=k: e.scalar_tensor_tensor(
                    ctx.wconv.ap(), ctx.wpre.ap()[:, 1 + k: 2049 + k],
                    ctx.convs.ap()[:, dt * 4 + k: dt * 4 + k + 1], ctx.wconv.ap(),
                    AL.mult, AL.add))
            P.brk()
            P.v(lambda e: e.tensor_copy(ctx.wpre.ap()[:, 0:4], ctx.wpre.ap()[:, 2048:2052]))
            P.a(lambda e: e.activation(ctx.wpre.ap()[:, 4:2052], ctx.wconv.ap(), AF.Square))
            for tgh in range(4):
                P.t(lambda e, tgh=tgh: e.matmul(
                    big1[0:2, 0:512], ctx.onesblk.ap(),
                    ctx.wpre.ap()[:, 4 + tgh * 512: 4 + (tgh + 1) * 512],
                    start=True, stop=True, skip_group_check=True))
                P.v(lambda e: e.tensor_scalar_add(ctx.nrm2.ap(), big1[0:2, 0:512], 1e-12))
                P.a(lambda e: e.activation(ctx.bgS.ap()[0:2, :], ctx.nrm2.ap(), AF.Ln))
                P.brk2()
                P.a(lambda e: e.activation(ctx.nrm2.ap(), ctx.bgS.ap()[0:2, :],
                                           AF.Exp, scale=-0.5))
                P.t(lambda e: e.matmul(big2[:, 0:512], ctx.onesblkT.ap(), ctx.nrm2.ap(),
                                       start=True, stop=True, skip_group_check=True))
                P.v(lambda e, tgh=tgh: e.tensor_tensor(
                    ctx.wpre.ap()[:, 4 + tgh * 512: 4 + (tgh + 1) * 512],
                    ctx.wconv.ap()[:, tgh * 512:(tgh + 1) * 512], big2[:, 0:512], AL.mult))
                for s4 in range(4):
                    tt = half * 16 + tgh * 4 + s4
                    P.t(lambda e, tgh=tgh, s4=s4: e.transpose(
                        psT[:, 0:128],
                        ctx.wpre.ap()[:, 4 + tgh * 512 + s4 * 128: 4 + tgh * 512 + (s4 + 1) * 128],
                        ctx.idb.ap()))
                    P.a(lambda e, tt=tt, dt=dt: e.copy(
                        ctx.w_tok.ap()[:, tt * 512 + dt * 128: tt * 512 + (dt + 1) * 128],
                        psT[:, 0:128]))


def emit_head(P, ctx, ch, h, dram, win=WIN):
    P.chain = ch.cid
    P.s(lambda e: e.dma_start(out=ch.Kd.ap()[64:67, :],
                              in_=dram["c_ones3T"][:, 0:T]))
    P.s(lambda e, h=h: e.dma_start(out=ch.Kd.ap()[67:70, :], in_=P.negG3_hbm[h]))
    P.s(lambda e: e.dma_start(out=ch.q70.ap()[67:70, :],
                              in_=dram["c_ones3T"][:, 0:256]))
    P.v(lambda e: e.memset(ch.q70.ap()[64:67, :], 0.0))
    P.v(lambda e: e.memset(ch.sarow.ap(), 0.0))


def emit_chunk(P, ctx, ch, h, c, win=WIN):
    P.chain = ch.cid
    t0 = c * CHUNK
    tt = c
    r = c % win
    wstart = (c // win) * win * CHUNK

    qs = ctx.q_tok.ap()[:, tt * 64 * H + 64 * h: tt * 64 * H + 64 * (h + 1)]
    ks = ctx.k_tok.ap()[:, tt * 64 * H + 64 * h: tt * 64 * H + 64 * (h + 1)]
    ws = ctx.w_tok.ap()[:, tt * 64 * H + 64 * h: tt * 64 * H + 64 * (h + 1)]
    bc = ctx.bcol.ap()[:, tt * H + h: tt * H + h + 1]
    gsl = ctx.negG3_tm.ap()[:, tt * 3 * H + 3 * h: tt * 3 * H + 3 * (h + 1)]

    seq, big, pvw, tp = ch.seq, ch.big, ch.pvw, ch.tp
    QbT = ch.QbT.ap()[0:64, 0:128]
    KbT = ch.QbT.ap()[0:64, 128:256]
    WbT = ch.QbT.ap()[0:64, 256:384]

    P.t(lambda e: e.transpose(tp[0:64, 0:128], qs, ctx.idb.ap()))
    P.t(lambda e: e.transpose(tp[0:64, 128:256], ks, ctx.idb.ap()))
    P.t(lambda e: e.transpose(tp[0:64, 256:384], ws, ctx.idb.ap()))
    P.t(lambda e: e.transpose(tp[0:3, 384:512], gsl, ctx.idb.ap()))
    P.a(lambda e: e.copy(ch.QbT.ap()[0:64, 0:384], tp[0:64, 0:384]))
    P.a(lambda e: e.copy(ch.negG3r.ap(), tp[0:3, 384:512]))
    P.a(lambda e: e.mul(ch.QbTs.ap(), QbT, SCALE))
    P.v(lambda e: e.tensor_scalar(ch.negbWb.ap(), ws, bc, -1.0, AL.mult, AL.mult))

    P.t(lambda e: e.matmul(seq[:, 0:128], WbT, WbT, start=True, stop=True,
                           skip_group_check=True))
    P.t(lambda e: e.matmul(seq[:, 128:256], WbT, KbT, start=True, stop=True,
                           skip_group_check=True))
    P.t(lambda e: e.matmul(seq[:, 256:384], WbT, QbT, start=True, stop=True,
                           skip_group_check=True))
    P.v(lambda e: e.scalar_tensor_tensor(ch.Xp.ap()[:, 0:128], seq[:, 0:128], bc,
                                         ctx.mask_slneg.ap(), AL.mult, AL.mult))
    P.v(lambda e: e.tensor_tensor(ch.X_u.ap(), seq[:, 0:128], ctx.mask_su.ap(),
                                  AL.mult))
    P.v(lambda e: e.tensor_tensor(ch.negsWK.ap(), seq[:, 128:256],
                                  ctx.mask_slneg.ap(), AL.mult))
    P.v(lambda e: e.tensor_tensor(ch.QWt_m.ap(), seq[:, 256:384],
                                  ctx.trilf.ap(), AL.mult))
    P.t(lambda e: e.transpose(seq[:, 384:512], ch.Xp.ap()[:, 0:128], ctx.idf.ap()))
    P.a(lambda e: e.copy(ch.Xp.ap()[:, 128:256], seq[:, 384:512]))

    P.v(lambda e: e.tensor_tensor(ch.Mp.ap()[:, 0:128], ch.Xp.ap()[:, 0:128],
                                  ctx.idf.ap(), AL.add))
    for lev in range(6):
        last = lev == 5
        if not last:
            P.t(lambda e: e.matmul(seq[:, 0:128], ch.Xp.ap()[:, 128:256],
                                   ch.Xp.ap()[:, 0:128], start=True, stop=True,
                                   skip_group_check=True))
        P.t(lambda e: e.matmul(seq[:, 128:256], ch.Xp.ap()[:, 0:128],
                               ch.Xp.ap()[:, 128:256], start=True, stop=True,
                               skip_group_check=True))
        if not last:
            P.a(lambda e: e.copy(ch.Xp.ap()[:, 0:256], seq[:, 0:256]))
        else:
            P.a(lambda e: e.copy(ch.Xp.ap()[:, 128:256], seq[:, 128:256]))
        P.t(lambda e: e.matmul(seq[:, 256:384], ch.Xp.ap()[:, 128:256],
                               ch.Mp.ap()[:, 0:128], start=True, stop=True,
                               skip_group_check=True))
        P.v(lambda e: e.tensor_tensor(ch.Mp.ap()[:, 0:128], ch.Mp.ap()[:, 0:128],
                                      seq[:, 256:384], AL.add))

    Minv = ch.Mp.ap()[:, 0:128]
    P.t(lambda e: e.matmul(seq[:, 0:128], Minv, ch.QWt_m.ap(), start=True,
                           stop=True, skip_group_check=True))
    P.a(lambda e: e.copy(ch.RT.ap(), seq[:, 0:128]))
    P.v(lambda e: e.tensor_scalar(ch.RbT.ap(), ch.RT.ap(), bc, None, AL.mult))
    P.a(lambda e: e.mul(ch.RbTs.ap(), ch.RbT.ap(), SCALE))
    P.t(lambda e: e.matmul(seq[0:64, 128:256], ch.negbWb.ap(), ch.RT.ap(),
                           start=True, stop=True, skip_group_check=True))
    P.v(lambda e: e.tensor_tensor(ch.QtT.ap(), seq[0:64, 128:256], QbT, AL.add))
    P.a(lambda e: e.mul(ch.q70.ap()[0:64, 128:256], ch.QtT.ap(), SCALE))
    if r == 0:
        P.a(lambda e: e.mul(ch.q70.ap()[0:64, 0:128], ch.QtT.ap(), SCALE))
    else:
        P.t(lambda e: e.matmul(seq[0:64, 256:384], ch.S.ap()[:, 0:64], ch.QtT.ap(),
                               start=True, stop=True, skip_group_check=True))
        P.a(lambda e: e.mul(ch.q70.ap()[0:64, 0:128], seq[0:64, 256:384], SCALE))

    groups = []
    for j0 in range(0, wstart, 512):
        groups.append((j0, min(512, wstart - j0), 0))
    if t0 > wstart:
        groups.append((wstart, t0 - wstart, 128))
    nmp = 0
    for (j0, wdt, voff) in groups:
        P.t(lambda e, j0=j0, wdt=wdt, voff=voff:
            e.matmul(big[:, 0:wdt], ch.q70.ap()[:, voff:voff + 128],
                     ch.Kd.ap()[:, j0:j0 + wdt], start=True, stop=True,
                     skip_group_check=True))
        P.v(lambda e, wdt=wdt, nmp=nmp:
            e.tensor_reduce(ch.mpart.ap()[:, nmp:nmp + 1], big[:, 0:wdt],
                            AX.X, AL.max))
        nmp += 1
    P.t(lambda e: e.matmul(big[:, 0:128], ch.QbTs.ap(), KbT, start=True,
                           stop=False, skip_group_check=True))
    P.t(lambda e: e.matmul(big[:, 0:128], ch.RbTs.ap(), ch.negsWK.ap(),
                           start=False, stop=False, skip_group_check=True))
    P.t(lambda e: e.matmul(big[:, 0:128], ctx.ones3.ap(), ch.negG3r.ap(),
                           start=False, stop=False, skip_group_check=True))
    if c > 0:
        P.t(lambda e: e.matmul(big[:, 0:128], ch.srow_b.ap(),
                               ctx.ones3.ap()[0:1, 0:128], start=False,
                               stop=False, skip_group_check=True))
    P.t(lambda e: e.matmul(big[:, 0:128], ctx.maskA.ap(), ctx.idb.ap(),
                           start=False, stop=True, skip_group_check=True))
    P.v(lambda e, nmp=nmp: e.tensor_reduce(ch.mpart.ap()[:, nmp:nmp + 1],
                                           big[:, 0:128], AX.X, AL.max))
    nmp += 1
    P.brk()
    P.v(lambda e, nmp=nmp: e.tensor_reduce(ch.mcol.ap()[:, 0:1],
                                           ch.mpart.ap()[:, 0:nmp], AX.X, AL.max))
    P.a(lambda e: e.mul(ch.negm_sb.ap(), ch.mcol.ap()[:, 0:1], -1.0))
    P.t(lambda e: e.transpose(pvw[0:1, 128:256], ch.negm_sb.ap(), ctx.idf.ap()))
    P.v(lambda e: e.tensor_tensor(ch.sarow.ap(), ch.sarow.ap(),
                                  pvw[0:1, 128:256], AL.add))
    P.a(lambda e: e.copy(ch.srow_b.ap(), ch.sarow.ap()))
    P.v(lambda e: e.tensor_copy(ch.q70.ap()[64:65, 0:128], ch.srow_b.ap()))
    P.v(lambda e: e.tensor_copy(ch.q70.ap()[64:65, 128:256], ch.srow_b.ap()))

    n_jt = t0 // 128
    total_jt = n_jt + 1
    pv_count = [0]

    def emit_pv(jt, pslice):
        vsl = ctx.v65.ap()[:, jt * 65 * H + 65 * h: jt * 65 * H + 65 * h + 65]
        first = pv_count[0] == 0
        last = pv_count[0] == total_jt - 1
        P.t(lambda e, vsl=vsl, pslice=pslice, first=first, last=last: e.matmul(
            pvw[0:65, 0:128], vsl, pslice, start=first, stop=last,
            skip_group_check=True))
        pv_count[0] += 1

    pos = 0
    pbswap = 0
    while pos < n_jt:
        grp = []
        while pos < n_jt and len(grp) < 4:
            grp.append(pos)
            pos += 1
        for gi, jt in enumerate(grp):
            voff = 0 if (jt * 128) < wstart else 128
            P.t(lambda e, jt=jt, gi=gi, voff=voff, lg=(gi == len(grp) - 1):
                e.matmul(big[:, gi * 128:(gi + 1) * 128],
                         ch.Kd.ap()[:, jt * 128:(jt + 1) * 128],
                         ch.q70.ap()[:, voff:voff + 128],
                         start=(gi == 0), stop=lg, skip_group_check=True))
        nc_ = len(grp) * 128
        pb = ch.P_buf.ap()[:, (pbswap % 2) * 512:(pbswap % 2) * 512 + nc_]
        P.a(lambda e, pb=pb, nc_=nc_: e.activation(pb, big[:, 0:nc_], AF.Exp))
        for gi, jt in enumerate(grp):
            emit_pv(jt, ch.P_buf.ap()[:, (pbswap % 2) * 512 + gi * 128:
                                      (pbswap % 2) * 512 + (gi + 1) * 128])
        pbswap += 1
    P.t(lambda e: e.matmul(big[:, 0:128], KbT, ch.QbTs.ap(), start=True,
                           stop=False, skip_group_check=True))
    P.t(lambda e: e.matmul(big[:, 0:128], ch.negsWK.ap(), ch.RbTs.ap(),
                           start=False, stop=False, skip_group_check=True))
    P.t(lambda e: e.matmul(big[:, 0:128], ch.negG3r.ap(), ctx.ones3.ap(),
                           start=False, stop=False, skip_group_check=True))
    P.t(lambda e: e.matmul(big[:, 0:128], ctx.ones3.ap()[0:1, 0:128],
                           ch.srow_b.ap(), start=False, stop=False,
                           skip_group_check=True))
    P.t(lambda e: e.matmul(big[:, 0:128], ctx.maskB.ap(), ctx.idb.ap(),
                           start=False, stop=True, skip_group_check=True))
    P.a(lambda e: e.activation(ch.P_buf.ap()[:, 896:1024], big[:, 0:128], AF.Exp))
    emit_pv(n_jt, ch.P_buf.ap()[:, 896:1024])

    P.v(lambda e: e.reciprocal(ch.zinv.ap(), pvw[64:65, 0:128]))
    P.v(lambda e: e.tensor_copy(ch.ot_raw.ap(), pvw[0:64, 0:128]))
    P.t(lambda e: e.matmul(pvw[0:64, 256:384], ctx.ones1_64.ap(), ch.zinv.ap(),
                           start=True, stop=True, skip_group_check=True))
    P.a(lambda e: e.copy(ch.zrep.ap(), pvw[0:64, 256:384]))
    ob = ctx.oT.ap()[ch.cid * 64: ch.cid * 64 + 64, t0:t0 + CHUNK]
    P.v(lambda e, ob=ob: e.tensor_tensor(ob, ch.ot_raw.ap(), ch.zrep.ap(), AL.mult))

    P.t(lambda e: e.matmul(seq[:, 0:128], Minv, ch.X_u.ap(), start=True,
                           stop=True, skip_group_check=True))
    P.a(lambda e: e.copy(ch.RwT.ap(), seq[:, 0:128]))
    P.t(lambda e: e.matmul(seq[:, 128:192], ch.RwT.ap(), ch.negbWb.ap(),
                           start=True, stop=True, skip_group_check=True))
    P.v(lambda e: e.tensor_tensor(ch.V_sb.ap(), seq[:, 128:192], ws, AL.add))
    P.t(lambda e: e.matmul(seq[0:64, 192:256], ch.V_sb.ap(), ch.negbWb.ap(),
                           start=True, stop=True, skip_group_check=True))
    P.v(lambda e: e.tensor_tensor(ch.McT.ap()[0:64, 0:64], seq[0:64, 192:256],
                                  ctx.idf.ap()[0:64, 0:64], AL.add))
    P.v(lambda e: e.tensor_scalar(ch.bsWK.ap(), ch.negsWK.ap(), bc, -1.0,
                                  AL.mult, AL.mult))
    P.t(lambda e: e.matmul(seq[:, 256:384], ch.RwT.ap(), ch.bsWK.ap(),
                           start=True, stop=True, skip_group_check=True))
    P.v(lambda e: e.scalar_tensor_tensor(ch.D_sb.ap(), seq[:, 256:384], -1.0,
                                         ch.negsWK.ap(), AL.mult, AL.subtract))
    if t0 > wstart:
        wdt = t0 - wstart
        P.t(lambda e, wdt=wdt: e.matmul(pvw[0:64, 0:wdt], ch.McT.ap()[0:64, 0:64],
                                        ch.Kd.ap()[0:64, wstart:t0],
                                        start=True, stop=True, skip_group_check=True))
        P.a(lambda e, wdt=wdt: e.copy(ch.Kd.ap()[0:64, wstart:t0], pvw[0:64, 0:wdt]))
    P.t(lambda e: e.matmul(seq[0:64, 384:512], ch.negbWb.ap(), ch.D_sb.ap(),
                           start=True, stop=True, skip_group_check=True))
    P.v(lambda e: e.tensor_tensor(ch.Kd.ap()[0:64, t0:t0 + CHUNK],
                                  seq[0:64, 384:512], KbT, AL.add))
    if r == 0:
        P.t(lambda e: e.transpose(tp[0:64, 0:64], ch.McT.ap()[0:64, 0:64],
                                  ctx.idb.ap()[0:64, 0:64]))
        P.a(lambda e: e.copy(ch.S.ap()[:, 0:64], tp[0:64, 0:64]))
        P.a(lambda e: e.copy(ch.S.ap()[:, 64:128], ch.McT.ap()[0:64, 0:64]))
    else:
        P.t(lambda e: e.matmul(pvw[0:64, 384:448], ch.McT.ap()[0:64, 0:64],
                               ch.S.ap()[:, 0:64], start=True, stop=True,
                               skip_group_check=True))
        P.t(lambda e: e.matmul(pvw[0:64, 448:512], ch.S.ap()[:, 0:64],
                               ch.McT.ap()[0:64, 0:64], start=True, stop=True,
                               skip_group_check=True))
        P.a(lambda e: e.copy(ch.S.ap()[:, 0:128], pvw[0:64, 384:512]))
    if r == win - 1 and wstart > 0:
        for j0 in range(0, wstart, 512):
            wdt = min(512, wstart - j0)
            P.t(lambda e, j0=j0, wdt=wdt: e.matmul(
                pvw[0:64, 0:wdt], ch.S.ap()[:, 64:128],
                ch.Kd.ap()[0:64, j0:j0 + wdt], start=True, stop=True,
                skip_group_check=True))
            P.a(lambda e, j0=j0, wdt=wdt: e.copy(ch.Kd.ap()[0:64, j0:j0 + wdt],
                                                 pvw[0:64, 0:wdt]))


class BCtx:
    def __init__(self, nc, actx=None):
        self._guards = []
        if actx is not None:
            def A(name, shape, dt):
                g = nc.sbuf_tensor(name, shape, dt)
                h = g.__enter__()
                self._guards.append(g)
                return h
            AP_ = None
        else:
            A = nc.alloc_sbuf_tensor
            AP_ = nc.alloc_psum_tensor
        self._actx = actx
        self.moe = A("moe", [128, 8 * 2048], F32)
        self.x2T = A("x2T", [128, 16 * 1024], BF16)
        self.he = A("he", [128, 11 * 1024], BF16)
        self.gblk = A("gblk", [128, 2 * 16 * 256], BF16)
        self.ublk = A("ublk", [128, 2 * 16 * 256], BF16)
        self.dblk = A("dblk", [128, 2 * 11 * 512], BF16)
        self.otk = A("otk", [128, 2 * 2048], BF16)
        self.xbuf = A("xbuf", [128, 2048], F32)
        self.x2b = A("x2b", [128, 2048], BF16)
        self.crep = A("crep", [128, 1024], BF16)
        self.rwT = A("rwT", [128, 16 * 8], BF16)
        self.comb = A("comb", [128, 64], F32)
        self.combb = A("combb", [128, 64], BF16)
        self.combT = A("combT", [8, 1024], BF16)
        self.sel = A("sel", [8, 1024], BF16)
        self.sgb = A("sgb", [128, 22], F32)
        self.sub = A("sub", [128, 22], F32)
        if actx is not None:
            self.idb, self.idf = actx.idb, actx.idf
            self.onesrow1, self.junk, self.fvals = actx.onesrow1, actx.junk, actx.fvals
        else:
            self.idb = A("idb", [128, 128], BF16)
            self.idf = A("idf", [128, 128], F32)
            self.onesrow1 = A("onesrow1", [1, 128], F32)
            self.junk = A("junk", [1, 8], F32)
            self.fvals = A("fvals", [128, 8], F32)
            nc.const_aps.register(F32, 1e-6, self.fvals.ap()[:, 0:1])
            nc.const_aps.register(F32, 1.0, self.fvals.ap()[:, 1:2])
            nc.const_aps.register(F32, 1e-5, self.fvals.ap()[:, 2:3])
            nc.const_aps.register(F32, 1e-12, self.fvals.ap()[:, 3:4])
        self.r1 = A("r1", [128, 1], F32)
        self.r2 = A("r2", [128, 1], F32)
        self.r3 = A("r3", [128, 1], F32)
        self.r4 = A("r4", [128, 1], F32)
        self.r5 = A("r5", [128, 8], F32)
        self.r6 = A("r6", [128, 8], F32)
        self.sg_s = A("sg_s", [128, 512], BF16)
        self.uc_s = A("uc_s", [128, 512], BF16)
        if actx is not None:
            self.ps = [actx.psA, actx.psM, actx.psQ, actx.big1, actx.big2,
                       actx.psPV, actx.psA]
            self.psT = actx.psT
        else:
            self.ps = [AP_(f"ps{i}", [128, 512], F32) for i in range(7)]
            self.psT = AP_("psTb", [128, 1024], BF16)


def emit_ln(P, ctx, src_ap, tile_cols, out_ap):
    P.v(lambda e: e.tensor_reduce(ctx.r1.ap(), src_ap, AX.X, AL.add))
    P.a(lambda e: e.activation(ctx.xbuf.ap(), src_ap, AF.Square))
    P.v(lambda e: e.tensor_reduce(ctx.r2.ap(), ctx.xbuf.ap(), AX.X, AL.add))
    P.a(lambda e: e.mul(ctx.r3.ap(), ctx.r1.ap(), 1.0 / 2048))
    P.brk2()
    P.a(lambda e: e.activation(ctx.r1.ap(), ctx.r3.ap(), AF.Square))
    P.v(lambda e: e.scalar_tensor_tensor(ctx.r2.ap(), ctx.r2.ap(), 1.0 / 2048,
                                         ctx.r1.ap(), AL.mult, AL.subtract))
    P.a(lambda e: e.activation(ctx.r4.ap(), ctx.r2.ap(), AF.Ln, bias=1e-5))
    P.brk2()
    P.a(lambda e: e.activation(ctx.r2.ap(), ctx.r4.ap(), AF.Exp, scale=-0.5))
    P.a(lambda e: e.mul(ctx.r4.ap(), ctx.r3.ap(), -1.0))
    P.v(lambda e: e.tensor_scalar(out_ap, src_ap, ctx.r4.ap(), ctx.r2.ap(),
                                  AL.add, AL.mult))


def emit_launchB(P, ctx, dram, fused=False):
    nc = P.nc
    ps = [p.ap() for p in ctx.ps]
    psT = ctx.psT.ap()

    def otk_load(e, tt, par):
        dst = ctx.otk.ap()[:, par * 2048:(par + 1) * 2048] \
            .rearrange("p (k n) -> p k n", n=128)
        if fused:
            if "tq" not in P.regs:
                pid = e.partition_id()
                tq = e.scalar_reg_alu(AL.mod, pid, 4)
                P.regs["tq"] = e.scalar_reg_alu(AL.mult, tq, TQ)
            src3 = dram["oTq"].rearrange("(k p) t -> p k t", p=128)
            return e.dma_start(out=dst,
                               in_=src3[:, :, bass.ds(P.regs["tq"] + tt * 128, 128)])
        return e.dma_start(out=dst, in_=dram["oTq"][:, tt * 128:(tt + 1) * 128]
                           .rearrange("(k p) n -> p k n", p=128))

    n_rwT = P.gd(lambda e: e.dma_start(
        out=ctx.rwT.ap().rearrange("p (k n) -> p k n", n=8),
        in_=dram["rwT"].rearrange("(k p) n -> p k n", p=128)))
    P.s(lambda e: e.dma_start(out=ctx.sgb.ap(), in_=dram["sgb"]))
    P.s(lambda e: e.dma_start(out=ctx.sub.ap(), in_=dram["sub"]))
    P.s(lambda e: e.dma_start(out=ctx.idb.ap(), in_=dram["c_idb"]))
    P.s(lambda e: e.dma_start(out=ctx.idf.ap(), in_=dram["c_idf"]))
    P.s(lambda e: e.dma_start(out=ctx.onesrow1.ap(), in_=dram["c_onesrow1"]))
    P.s(lambda e: e.dma_start(out=ctx.fvals.ap(), in_=dram["c_fvals"]))
    P.s(lambda e: e.dma_start(out=ctx.sel.ap(), in_=dram["sel"]))

    wo_slab = [ctx.gblk.ap()[:, 0:8192], ctx.ublk.ap()[:, 0:8192]]
    wo_ord = {}
    for og in range(2):
        wo_ord[og] = P.gd(lambda e, og=og: e.dma_start(
            out=wo_slab[og % 2].rearrange("p (k n) -> p k n", n=512),
            in_=dram["WoT"][:, og * 512:(og + 1) * 512]
            .rearrange("(k p) n -> p k n", p=128)))
    cc_tok = getattr(P, "cc_tok", None)
    otk_ord = {}
    otk_ord[0] = P.gd(lambda e: otk_load(e, 0, 0), wait=cc_tok)
    wo_tok = {}
    otk_tok = {}
    for og in range(4):
        for tt in range(8):
            bk = ps[tt % 2]
            idx = og * 8 + tt
            for kt in range(16):
                def mmfn(e, kt=kt, bk=bk, og=og, idx=idx):
                    if kt == 0:
                        P.gwait(wo_ord[og])(e)
                        P.gwait(otk_ord[idx])(e)
                    return e.matmul(
                        bk[:, 0:512],
                        ctx.otk.ap()[:, (idx % 2) * 2048 + kt * 128:
                                     (idx % 2) * 2048 + (kt + 1) * 128],
                        wo_slab[og % 2][:, kt * 512:(kt + 1) * 512],
                        start=(kt == 0), stop=(kt == 15), skip_group_check=True)
                P.t(mmfn)
            P.a(lambda e, tt=tt, og=og, bk=bk: e.copy(
                ctx.moe.ap()[:, tt * 2048 + og * 512: tt * 2048 + (og + 1) * 512],
                bk[:, 0:512]))
            otk_tok[idx] = P.token()
            nxt = idx + 1
            if nxt < 32:
                wt = otk_tok.get(nxt - 2) or cc_tok
                otk_ord[nxt] = P.gd(
                    lambda e, nxt=nxt: otk_load(e, nxt % 8, nxt % 2), wait=wt)
        wo_tok[og] = otk_tok[og * 8 + 7]
        if og + 2 < 4:
            wo_ord[og + 2] = P.gd(lambda e, og=og: e.dma_start(
                out=wo_slab[og % 2].rearrange("p (k n) -> p k n", n=512),
                in_=dram["WoT"][:, (og + 2) * 512:(og + 3) * 512]
                .rearrange("(k p) n -> p k n", p=128)), wait=wo_tok[og])

    for tt in range(8):
        mt = ctx.moe.ap()[:, tt * 2048:(tt + 1) * 2048]
        P.s(lambda e, tt=tt: e.dma_start(out=ctx.xbuf.ap(),
                                         in_=dram["x_tok"][tt * 128:(tt + 1) * 128, :]))
        P.v(lambda e, mt=mt: e.tensor_tensor(mt, mt, ctx.xbuf.ap(), AL.add))
        P.brk()
        emit_ln(P, ctx, mt, 2048, ctx.xbuf.ap())
        P.brk()
        P.v(lambda e: e.tensor_copy(ctx.x2b.ap(), ctx.xbuf.ap()))
        P.a(lambda e, mt=mt: e.copy(mt, ctx.xbuf.ap()))
        for kt in range(16):
            P.t(lambda e, kt=kt: e.transpose(psT[:, 0:128],
                                             ctx.x2b.ap()[:, kt * 128:(kt + 1) * 128],
                                             ctx.idb.ap()))
            P.a(lambda e, kt=kt, tt=tt: e.copy(
                ctx.x2T.ap()[:, kt * 1024 + tt * 128: kt * 1024 + (tt + 1) * 128],
                psT[:, 0:128]))

    for tt in range(8):
        bk = ps[2]
        for kt in range(16):
            def rfn(e, kt=kt, tt=tt, bk=bk):
                if kt == 0 and tt == 0:
                    P.gwait(n_rwT)(e)
                return e.matmul(
                    bk[:, 0:8],
                    ctx.x2T.ap()[:, kt * 1024 + tt * 128: kt * 1024 + (tt + 1) * 128],
                    ctx.rwT.ap()[:, kt * 8:(kt + 1) * 8],
                    start=(kt == 0), stop=(kt == 15), skip_group_check=True)
            P.t(rfn)
        P.v(lambda e, bk=bk: e.tensor_reduce(ctx.r1.ap(), bk[:, 0:8], AX.X, AL.max))
        P.a(lambda e: e.mul(ctx.r2.ap(), ctx.r1.ap(), -1.0))
        P.brk2()
        P.a(lambda e, bk=bk: e.activation(ctx.r5.ap(), bk[:, 0:8], AF.Exp,
                                          bias=ctx.r2.ap()))
        P.brk()
        P.v(lambda e: e.tensor_reduce(ctx.r1.ap(), ctx.r5.ap(), AX.X, AL.add))
        P.brk()
        P.v(lambda e: e.reciprocal(ctx.r2.ap(), ctx.r1.ap()))
        P.brk()
        P.v(lambda e: e.tensor_scalar(ctx.r5.ap(), ctx.r5.ap(), ctx.r2.ap(), None,
                                      AL.mult))
        P.brk()
        P.v(lambda e: e.tensor_reduce(ctx.r1.ap(), ctx.r5.ap(), AX.X, AL.max))
        P.brk()
        P.v(lambda e: e.tensor_scalar(ctx.r6.ap(), ctx.r5.ap(), ctx.r1.ap(), None,
                                      AL.is_ge))
        P.brk()
        P.v(lambda e: e.scalar_tensor_tensor(ctx.r6.ap(), ctx.r6.ap(), -2.0,
                                             ctx.r5.ap(), AL.mult, AL.add))
        P.brk()
        P.v(lambda e: e.tensor_reduce(ctx.r2.ap(), ctx.r6.ap(), AX.X, AL.max))
        P.brk()
        P.v(lambda e: e.tensor_tensor(ctx.r3.ap(), ctx.r1.ap(), ctx.r2.ap(), AL.add))
        P.brk()
        P.v(lambda e: e.reciprocal(ctx.r4.ap(), ctx.r3.ap()))
        P.v(lambda e: e.tensor_scalar(ctx.r6.ap(), ctx.r5.ap(), ctx.r2.ap(), None,
                                      AL.is_ge))
        P.brk()
        P.v(lambda e, tt=tt: e.scalar_tensor_tensor(
            ctx.comb.ap()[:, tt * 8:(tt + 1) * 8], ctx.r5.ap(), 1.0, ctx.r6.ap(),
            AL.mult, AL.mult))
        P.brk()
        P.v(lambda e, tt=tt: e.tensor_scalar(ctx.comb.ap()[:, tt * 8:(tt + 1) * 8],
                                             ctx.comb.ap()[:, tt * 8:(tt + 1) * 8],
                                             ctx.r4.ap(), None, AL.mult))
        P.brk()
        P.a(lambda e, tt=tt: e.copy(ctx.combb.ap()[:, tt * 8:(tt + 1) * 8],
                                    ctx.comb.ap()[:, tt * 8:(tt + 1) * 8]))
        P.t(lambda e, tt=tt: e.transpose(psT[0:8, 0:128],
                                         ctx.combb.ap()[:, tt * 8:(tt + 1) * 8],
                                         ctx.idb.ap()))
        P.a(lambda e, tt=tt: e.copy(ctx.combT.ap()[:, tt * 128:(tt + 1) * 128],
                                    psT[0:8, 0:128]))

    ex_blocks = [(0, 2), (2, 2), (4, 2), (6, 2), (8, 2), (10, 1)]
    sh_blocks = [(0, 2), (2, 2), (4, 2), (6, 2), (8, 2), (10, 1),
                 (11, 2), (13, 2), (15, 2), (17, 2), (19, 2), (21, 1)]
    items = []
    for ex in range(E):
        for (it0, itn) in ex_blocks:
            items.append(("gu", ("gate_w", ex), ("up_w", ex), it0, itn))
        for og in range(4):
            items.append(("d", ("down_w", ex), 0, og))
    for bi, (it0, itn) in enumerate(sh_blocks):
        items.append(("gu", ("s_gate_w", None), ("s_up_w", None), it0, itn))
        if (it0 + itn) in (11, 22):
            halfE = 0 if (it0 + itn) == 11 else 1
            for og in range(4):
                items.append(("d", ("s_down_w", None), halfE * 11 * 128, og))

    def wsrc(spec):
        name, ex = spec
        return dram[name][ex] if ex is not None else dram[name]

    def load_item(e, it):
        kind = it[0]
        if kind == "gu2":
            pass
        if kind == "gu":
            _, gsrc, usrc, it0, itn = it
            return None
        return None

    ords = {}
    toks = {}

    def emit_load(i, wait):
        it = items[i]
        if it[0] == "gu":
            _, gsrc, usrc, it0, itn = it
            par = i % 2
            o1 = P.gd(lambda e, gsrc=gsrc, it0=it0, itn=itn, par=par: e.dma_start(
                out=ctx.gblk.ap()[:, par * 4096: par * 4096 + 16 * itn * 128]
                .rearrange("p (k n) -> p k n", n=itn * 128),
                in_=wsrc(gsrc)[:, it0 * 128:(it0 + itn) * 128]
                .rearrange("(k p) n -> p k n", p=128)), wait=wait)
            o2 = P.gd(lambda e, usrc=usrc, it0=it0, itn=itn, par=par: e.dma_start(
                out=ctx.ublk.ap()[:, par * 4096: par * 4096 + 16 * itn * 128]
                .rearrange("p (k n) -> p k n", n=itn * 128),
                in_=wsrc(usrc)[:, it0 * 128:(it0 + itn) * 128]
                .rearrange("(k p) n -> p k n", p=128)), wait=None)
            ords[i] = o2
        else:
            _, dsrc, rows0, og = it
            par = i % 2
            ords[i] = P.gd(lambda e, dsrc=dsrc, rows0=rows0, og=og, par=par:
                           e.dma_start(
                out=ctx.dblk.ap()[:, par * 5632: par * 5632 + 11 * 512]
                .rearrange("p (k n) -> p k n", n=512),
                in_=wsrc(dsrc)[rows0: rows0 + 11 * 128, og * 512:(og + 1) * 512]
                .rearrange("(k p) n -> p k n", p=128)), wait=wait)

    emit_load(0, wo_tok[2])
    emit_load(1, wo_tok[3])
    crep_done = -1
    for i, it in enumerate(items):
        par = i % 2
        if it[0] == "gu":
            _, gsrc, usrc, it0, itn = it
            shared = gsrc[1] is None
            ex = gsrc[1]
            if not shared and ex != crep_done and it0 == 0:
                for half in range(2):
                    P.t(lambda e, ex=ex, half=half: e.matmul(
                        ps[6][:, 0:512], ctx.sel.ap()[:, ex * 128:(ex + 1) * 128],
                        ctx.combT.ap()[:, half * 512:(half + 1) * 512],
                        start=True, stop=True, skip_group_check=True))
                    P.a(lambda e, half=half: e.copy(
                        ctx.crep.ap()[:, half * 512:(half + 1) * 512],
                        ps[6][:, 0:512]))
                crep_done = ex
            for itl in range(itn):
                itg = it0 + itl
                ith = itg % 11
                for tg2 in range(2):
                    bkg, bku = ps[(itl + tg2) % 2], ps[2 + (itl + tg2) % 2]
                    for kt in range(16):
                        def gfn(e, kt=kt, itl=itl, itn=itn, tg2=tg2, bkg=bkg,
                                par=par, i=i):
                            if kt == 0:
                                P.gwait(ords[i])(e)
                            return e.matmul(
                                bkg[:, 0:512],
                                ctx.gblk.ap()[:, par * 4096 + kt * itn * 128 + itl * 128:
                                              par * 4096 + kt * itn * 128 + (itl + 1) * 128],
                                ctx.x2T.ap()[:, kt * 1024 + tg2 * 512:
                                             kt * 1024 + (tg2 + 1) * 512],
                                start=(kt == 0), stop=(kt == 15),
                                skip_group_check=True)
                        P.t(gfn)
                        P.t(lambda e, kt=kt, itl=itl, itn=itn, tg2=tg2, bku=bku,
                            par=par: e.matmul(
                            bku[:, 0:512],
                            ctx.ublk.ap()[:, par * 4096 + kt * itn * 128 + itl * 128:
                                          par * 4096 + kt * itn * 128 + (itl + 1) * 128],
                            ctx.x2T.ap()[:, kt * 1024 + tg2 * 512:
                                         kt * 1024 + (tg2 + 1) * 512],
                            start=(kt == 0), stop=(kt == 15), skip_group_check=True))
                    if shared:
                        P.a(lambda e, bkg=bkg, itg=itg: e.activation(
                            ctx.sg_s.ap(), bkg[:, 0:512], AF.Silu,
                            bias=ctx.sgb.ap()[:, itg:itg + 1]))
                        P.v(lambda e, bku=bku, itg=itg: e.tensor_scalar_add(
                            ctx.uc_s.ap(), bku[:, 0:512], ctx.sub.ap()[:, itg:itg + 1]))
                    else:
                        P.a(lambda e, bkg=bkg: e.activation(ctx.sg_s.ap(),
                                                            bkg[:, 0:512], AF.Silu))
                        P.v(lambda e, bku=bku, tg2=tg2: e.tensor_tensor(
                            ctx.uc_s.ap(), bku[:, 0:512],
                            ctx.crep.ap()[:, tg2 * 512:(tg2 + 1) * 512], AL.mult))
                    P.brk()
                    P.v(lambda e, ith=ith, tg2=tg2: e.tensor_tensor(
                        ctx.he.ap()[:, ith * 1024 + tg2 * 512:
                                    ith * 1024 + (tg2 + 1) * 512],
                        ctx.sg_s.ap(), ctx.uc_s.ap(), AL.mult))
            toks[i] = P.token()
        else:
            _, dsrc, rows0, og = it
            for tt in range(8):
                bk = ps[4 + tt % 2]
                for kt in range(11):
                    def dfn(e, kt=kt, tt=tt, bk=bk, par=par, i=i):
                        if kt == 0 and tt == 0:
                            P.gwait(ords[i])(e)
                        return e.matmul(
                            bk[:, 0:512],
                            ctx.he.ap()[:, kt * 1024 + tt * 128:
                                        kt * 1024 + tt * 128 + 128],
                            ctx.dblk.ap()[:, par * 5632 + kt * 512:
                                          par * 5632 + (kt + 1) * 512],
                            start=(kt == 0), stop=(kt == 10),
                            skip_group_check=True)
                    P.t(dfn)
                P.v(lambda e, tt=tt, og=og, bk=bk: e.tensor_tensor(
                    ctx.moe.ap()[:, tt * 2048 + og * 512: tt * 2048 + (og + 1) * 512],
                    ctx.moe.ap()[:, tt * 2048 + og * 512: tt * 2048 + (og + 1) * 512],
                    bk[:, 0:512], AL.add))
            toks[i] = P.token()
        if i + 2 < len(items):
            emit_load(i + 2, toks.get(i))

    P.s(lambda e: e.dma_start(out=ctx.xbuf.ap()[0:1, :], in_=dram["sdb"]))
    for og in range(4):
        P.t(lambda e, og=og: e.matmul(ps[6][:, 0:512], ctx.onesrow1.ap(),
                                      ctx.xbuf.ap()[0:1, og * 512:(og + 1) * 512],
                                      start=True, stop=True, skip_group_check=True))
        for tt in range(8):
            P.v(lambda e, tt=tt, og=og: e.tensor_tensor(
                ctx.moe.ap()[:, tt * 2048 + og * 512: tt * 2048 + (og + 1) * 512],
                ctx.moe.ap()[:, tt * 2048 + og * 512: tt * 2048 + (og + 1) * 512],
                ps[6][:, 0:512], AL.add))
    for tt in range(8):
        mt = ctx.moe.ap()[:, tt * 2048:(tt + 1) * 2048]
        emit_ln(P, ctx, mt, 2048, ctx.xbuf.ap())
        P.s(lambda e, tt=tt: e.dma_start(out=P.out_q[tt * 128:(tt + 1) * 128, :],
                                         in_=ctx.xbuf.ap()))
        P.brk()


def build_launchB():
    nc = bass.Bass("TRN2", target_bir_lowering=False)
    dram = {}
    for name in ["c_idb", "c_idf", "c_onesrow1", "c_fvals"]:
        arr = CONST_SPECS[name]
        dt = BF16 if arr.dtype == BF else F32
        dram[name] = nc.dram_tensor(name, list(arr.shape), dt, kind="ExternalInput").ap()
    spec = [("oTq", [D, TQ], BF16), ("x_tok", [TQ, D], F32), ("WoT", [D, D], F32),
            ("rwT", [D, 8], F32), ("sel", [8, 1024], BF16),
            ("s_gate_w", [D, SI], F32), ("s_up_w", [D, SI], F32),
            ("s_down_w", [SI, D], F32), ("sgb", [128, 22], F32),
            ("sub", [128, 22], F32), ("sdb", [1, 2048], F32)]
    for name, shape, dt in spec:
        dram[name] = nc.dram_tensor("d_" + name, shape, dt, kind="ExternalInput").ap()
    dram["gate_w"] = nc.dram_tensor("d_gate_w", [E, D, I_FF], F32, kind="ExternalInput").ap()
    dram["up_w"] = nc.dram_tensor("d_up_w", [E, D, I_FF], F32, kind="ExternalInput").ap()
    dram["down_w"] = nc.dram_tensor("d_down_w", [E, I_FF, D], F32, kind="ExternalInput").ap()
    out_q = nc.dram_tensor("out_q", [TQ, D], F32, kind="ExternalOutput").ap()

    ctx = BCtx(nc)
    with nc.Block() as block:
        P = Prog(nc, nbar=8)
        P._junk = ctx.junk.ap()
        P.out_q = out_q
        emit_launchB(P, ctx, dram)
        P.v(lambda e: e.engine_nop())
        P.lower(block)
    return nc


def build_fused():
    nc = bass.Bass("TRN2", target_bir_lowering=False)
    dram = {}
    for name, arr in CONST_SPECS.items():
        dt = BF16 if arr.dtype == BF else F32
        dram[name] = nc.dram_tensor(name, list(arr.shape), dt, kind="ExternalInput").ap()
    insA = [("xT", [D, T], F32), ("WqT", [D, 512], F32), ("WkT", [D, 512], F32),
            ("WvT", [D, 512], F32), ("W1T", [D, 32], F32), ("W2T", [32, 512], F32),
            ("bgT", [D, 64], F32), ("bgb", [64, 1], F32), ("convs", [128, 16], F32),
            ("qgrow", [1, 512], F32), ("kgrow", [1, 512], F32)]
    insB = [("x_tok", [TQ, D], F32), ("WoT", [D, D], F32),
            ("rwT", [D, 8], F32), ("sel", [8, 1024], BF16),
            ("s_gate_w", [D, SI], F32), ("s_up_w", [D, SI], F32),
            ("s_down_w", [SI, D], F32), ("sgb", [128, 22], F32),
            ("sub", [128, 22], F32), ("sdb", [1, 2048], F32),
            ("gate_w", [E, D, I_FF], F32), ("up_w", [E, D, I_FF], F32),
            ("down_w", [E, I_FF, D], F32)]
    for name, shape, dt in insA + insB:
        dram[name] = nc.dram_tensor("d_" + name, shape, dt, kind="ExternalInput").ap()
    out_q = nc.dram_tensor("out_q", [TQ, D], F32, kind="ExternalOutput").ap()
    negG3_hbm = nc.dram_tensor("negG3_scr", [H, 3, T], BF16, kind="Internal").ap()
    cc_in = nc.dram_tensor("cc_in", [512, T], BF16, kind="Internal").ap()
    cc_out = nc.dram_tensor("cc_out", [D, T], BF16, kind="Internal").ap()

    ctx = ACtx(nc, guarded=True)
    with nc.Block() as block:
        P = Prog(nc, nbar=8)
        P._junk = ctx.junk.ap()
        P.negG3_hbm = negG3_hbm
        P.out_q = out_q
        emit_load_consts(P, ctx, dram)
        emit_prep(P, ctx, dram)
        tokP = P.token()
        release_guards(ctx)
        ctx.alloc_attn()
        P.join(1, tokP)
        for pair in range(4):
            hA, hB = 2 * pair, 2 * pair + 1
            emit_head(P, ctx, ctx.ch[0], hA, dram)
            emit_head(P, ctx, ctx.ch[1], hB, dram)
            for c in range(NCH):
                P.capture_begin()
                emit_chunk(P, ctx, ctx.ch[0], hA, c)
                LA = P.capture_end()
                P.capture_begin()
                emit_chunk(P, ctx, ctx.ch[1], hB, c)
                LB = P.capture_end()
                P.interleave(LA, LB)
            P.chain = 0
            P.s(lambda e, hA=hA: e.dma_start(
                out=cc_in[hA * 64: hA * 64 + 64, :], in_=ctx.oT.ap()[0:64, :]))
            P.chain = 1
            P.s(lambda e, hB=hB: e.dma_start(
                out=cc_in[hB * 64: hB * 64 + 64, :], in_=ctx.oT.ap()[64:128, :]))
        tok1 = P.token()
        P.chain = 0
        P.join(0, tok1)
        ctx.release_attn()
        ctxB = BCtx(nc, actx=ctx)
        P.cc(lambda e: e.collective_compute(
            "AllGather", AL.bypass, [[0, 1, 2, 3], [4, 5, 6, 7]],
            [cc_in], [cc_out]))
        P.cc_tok = P.token()
        emit_launchB(P, ctxB, dict(dram, oTq=cc_out), fused=True)
        P.v(lambda e: e.engine_nop())
        P.lower(block)
    return nc


def kernel(**inputs):
    consts = host_consts()
    x = np.asarray(inputs["L0_attn_hidden_states"], dtype=np.float32)

    nc = build_fused()
    selc = np.zeros((8, 1024), np.float32)
    for ex in range(8):
        selc[ex, ex * 128:(ex + 1) * 128] = 1.0
    gate_w = np.ascontiguousarray(np.asarray(inputs["L0_ffn_gate_w"], np.float32))
    up_w = np.ascontiguousarray(np.asarray(inputs["L0_ffn_up_w"], np.float32))
    down_w = np.ascontiguousarray(np.asarray(inputs["L0_ffn_down_w"], np.float32))
    s_gate_w = np.ascontiguousarray(np.asarray(inputs["L0_ffn_s_gate_w"], np.float32))
    s_up_w = np.ascontiguousarray(np.asarray(inputs["L0_ffn_s_up_w"], np.float32))
    s_down_w = np.ascontiguousarray(np.asarray(inputs["L0_ffn_s_down_w"], np.float32))
    WoT = np.ascontiguousarray(np.asarray(inputs["L0_attn_Wo"]).T.astype(np.float32))
    rwT = np.ascontiguousarray(np.asarray(inputs["L0_ffn_router_w"]).T.astype(np.float32))
    W1T = np.ascontiguousarray(np.asarray(inputs["L0_attn_W1"]).T.astype(np.float32))
    sgb_l = np.ascontiguousarray(
        np.asarray(inputs["L0_ffn_s_gate_b"], np.float32).reshape(22, 128).T)
    sub_l = np.ascontiguousarray(
        np.asarray(inputs["L0_ffn_s_up_b"], np.float32).reshape(22, 128).T)
    sdb_l = np.asarray(inputs["L0_ffn_s_down_b"], np.float32)[None, :]
    xT = [np.ascontiguousarray(x[b].T) for b in range(B)]

    in_maps = []
    for c in range(N_CORES):
        b = c // 4
        hg = c % 4
        tq = c % 4
        hs = slice(hg * 8 * HD, (hg + 1) * 8 * HD)
        hh = slice(hg * 8, (hg + 1) * 8)
        tsl = slice(tq * TQ, (tq + 1) * TQ)
        m = dict(consts)
        m["d_xT"] = xT[b]
        m["d_WqT"] = np.ascontiguousarray(np.asarray(inputs["L0_attn_Wq"])[hs].T.astype(np.float32))
        m["d_WkT"] = np.ascontiguousarray(np.asarray(inputs["L0_attn_Wk"])[hs].T.astype(np.float32))
        m["d_WvT"] = np.ascontiguousarray(np.asarray(inputs["L0_attn_Wv"])[hs].T.astype(np.float32))
        m["d_W1T"] = W1T
        m["d_W2T"] = np.ascontiguousarray(np.asarray(inputs["L0_attn_W2"]).T[:, hs].astype(np.float32))
        bg = np.zeros((D, 64), np.float32)
        bg[:, 0:8] = np.asarray(inputs["L0_attn_bt_w"])[hh].T
        bg[:, 32:40] = np.asarray(inputs["L0_attn_g_w"])[hh].T
        m["d_bgT"] = bg
        bgb = np.zeros((64, 1), np.float32)
        bgb[0:8, 0] = np.asarray(inputs["L0_attn_bt_b"])[hh]
        bgb[32:40, 0] = np.asarray(inputs["L0_attn_g_b"])[hh]
        m["d_bgb"] = bgb
        cw = np.asarray(inputs["L0_attn_conv_w"])[hs]
        m["d_convs"] = np.ascontiguousarray(
            cw.reshape(4, 128, 4).transpose(1, 0, 2).reshape(128, 16)).astype(np.float32)
        m["d_qgrow"] = np.tile(np.asarray(inputs["L0_attn_q_norm_w"], np.float32), 8)[None, :]
        m["d_kgrow"] = np.tile(np.asarray(inputs["L0_attn_k_norm_w"], np.float32), 8)[None, :]
        m["d_x_tok"] = np.ascontiguousarray(x[b][tsl])
        m["d_WoT"] = WoT
        m["d_rwT"] = rwT
        m["d_sel"] = selc.astype(BF)
        m["d_gate_w"] = gate_w
        m["d_up_w"] = up_w
        m["d_down_w"] = down_w
        m["d_s_gate_w"] = s_gate_w
        m["d_s_up_w"] = s_up_w
        m["d_s_down_w"] = s_down_w
        m["d_sgb"] = sgb_l
        m["d_sub"] = sub_l
        m["d_sdb"] = sdb_l
        in_maps.append(m)

    res = run_bass_kernel_spmd(nc, in_maps, core_ids=list(range(N_CORES)))
    out = np.zeros((B, T, D), dtype=np.float32)
    for c in range(N_CORES):
        b = c // 4
        tq = c % 4
        out[b, tq * TQ:(tq + 1) * TQ] = res.results[c]["out_q"]
    return out
